# version 10
# baseline (speedup 1.0000x reference)
import numpy as np
import ml_dtypes
from contextlib import ExitStack

import concourse.bass as bass
import concourse.mybir as mybir
from concourse.bass_utils import run_bass_kernel_spmd

F32 = mybir.dt.float32
BF16 = mybir.dt.bfloat16
AF = mybir.ActivationFunctionType
ALU = mybir.AluOpType
AX = mybir.AxisListType

B, T, D = 8, 4096, 1024
DIN = 2048
NST = 16
RNK = 64
NE = 16
FF = 2048
NT = T // 128
CK = 256
NCH = T // CK
NDT = DIN // 128
SC = 512
NTT = SC // 128
NTH = max(1, SC // 512)
NSC = T // SC
EPS = 1e-5

DEBUG_SCRATCH_OUT = False
USE_DRAINS = True


def _dr(e, always=False):
    if USE_DRAINS or always:
        e.drain()
class Ctr:
    def __init__(self, h):
        self.h = h
        self.n = 0

    def bump(self, k=1):
        self.n += k
        return self.n


class Prog:
    ENGS = ("sync", "tensor", "vector", "scalar", "gpsimd")

    def __init__(self):
        self.segs = [{k: [] for k in self.ENGS}]

    def add(self, eng, fn):
        self.segs[-1][eng].append(fn)

    def mark(self):
        self.segs.append({k: [] for k in self.ENGS})

    @property
    def ops(self):
        out = {k: [] for k in self.ENGS}
        for seg in self.segs:
            for k in self.ENGS:
                out[k].extend(seg[k])
        return out

    def run(self, block):
        ops = self.ops
        for name in self.ENGS:
            fns = ops[name]
            if not fns:
                continue

            def body(e, fns=fns):
                for fn in fns:
                    fn(e)

            getattr(block, name)(body)


def seg_merge(a, b, b_reqs=None):
    out = Prog()
    na, nb = len(a.segs), len(b.segs)
    if b_reqs is None:
        b_reqs = [0] * nb
    i = j = 0
    while i < na or j < nb:
        take_a = j >= nb or (i < na and
                             (i * nb <= j * na or i < b_reqs[j]))
        if take_a:
            seg = a.segs[i]
            i += 1
        else:
            seg = b.segs[j]
            j += 1
        for k in Prog.ENGS:
            out.segs[-1][k].extend(seg[k])
    return out


def build_program():
    nc = bass.Bass("TRN2", target_bir_lowering=False)

    ap = lambda h: h.ap()
    x_d = ap(nc.dram_tensor("x", [T, D], F32, kind="ExternalInput"))
    win_d = ap(nc.dram_tensor("win", [32, 128, 8, 128], F32, kind="ExternalInput"))
    wx_d = ap(nc.dram_tensor("wx", [DIN, 96], F32, kind="ExternalInput"))
    wdt_d = ap(nc.dram_tensor("wdt", [RNK, DIN], F32, kind="ExternalInput"))
    convw_d = ap(nc.dram_tensor("convw", [128, 64], F32, kind="ExternalInput"))
    convb_d = ap(nc.dram_tensor("convb", [128, 16], F32, kind="ExternalInput"))
    bdt_d = ap(nc.dram_tensor("bdt", [128, 16], F32, kind="ExternalInput"))
    alog_d = ap(nc.dram_tensor("alog", [128, 256], F32, kind="ExternalInput"))
    dvec_d = ap(nc.dram_tensor("dvec", [128, 16], F32, kind="ExternalInput"))
    wout_d = ap(nc.dram_tensor("wout", [128, 16, 1024], F32, kind="ExternalInput"))
    wrt_d = ap(nc.dram_tensor("wrt", [D, NE], F32, kind="ExternalInput"))
    gup_d = ap(nc.dram_tensor("gup", [NE, 32, 128, 8, 128], F32, kind="ExternalInput"))
    dwn_d = ap(nc.dram_tensor("dwn", [NE, 128, 16, 1024], F32, kind="ExternalInput"))
    wgt_d = ap(nc.dram_tensor("wgt", [16, 128, 8, 128], F32, kind="ExternalInput"))
    wut_d = ap(nc.dram_tensor("wut", [16, 128, 8, 128], F32, kind="ExternalInput"))
    wdsh_d = ap(nc.dram_tensor("wdsh", [128, 16, 1024], F32, kind="ExternalInput"))
    oh_d = ap(nc.dram_tensor("oh", [16, 16 * 128], BF16, kind="ExternalInput"))
    idb_d = ap(nc.dram_tensor("idb", [128, 128], BF16, kind="ExternalInput"))
    idf_d = ap(nc.dram_tensor("idf", [128, 128], F32, kind="ExternalInput"))
    cv_d = ap(nc.dram_tensor("cvals", [128, 8], F32, kind="ExternalInput"))

    skind = "ExternalOutput" if DEBUG_SCRATCH_OUT else "Internal"
    win16_d = ap(nc.dram_tensor("win16_s", [32, 128, 8 * 128], BF16, kind="Internal"))
    wout16_d = ap(nc.dram_tensor("wout16_s", [128, 16, 1024], BF16, kind="Internal"))
    dwn16_d = ap(nc.dram_tensor("dwn16_s", [NE, 128, 16 * 1024], BF16, kind="Internal"))
    xnt_d = ap(nc.dram_tensor("xnt_s", [8, 128, T], BF16, kind=skind))
    attn_d = ap(nc.dram_tensor("attn_s", [T, D], F32, kind=skind))
    out_d = ap(nc.dram_tensor("out", [T, D], F32, kind="ExternalOutput"))

    ctx = ExitStack()
    with ctx:
        def sbuf(name, shape, dtype):
            return ctx.enter_context(nc.sbuf_tensor(name, shape, dtype))

        def sem(name):
            return Ctr(ctx.enter_context(nc.semaphore(name)))

        oh_sb = sbuf("oh_sb", [16, 16 * 128], BF16)
        idb_sb = sbuf("idb_sb", [128, 128], BF16)
        scores = sbuf("scores", [16, T], BF16)
        a_sb = sbuf("a_sb", [128, 256], F32)
        dvec_sb = sbuf("dvec_sb", [128, 16], F32)
        convw_sb = sbuf("convw_sb", [128, 64], F32)
        convb_sb = sbuf("convb_sb", [128, 16], F32)
        bdt_sb = sbuf("bdt_sb", [128, 16], F32)
        wx_sb = sbuf("wx_sb", [128, 16 * 96], BF16)
        wdt_sb = sbuf("wdt_sb", [64, 16 * 128], BF16)
        cv_sb = sbuf("cv_sb", [128, 8], F32)
        CV_EPS, CV_ONE, CV_INVD, CV_NHALF, CV_NONE = (
            cv_sb[:, 0:1], cv_sb[:, 1:2], cv_sb[:, 2:3], cv_sb[:, 3:4], cv_sb[:, 4:5])

        c_ld = sem("c_ld")
        c_gld = sem("c_gld")
        c_prep = sem("c_prep")
        c_prep2 = sem("c_prep2")
        bar1 = sem("bar1")
        bar2 = sem("bar2")

        s1x = [sem("s1x0"), sem("s1x1")]
        s1v = sem("s1v")
        s1s = sem("s1s")
        s1t = sem("s1t")
        s1e = sem("s1e")
        s1w = [sem("s1w0"), sem("s1w1")]
        s1r = sem("s1r")
        s1sc = sem("s1sc")

        m_xnc = sem("m_xnc")
        m_win = [sem("m_win0"), sem("m_win1")]
        m_wres = sem("m_wres")
        m_wiA = sem("m_wiA")
        m_evA = sem("m_evA")
        m_hist = sem("m_hist")
        m_conv = sem("m_conv")
        m_u = sem("m_u")
        m_wx = sem("m_wx")
        m_evBs = sem("m_evBs")
        m_evBv = sem("m_evBv")
        m_exp = sem("m_exp")
        m_evE = sem("m_evE")
        m_wdt = sem("m_wdt")
        m_spr = sem("m_spr")
        m_dsp = sem("m_dsp")
        m_dlt = sem("m_dlt")
        m_z = sem("m_z")
        m_a = sem("m_a")
        m_b = sem("m_b")
        m_yg = sem("m_yg")
        m_scan = sem("m_scan")
        m_y = sem("m_y")
        m_wo = sem("m_wo")
        m_evC = sem("m_evC")
        m_ast = [sem("m_ast0"), sem("m_ast1")]

        n_str = [sem("n_str0"), sem("n_str1")]
        n_X = sem("n_X")
        n_sc = sem("n_sc")
        n_scev = sem("n_scev")
        n_wg = [sem("n_wg0"), sem("n_wg1")]
        n_wdn = [sem("n_wdn0"), sem("n_wdn1")]
        n_gu = sem("n_gu")
        n_sl = sem("n_sl")
        n_h = sem("n_h")
        n_dn = sem("n_dn")
        n_acc = sem("n_acc")
        n_xat = [sem("n_xat0"), sem("n_xat1")]
        n_ep = sem("n_ep")
        n_ox = [sem("n_ox0"), sem("n_ox1")]
        c_wl = [sem("c_wl0"), sem("c_wl1")]
        c_ws = [sem("c_ws0"), sem("c_ws1")]
        m_wol = [sem("m_wol0"), sem("m_wol1")]
        m_woc = sem("m_woc")
        n_out = [sem("n_out0"), sem("n_out1")]

        with ExitStack() as p1:
            sb1 = lambda n_, s, d: p1.enter_context(nc.sbuf_tensor(n_, s, d))
            ps1 = lambda n_, s, d=F32: p1.enter_context(nc.psum_tensor(n_, s, d))

            idf_sb = sb1("idf_sb", [128, 128], F32)
            wrt_sb = sb1("wrt_sb", [128, 8 * 16], BF16)
            xin = [sb1(f"xin{i}", [128, D], F32) for i in range(2)]
            junk = sb1("junk", [128, D], F32)
            stat = sb1("stat", [128, 8], F32)
            xn_t = sb1("xn_t", [128, D], BF16)
            xnt_sb = sb1("xnt_sb", [128, 8 * 512], BF16)
            sgt = sb1("sgt", [128, 16], F32)
            mskt = sb1("mskt", [128, 16], F32)
            sct = sb1("sct", [128, 16], F32)
            mxt = sb1("mxt", [128, 1], F32)
            atmp = sb1("atmp", [128, 256], F32)
            tp_ps = [ps1(f"tp_ps{i}", [128, 1024], BF16) for i in range(2)]
            lg_ps = ps1("lg_ps", [128, 512], F32)
            st_ps = ps1("st_ps", [16, 512], F32)

            P1 = Prog()

            def c_sync(e):
                for dst, src in ((oh_sb, oh_d), (idf_sb, idf_d),
                                 (convw_sb, convw_d), (convb_sb, convb_d),
                                 (bdt_sb, bdt_d), (dvec_sb, dvec_d),
                                 (atmp, alog_d), (cv_sb, cv_d)):
                    e.dma_start(dst[:], src[:]).then_inc(c_ld.h, 16)
            P1.add("sync", c_sync)
            c_ld.bump(16 * 8)

            def c_gp(e):
                e.dma_start(idb_sb[:], idb_d[:]).then_inc(c_gld.h, 16)
                e.dma_start(
                    wx_sb[:].rearrange("p (c k) -> p c k", c=16),
                    wx_d.rearrange("(c p) k -> p c k", p=128),
                ).then_inc(c_gld.h, 16)
                e.dma_start(
                    wdt_sb[:].rearrange("p (c k) -> p c k", c=16),
                    wdt_d.rearrange("p (c k) -> p c k", c=16),
                ).then_inc(c_gld.h, 16)
                e.dma_start(
                    wrt_sb[:].rearrange("p (c k) -> p c k", c=8),
                    wrt_d.rearrange("(c p) k -> p c k", p=128),
                ).then_inc(c_gld.h, 16)
            P1.add("gpsimd", c_gp)
            c_gld.bump(16 * 4)

            def c_sc(e):
                e.wait_ge(c_ld.h, 16 * 8)
                e.activation(atmp[:], atmp[:], AF.Exp).then_inc(c_prep.h, 1)
                _dr(e, True)
            P1.add("scalar", c_sc)
            c_prep.bump()

            def c_vec(e):
                e.wait_ge(c_prep.h, 1)
                e.tensor_scalar_mul(a_sb[:], atmp[:], -1.0).then_inc(c_prep2.h, 1)
                _dr(e, True)
            P1.add("vector", c_vec)
            c_prep2.bump()

            v_done = {}
            s_done = {}
            sg_done = {}
            e_done = {}
            w_done = {}
            mul_done = {}
            sce_done = {}
            rt_done = {}
            stp_done = {}

            for tt in range(NT):
                buf = xin[tt % 2]

                xp = s1x[tt % 2]

                def f_sync(e, tt=tt, buf=buf, xp=xp,
                           wv=(v_done.get(tt - 2)), ws=(s_done.get(tt - 2))):
                    if wv is not None:
                        e.wait_ge(s1v.h, wv)
                        e.wait_ge(s1s.h, ws)
                    e.dma_start(buf[:], x_d[tt * 128:(tt + 1) * 128, :]
                                ).then_inc(xp.h, 16)
                P1.add("sync", f_sync)
                xp.bump(16)
                x_ready = xp.n

                def f_v1(e, buf=buf, xp=xp, xr=x_ready, ws=s_done.get(tt - 1)):
                    e.wait_ge(xp.h, xr)
                    if ws is not None:
                        e.wait_ge(s1s.h, ws)
                    e.tensor_reduce(stat[:, 0:1], buf[:], AX.X, ALU.add)
                    _dr(e, True)
                    e.tensor_scalar_mul(stat[:, 0:1], stat[:, 0:1], -1.0 / D
                                        ).then_inc(s1v.h, 1)
                    _dr(e, True)
                P1.add("vector", f_v1)
                s1v.bump()
                nm_ready = s1v.n

                def f_s1(e, buf=buf, xp=xp, xr=x_ready, nmr=nm_ready):
                    e.wait_ge(xp.h, xr)
                    e.wait_ge(s1v.h, nmr)
                    e.activation(junk[:], buf[:], AF.Square,
                                 bias=stat[:, 0:1], accum_out=stat[:, 1:2])
                    _dr(e, True)
                    e.activation(stat[:, 2:3], stat[:, 1:2], AF.Ln,
                                 scale=CV_INVD, bias=CV_EPS)
                    _dr(e, True)
                    e.activation(stat[:, 3:4], stat[:, 2:3], AF.Exp,
                                 scale=CV_NHALF).then_inc(s1s.h, 1)
                    _dr(e, True)
                P1.add("scalar", f_s1)
                s1s.bump()
                rstd_ready = s1s.n
                s_done[tt] = s1s.n

                def f_xn(e, buf=buf, rr=rstd_ready):
                    e.wait_ge(s1s.h, rr)
                    e.tensor_scalar(xn_t[:], buf[:], stat[:, 0:1],
                                    stat[:, 3:4], ALU.add, ALU.mult
                                    ).then_inc(s1v.h, 1)
                    _dr(e, True)
                P1.add("vector", f_xn)
                s1v.bump()
                xn_ready = s1v.n
                v_done[tt] = s1v.n

                for j in range(8):
                    tpw = s1t.n - 1

                    def f_t(e, j=j, xr=xn_ready, war=tpw):
                        e.wait_ge(c_gld.h, 16 * 4)
                        e.wait_ge(s1v.h, xr)
                        if war >= 1:
                            e.wait_ge(s1e.h, war)
                        e.transpose(tp_ps[j % 2][:, 0:128],
                                    xn_t[:, j * 128:(j + 1) * 128],
                                    idb_sb[:]).then_inc(s1t.h, 1)
                    P1.add("tensor", f_t)
                    s1t.bump()
                    tp_ready = s1t.n

                    g = tt // 4

                    def f_ev(e, j=j, tt=tt, tr=tp_ready, g=g,
                             ww=w_done.get(g - 1),
                             wr=(8 * g - 1 if g >= 1 else None)):
                        e.wait_ge(s1t.h, tr)
                        if ww is not None:
                            e.wait_ge(s1w[(g - 1) % 2].h, ww)
                        if wr is not None:
                            e.wait_ge(s1r.h, wr)
                        col = j * 512 + (tt % 4) * 128
                        e.tensor_copy(xnt_sb[:, col:col + 128],
                                      tp_ps[j % 2][:, 0:128]).then_inc(s1e.h, 1)
                        _dr(e, True)
                    P1.add("vector", f_ev)
                    s1e.bump()
                e_done[tt] = s1e.n

                def f_rt(e, tt=tt, er=e_done[tt],
                         wm=mul_done.get(tt - 1), wsg=sg_done.get(tt - 1)):
                    e.wait_ge(c_gld.h, 16 * 4)
                    e.wait_ge(s1e.h, er)
                    if wm is not None:
                        e.wait_ge(s1sc.h, wm)
                        e.wait_ge(s1s.h, wsg)
                    for j in range(8):
                        col = j * 512 + (tt % 4) * 128
                        inst = e.matmul(lg_ps[:, 0:16], xnt_sb[:, col:col + 128],
                                        wrt_sb[:, j * 16:(j + 1) * 16],
                                        start=(j == 0), stop=(j == 7))
                    inst.then_inc(s1r.h, 1)
                P1.add("tensor", f_rt)
                s1r.bump()
                rt_done[tt] = s1r.n
                lg_ready = s1r.n

                def f_sg(e, lr=lg_ready):
                    e.wait_ge(s1r.h, lr)
                    e.activation(sgt[:], lg_ps[:, 0:16], AF.Sigmoid).then_inc(s1s.h, 1)
                    _dr(e, True)
                P1.add("scalar", f_sg)
                s1s.bump()
                sg_done[tt] = s1s.n

                def f_msk(e, lr=lg_ready, sgr=sg_done[tt],
                          wst=stp_done.get(tt - 1)):
                    e.wait_ge(s1r.h, lr)
                    if wst is not None:
                        e.wait_ge(s1r.h, wst)
                    e.tensor_reduce(mxt[:], lg_ps[:, 0:16], AX.X, ALU.max)
                    _dr(e, True)
                    e.tensor_scalar(mskt[:], lg_ps[:, 0:16], mxt[:, 0:1], None,
                                    ALU.is_ge)
                    _dr(e, True)
                    e.wait_ge(s1s.h, sgr)
                    e.tensor_mul(sct[:], mskt[:], sgt[:]).then_inc(s1sc.h, 1)
                    _dr(e, True)
                P1.add("vector", f_msk)
                s1sc.bump()
                mul_done[tt] = s1sc.n
                sc_ready = s1sc.n

                def f_sct(e, scr=sc_ready, wse=sce_done.get(tt - 1)):
                    e.wait_ge(s1sc.h, scr)
                    if wse is not None:
                        e.wait_ge(s1sc.h, wse)
                    e.transpose(st_ps[:, 0:128], sct[:], idf_sb[:]).then_inc(s1r.h, 1)
                P1.add("tensor", f_sct)
                s1r.bump()
                stp_done[tt] = s1r.n
                st_ready = s1r.n

                def f_sce(e, sr=st_ready, tt=tt):
                    e.wait_ge(s1r.h, sr)
                    e.tensor_copy(scores[:, tt * 128:(tt + 1) * 128],
                                  st_ps[:, 0:128]).then_inc(s1sc.h, 1)
                    _dr(e, True)
                P1.add("vector", f_sce)
                s1sc.bump()
                sce_done[tt] = s1sc.n

                if tt % 4 == 3:
                    g4 = tt // 4
                    wp = s1w[g4 % 2]

                    def f_wr(e, tt=tt, wp=wp, er=e_done[tt]):
                        e.wait_ge(s1e.h, er)
                        for j in range(8):
                            e.dma_start(
                                xnt_d[j, :, (tt - 3) * 128:(tt + 1) * 128],
                                xnt_sb[:, j * 512:(j + 1) * 512]
                            ).then_inc(wp.h, 16)
                    P1.add("sync", f_wr)
                    wp.bump(16 * 8)
                    w_done[g4] = wp.n

            wtmp = [sb1(f"wtmp{i}", [128, 2048], BF16) for i in range(2)]
            bounce = []
            for pt in range(32):
                bounce.append((win_d[pt].rearrange("p c f -> p (c f)"),
                               win16_d[pt], 1024))
            for gg in range(8):
                bounce.append((
                    wout_d[:, 2 * gg:2 * gg + 2, :].rearrange("p c f -> p (c f)"),
                    wout16_d[:, 2 * gg:2 * gg + 2, :].rearrange("p c f -> p (c f)"),
                    2048))
            for ei in range(NE):
                for gg in range(8):
                    bounce.append((
                        dwn_d[ei, :, 2 * gg:2 * gg + 2, :].rearrange("p c f -> p (c f)"),
                        dwn16_d[ei, :, gg * 2048:(gg + 1) * 2048],
                        2048))
            for k, (src, dst, ne) in enumerate(bounce):
                par = k % 2

                def f_bl(e, src=src, ne=ne, par=par, k=k,
                         war=(c_ws[par].n if k >= 2 else None)):
                    if war is not None:
                        e.wait_ge(c_ws[par].h, war)
                    e.dma_start(wtmp[par][:, 0:ne], src).then_inc(c_wl[par].h, 16)
                P1.add("gpsimd", f_bl)
                c_wl[par].bump(16)
                bl_ready = c_wl[par].n

                def f_bs(e, dst=dst, ne=ne, par=par, blr=bl_ready):
                    e.wait_ge(c_wl[par].h, blr)
                    e.dma_start(dst, wtmp[par][:, 0:ne]).then_inc(c_ws[par].h, 16)
                P1.add("sync", f_bs)
                c_ws[par].bump(16)

            def b_sync(e, w0=s1w[0].n, w1=s1w[1].n,
                       ws0=c_ws[0].n, ws1=c_ws[1].n):
                e.wait_ge(s1w[0].h, w0)
                e.wait_ge(s1w[1].h, w1)
                e.wait_ge(c_ws[0].h, ws0)
                e.wait_ge(c_ws[1].h, ws1)
                e.nop().then_inc(bar1.h, 1)
            P1.add("sync", b_sync)

            def b_t(e):
                e.nop().then_inc(bar1.h, 1)
            P1.add("tensor", b_t)

            for eng in ("vector", "scalar", "gpsimd"):
                def b_e(e):
                    e.drain().then_inc(bar1.h, 1)
                P1.add(eng, b_e)
            bar1.bump(5)

            blk1 = p1.enter_context(nc.Block("p1"))
            P1.run(blk1)

        with ExitStack() as p23:
            sbX = lambda n_, s, d: p23.enter_context(nc.sbuf_tensor(n_, s, d))
            psX = lambda n_, s: p23.enter_context(nc.psum_tensor(n_, s, F32))

            RB = CK + 8
            xnc = sbX("xnc", [128, 8 * CK], BF16)
            winb = [sbX(f"winb{i}", [128, 8 * 128], BF16) for i in range(2)]
            wob = [sbX(f"wob{i}", [128, 512], BF16) for i in range(2)]
            ring = sbX("ring", [128, 16 * RB], BF16)
            conv2 = [sbX(f"conv2{i}", [128, CK], F32) for i in range(2)]
            u_bf = sbX("u_bf", [128, 16 * CK], BF16)
            res_bf = sbX("res_bf", [128, 16 * CK], BF16)
            dr_bf = sbX("dr_bf", [64, CK], BF16)
            b_sbuf = sbX("b_sbuf", [16, CK], BF16)
            c_sbuf = sbX("c_sbuf", [16, CK], BF16)
            bhat = sbX("bhat", [128, 16 * CK], BF16)
            chat = sbX("chat", [128, 16 * CK], BF16)
            sp1 = sbX("sp1", [128, CK], F32)
            sp2 = sbX("sp2", [128, CK], F32)
            delta = sbX("delta", [128, CK], F32)
            z_sb = sbX("z_sb", [128, CK], F32)
            asb = [sbX(f"asb{i}", [128, CK], F32) for i in range(3)]
            bsb = [sbX(f"bsb{i}", [128, CK], F32) for i in range(3)]
            ssb = [sbX(f"ssb{i}", [128, CK], F32) for i in range(2)]
            ytmp = sbX("ytmp", [128, CK], F32)
            ytmp2 = sbX("ytmp2", [128, CK], F32)
            yacc = sbX("yacc", [128, CK], F32)
            yacc2 = sbX("yacc2", [128, CK], F32)
            carry = sbX("carry", [128, 256], F32)
            ybf = sbX("ybf", [128, 16 * CK], BF16)
            attn_t = sbX("attn_t", [128, 1024], F32)

            strb = [sbX(f"strb{i}", [128, SC], BF16) for i in range(2)]
            shat = [sbX(f"shat{i}", [128, SC], BF16) for i in range(2)]
            xe = sbX("xe", [128, 8 * SC], BF16)
            wgt_t = [sbX(f"wgt_t{i}", [128, 8 * 128], BF16) for i in range(2)]
            wut_t = [sbX(f"wut_t{i}", [128, 8 * 128], BF16) for i in range(2)]
            h_sb = sbX("h_sb", [128, 16 * SC], BF16)
            wdn_sb = sbX("wdn_sb", [128, 16 * 1024], BF16)
            acc = sbX("acc", [128, NTT * 1024], F32)
            sgtmp = sbX("sgtmp", [128, 512], F32)
            att_t = [sbX(f"att_t{i}", [128, 1024], F32) for i in range(2)]

            ps_win = [psX(f"ps_win{i}", [128, 512]) for i in range(2)]
            psB = psX("psB", [128, 512])
            ps_g = [psX(f"ps_g{i}", [128, 512]) for i in range(2)]
            ps_u = [psX(f"ps_u{i}", [128, 512]) for i in range(2)]
            ps_d = psX("ps_d", [128, 512])

            P2S = Prog()
            P3S = Prog()
            p3_reqs = [0]

            for eng in Prog.ENGS:
                def pf(e):
                    e.wait_ge(bar1.h, 5)
                P2S.add(eng, pf)

            def pf_v(e):
                e.memset(carry[:], 0.0)
                for dt in range(NDT):
                    e.memset(ring[:, dt * RB:dt * RB + 3], 0.0)
                e.drain().then_inc(m_hist.h, NDT)
            P2S.add("vector", pf_v)
            m_hist.bump(NDT)
            P2S.mark()

            gpt = 0
            g_dtn = 0
            g_gp = {}
            nonlocal_gwok = [0]
            woc_base = {}
            ast_at = {}

            def emit_wout(cprev):
                woc_base[cprev] = m_woc.n
                evc0 = m_evC.n
                for wt in range(CK // 128):
                    base = nonlocal_gwok[0]

                    def f_wols(e, base=base):
                        for half in range(2):
                            for ct in range(16):
                                k = base + half * 16 + ct
                                if k >= 2:
                                    e.wait_ge(m_woc.h, k - 1)
                                e.dma_start(
                                    wob[k % 2][:],
                                    wout16_d[:, ct, half * 512:half * 512 + 512],
                                ).then_inc(m_wol[k % 2].h, 16)
                    P2S.add("sync", f_wols)
                    wol_counts = []
                    for half in range(2):
                        for ct in range(16):
                            k = base + half * 16 + ct
                            m_wol[k % 2].bump(16)
                            wol_counts.append(m_wol[k % 2].n)

                    def f_wo(e, cprev=cprev, wt=wt, base=base,
                             wol=tuple(wol_counts),
                             yb=16 * cprev, spr=(16 * (cprev + 1)),
                             evc0=evc0):
                        e.wait_ge(m_spr.h, spr)
                        for half in range(2):
                            g = 2 * wt + half
                            e.wait_ge(m_evC.h, evc0 + g)
                            for ct in range(16):
                                k = base + half * 16 + ct
                                e.wait_ge(m_wol[k % 2].h, wol[half * 16 + ct])
                                e.wait_ge(m_y.h, yb + ct + 1)
                                lhs = ybf[:, ct * CK + wt * 128:
                                          ct * CK + wt * 128 + 128]
                                inst = e.matmul(psB[:], lhs, wob[k % 2][:],
                                                start=(ct == 0), stop=(ct == 15))
                                inst.then_inc(m_woc.h, 1)
                    P2S.add("tensor", f_wo)
                    m_woc.bump(32)
                    nonlocal_gwok[0] += 32
                    wo_ready = m_woc.n

                    gidx = cprev * (CK // 128) + wt
                    app = m_ast[gidx % 2]
                    app_prev = m_ast[(gidx - 1) % 2]

                    def f_evc(e, wr=wo_ready, app_prev=app_prev,
                              war=(app_prev.n if gidx >= 1 else None)):
                        if war is not None:
                            e.wait_ge(app_prev.h, war)
                        e.wait_ge(m_woc.h, wr - 16)
                        e.copy(attn_t[:, 0:512], psB[:]).then_inc(m_evC.h, 1)
                        e.wait_ge(m_woc.h, wr)
                        e.copy(attn_t[:, 512:1024], psB[:]).then_inc(m_evC.h, 1)
                    P2S.add("scalar", f_evc)
                    m_evC.bump(2)
                    evc_ready = m_evC.n

                    def f_ast(e, cprev=cprev, wt=wt, app=app, er=evc_ready):
                        e.wait_ge(m_evC.h, er)
                        row = cprev * CK + wt * 128
                        e.dma_start(attn_d[row:row + 128, :], attn_t[:]
                                    ).then_inc(app.h, 16)
                    P2S.add("sync", f_ast)
                    app.bump(16)
                ast_at[cprev] = [(m_ast[0], m_ast[0].n), (m_ast[1], m_ast[1].n)]

            for c in range(NCH):
                def f_xnc(e, c=c, war=m_wiA.n):
                    if c >= 1:
                        e.wait_ge(m_wiA.h, war)
                    e.dma_start(
                        xnc[:].rearrange("p (j t) -> p j t", j=8),
                        xnt_d[:, :, c * CK:(c + 1) * CK].transpose([1, 0, 2]),
                    ).then_inc(m_xnc.h, 16)
                P2S.add("sync", f_xnc)
                m_xnc.bump(16)
                xnc_ready = m_xnc.n

                for pt in range(32):
                    wpp = m_win[gpt % 2]

                    def f_wl(e, pt=pt, gpt=gpt, wpp=wpp, war=(m_wiA.n - 1)):
                        if gpt >= 2:
                            e.wait_ge(m_wiA.h, war)
                        e.dma_start(winb[gpt % 2][:], win16_d[pt]
                                    ).then_inc(wpp.h, 16)
                    P2S.add("sync", f_wl)
                    wpp.bump(16)
                    w_ready = wpp.n

                    def f_mm(e, gpt=gpt, c=c, pt=pt, wpp=wpp, xr=xnc_ready,
                             wr=w_ready, war=(m_evA.n - 1),
                             expf=(32 * c if (gpt % 2 == 0 and c >= 1) else None)):
                        e.wait_ge(m_xnc.h, xr)
                        e.wait_ge(wpp.h, wr)
                        if gpt >= 2:
                            e.wait_ge(m_evA.h, war)
                        if expf is not None:
                            e.wait_ge(m_evE.h, expf)
                        for j in range(8):
                            inst = e.matmul(ps_win[gpt % 2][:, 0:CK],
                                            winb[gpt % 2][:, j * 128:(j + 1) * 128],
                                            xnc[:, j * CK:(j + 1) * CK],
                                            start=(j == 0), stop=(j == 7))
                        inst.then_inc(m_wiA.h, 1)
                    P2S.add("tensor", f_mm)
                    m_wiA.bump()
                    mm_ready = m_wiA.n

                    def f_ev(e, pt=pt, gpt=gpt, c=c, mr=mm_ready,
                             hist=(16 * c + (pt % 16) + 1),
                             wy=(16 * (c - 1) + (pt - 16) + 1
                                 if (c >= 1 and pt >= 16) else None)):
                        e.wait_ge(m_wiA.h, mr)
                        if pt < 16:
                            e.wait_ge(m_hist.h, hist)
                            e.copy(ring[:, pt * RB + 3:pt * RB + 3 + CK],
                                   ps_win[gpt % 2][:, 0:CK]).then_inc(m_evA.h, 1)
                        else:
                            if wy is not None:
                                e.wait_ge(m_y.h, wy)
                            e.activation(res_bf[:, (pt - 16) * CK:(pt - 15) * CK],
                                         ps_win[gpt % 2][:, 0:CK], AF.Silu
                                         ).then_inc(m_evA.h, 1)
                        _dr(e)
                    P2S.add("scalar", f_ev)
                    m_evA.bump()
                    gpt += 1

                evA_base = m_evA.n - 32

                for dt in range(NDT):
                    def f_cv(e, dt=dt, ev=(evA_base + dt + 1),
                             wu=(m_u.n - 1)):
                        e.wait_ge(m_evA.h, ev)
                        if wu >= 1:
                            e.wait_ge(m_u.h, wu)
                        base = dt * RB
                        ca = conv2[dt % 2]
                        e.tensor_scalar(ca[:], ring[:, base:base + CK],
                                        convw_sb[:, dt * 4:dt * 4 + 1], None,
                                        ALU.mult)
                        _dr(e, True)
                        for k in range(1, 4):
                            inst = e.scalar_tensor_tensor(
                                ca[:], ring[:, base + k:base + k + CK],
                                convw_sb[:, dt * 4 + k:dt * 4 + k + 1],
                                ca[:], ALU.mult, ALU.add)
                            _dr(e, True)
                        inst.then_inc(m_conv.h, 1)
                        e.tensor_copy(ring[:, base:base + 3],
                                      ring[:, base + CK:base + CK + 3]
                                      ).then_inc(m_hist.h, 1)
                        _dr(e)
                    P2S.add("vector", f_cv)
                    m_conv.bump()
                    m_hist.bump()
                    cv_ready = m_conv.n

                    def f_u(e, dt=dt, c=c, cr=cv_ready,
                            wwx=(3 * c if c >= 1 else None),
                            wz=(16 * c if c >= 1 else None),
                            wy=(16 * c if c >= 1 else None)):
                        e.wait_ge(m_conv.h, cr)
                        if wwx is not None:
                            e.wait_ge(m_wx.h, wwx)
                            e.wait_ge(m_z.h, wz)
                            e.wait_ge(m_y.h, wy)
                        e.activation(u_bf[:, dt * CK:(dt + 1) * CK],
                                     conv2[dt % 2][:], AF.Silu,
                                     bias=convb_sb[:, dt:dt + 1]
                                     ).then_inc(m_u.h, 1)
                        _dr(e)
                    P2S.add("scalar", f_u)
                    m_u.bump()

                u_all = m_u.n

                def f_wx(e, c=c, ur=u_all, war_s=m_evBs.n, war_v=m_evBv.n,
                         evc=(2 * (CK // 128) * c), bev=(m_evBv.n + 1)):
                    e.wait_ge(m_u.h, ur)
                    e.wait_ge(m_evC.h, evc)
                    if c >= 1:
                        e.wait_ge(m_evBs.h, war_s)
                        e.wait_ge(m_evBv.h, war_v)
                    for ct in range(16):
                        inst = e.matmul(psB[0:64, 0:CK],
                                        wx_sb[:, ct * 96:ct * 96 + 64],
                                        u_bf[:, ct * CK:(ct + 1) * CK],
                                        start=(ct == 0), stop=(ct == 15))
                    inst.then_inc(m_wx.h, 1)
                    e.wait_ge(m_evBs.h, war_s + 1)
                    for ct in range(16):
                        inst = e.matmul(psB[0:16, 256:256 + CK],
                                        wx_sb[:, ct * 96 + 64:ct * 96 + 80],
                                        u_bf[:, ct * CK:(ct + 1) * CK],
                                        start=(ct == 0), stop=(ct == 15))
                    inst.then_inc(m_wx.h, 1)
                    e.wait_ge(m_evBv.h, bev)
                    for ct in range(16):
                        inst = e.matmul(psB[0:16, 256:256 + CK],
                                        wx_sb[:, ct * 96 + 80:ct * 96 + 96],
                                        u_bf[:, ct * CK:(ct + 1) * CK],
                                        start=(ct == 0), stop=(ct == 15))
                    inst.then_inc(m_wx.h, 1)
                P2S.add("tensor", f_wx)
                m_wx.bump(3)
                wx_done = m_wx.n

                def f_evdr(e, wr=(wx_done - 2),
                           wdt_prev=(m_wdt.n if c >= 1 else None)):
                    e.wait_ge(m_wx.h, wr)
                    if wdt_prev is not None:
                        e.wait_ge(m_wdt.h, wdt_prev)
                    e.copy(dr_bf[:], psB[0:64, 0:CK]).then_inc(m_evBs.h, 1)
                    _dr(e)
                P2S.add("scalar", f_evdr)
                m_evBs.bump()
                dr_ready = m_evBs.n

                def f_evbc(e, wrB=(wx_done - 1), wrC=wx_done,
                           wexp=(m_exp.n if c >= 1 else None)):
                    e.wait_ge(m_wx.h, wrB)
                    if wexp is not None:
                        e.wait_ge(m_exp.h, wexp)
                    e.tensor_copy(b_sbuf[:], psB[0:16, 256:256 + CK]
                                  ).then_inc(m_evBv.h, 1)
                    _dr(e)
                    e.wait_ge(m_wx.h, wrC)
                    e.tensor_copy(c_sbuf[:], psB[0:16, 256:256 + CK]
                                  ).then_inc(m_evBv.h, 1)
                    _dr(e)
                P2S.add("vector", f_evbc)
                m_evBv.bump(2)
                bc_ready = m_evBv.n

                for src, dst in ((b_sbuf, bhat), (c_sbuf, chat)):
                    for n in range(NST):
                        def f_xp(e, src=src, n=n, c=c,
                                 br=(bc_ready - (1 if src is b_sbuf else 0)),
                                 afree=(m_evA.n - 1), war=m_evE.n):
                            e.wait_ge(m_evBv.h, br)
                            e.wait_ge(m_evA.h, afree)
                            if war >= 1:
                                e.wait_ge(m_evE.h, war)
                            e.matmul(ps_win[0][:, 0:CK],
                                     oh_sb[:, n * 128:(n + 1) * 128],
                                     src[:], start=True, stop=True
                                     ).then_inc(m_exp.h, 1)
                        P2S.add("tensor", f_xp)
                        m_exp.bump()
                        xp_ready = m_exp.n

                        def f_xpe(e, dst=dst, n=n, xr=xp_ready,
                                  wb=(256 * c if c >= 1 else None)):
                            e.wait_ge(m_exp.h, xr)
                            if wb is not None:
                                e.wait_ge(m_b.h, wb)
                            e.tensor_copy(dst[:, n * CK:(n + 1) * CK],
                                          ps_win[0][:, 0:CK]).then_inc(m_evE.h, 1)
                            _dr(e)
                        P2S.add("vector", f_xpe)
                        m_evE.bump()
                evE_base = m_evE.n - 32

                for dt in range(NDT):
                    def f_wdt(e, dt=dt, c=c, drr=dr_ready, war=m_spr.n,
                              bcf=bc_ready):
                        e.wait_ge(m_evBs.h, drr)
                        e.wait_ge(m_evBv.h, bcf)
                        if war >= 1:
                            e.wait_ge(m_spr.h, war)
                        e.matmul(psB[:, 0:CK],
                                 wdt_sb[:, dt * 128:(dt + 1) * 128],
                                 dr_bf[:], start=True, stop=True
                                 ).then_inc(m_wdt.h, 1)
                    P2S.add("tensor", f_wdt)
                    m_wdt.bump()
                    dpre_ready = m_wdt.n

                    def f_sp(e, dt=dt, dr=dpre_ready, war=m_dlt.n):
                        e.wait_ge(m_wdt.h, dr)
                        if war >= 1:
                            e.wait_ge(m_dlt.h, war)
                        bcol = bdt_sb[:, dt:dt + 1]
                        e.activation(sp1[:], psB[:, 0:CK], AF.Abs,
                                     bias=bcol)
                        _dr(e)
                        e.activation(sp2[:], psB[:, 0:CK], AF.Relu,
                                     bias=bcol).then_inc(m_spr.h, 1)
                        _dr(e)
                        e.activation(delta[:], sp1[:], AF.Exp, scale=CV_NONE)
                        _dr(e)
                        e.activation(sp1[:], delta[:], AF.Ln, bias=CV_ONE
                                     ).then_inc(m_dsp.h, 1)
                        _dr(e)
                    P2S.add("scalar", f_sp)
                    m_spr.bump()
                    m_dsp.bump()
                    sp_ready = m_dsp.n

                    def f_dlt(e, sr=sp_ready, wa=m_a.n, wz=m_z.n):
                        e.wait_ge(m_dsp.h, sr)
                        if wa >= 1:
                            e.wait_ge(m_a.h, wa)
                            e.wait_ge(m_z.h, wz)
                        e.tensor_add(delta[:], sp2[:], sp1[:]).then_inc(m_dlt.h, 1)
                        _dr(e)
                    P2S.add("vector", f_dlt)
                    m_dlt.bump()
                    dlt_ready = m_dlt.n

                    def f_z(e, dt=dt, dr=dlt_ready):
                        e.wait_ge(m_dlt.h, dr)
                        e.tensor_mul(z_sb[:], delta[:],
                                     u_bf[:, dt * CK:(dt + 1) * CK]
                                     ).then_inc(m_z.h, 1)
                        _dr(e)
                    P2S.add("gpsimd", f_z)
                    m_z.bump()
                    z_ready = m_z.n

                    def f_yi(e, dt=dt, ur=u_all):
                        e.wait_ge(m_u.h, ur)
                        e.tensor_scalar(yacc[:], u_bf[:, dt * CK:(dt + 1) * CK],
                                        dvec_sb[:, dt:dt + 1], None, ALU.mult)
                        _dr(e)
                    P2S.add("vector", f_yi)

                    for n in range(NST):
                        par = g_dtn % 3
                        spar = g_dtn % 2
                        on_gp = (n % 3 == 1)

                        def f_a(e, dt=dt, n=n, par=par, dr=dlt_ready,
                                war=(m_scan.n - 2)):
                            e.wait_ge(m_dlt.h, dr)
                            if war >= 1:
                                e.wait_ge(m_scan.h, war)
                            e.activation(asb[par][:], delta[:], AF.Exp,
                                         scale=a_sb[:, dt * 16 + n:dt * 16 + n + 1]
                                         ).then_inc(m_a.h, 1)
                            _dr(e)
                        P2S.add("scalar", f_a)
                        m_a.bump()
                        a_ready = m_a.n

                        def f_b(e, n=n, par=par, zr=z_ready,
                                be=(evE_base + n + 1), war=(m_scan.n - 2)):
                            e.wait_ge(m_z.h, zr)
                            e.wait_ge(m_evE.h, be)
                            if war >= 1:
                                e.wait_ge(m_scan.h, war)
                            e.tensor_mul(bsb[par][:], z_sb[:],
                                         bhat[:, n * CK:(n + 1) * CK]
                                         ).then_inc(m_b.h, 1)
                            _dr(e)
                        P2S.add("gpsimd", f_b)
                        m_b.bump()
                        b_ready = m_b.n

                        gwar = g_gp.get(g_dtn - 2)

                        def f_scan(e, dt=dt, n=n, par=par, spar=spar,
                                   ar=a_ready, br=b_ready, on_gp=on_gp,
                                   gwar=gwar, ce=(evE_base + 16 + n + 1)):
                            e.wait_ge(m_a.h, ar)
                            e.wait_ge(m_b.h, br)
                            if gwar is not None:
                                e.wait_ge(m_yg.h, gwar)
                            col = dt * 16 + n
                            e.tensor_tensor_scan(ssb[spar][:], asb[par][:],
                                                 bsb[par][:],
                                                 carry[:, col:col + 1],
                                                 ALU.mult, ALU.add
                                                 ).then_inc(m_scan.h, 1)
                            _dr(e)
                            e.tensor_copy(carry[:, col:col + 1],
                                          ssb[spar][:, CK - 1:CK])
                            _dr(e)
                            if not on_gp:
                                e.wait_ge(m_evE.h, ce)
                                e.tensor_mul(ytmp[:], ssb[spar][:],
                                             chat[:, n * CK:(n + 1) * CK])
                                _dr(e)
                                e.tensor_add(yacc[:], yacc[:], ytmp[:])
                                _dr(e)
                        P2S.add("vector", f_scan)
                        m_scan.bump()
                        if on_gp:
                            def f_ym(e, n=n, par=spar, sr=m_scan.n,
                                     ce=(evE_base + 16 + n + 1),
                                     first=(n == 1), dt=dt, c=c,
                                     wy=(m_y.n if (c + dt) > 0 else None)):
                                e.wait_ge(m_scan.h, sr)
                                e.wait_ge(m_evE.h, ce)
                                e.tensor_mul(ytmp2[:], ssb[par][:],
                                             chat[:, n * CK:(n + 1) * CK])
                                _dr(e)
                                if first:
                                    if wy is not None:
                                        e.wait_ge(m_y.h, wy)
                                    e.tensor_copy(yacc2[:], ytmp2[:]
                                                  ).then_inc(m_yg.h, 1)
                                else:
                                    e.tensor_add(yacc2[:], yacc2[:], ytmp2[:]
                                                 ).then_inc(m_yg.h, 1)
                                _dr(e)
                            P2S.add("gpsimd", f_ym)
                            m_yg.bump()
                            g_gp[g_dtn] = m_yg.n
                        g_dtn += 1

                    def f_yb(e, dt=dt, c=c, rr=(evA_base + 16 + dt + 1),
                             ygr=m_yg.n,
                             wwo=(woc_base[c - 1] + 32 + 16 + dt + 1
                                  if c >= 1 else None)):
                        e.wait_ge(m_evA.h, rr)
                        e.wait_ge(m_yg.h, ygr)
                        if wwo is not None:
                            e.wait_ge(m_woc.h, wwo)
                        e.tensor_add(yacc[:], yacc[:], yacc2[:])
                        _dr(e)
                        e.tensor_mul(ybf[:, dt * CK:(dt + 1) * CK],
                                     yacc[:],
                                     res_bf[:, dt * CK:(dt + 1) * CK]
                                     ).then_inc(m_y.h, 1)
                        _dr(e)
                    P2S.add("vector", f_yb)
                    m_y.bump()
                emit_wout(c)
                P2S.mark()

            for eng in Prog.ENGS:
                def pf3(e):
                    e.wait_ge(bar1.h, 5)
                P3S.add(eng, pf3)
            P3S.mark()
            p3_reqs.append(0)

            gih = 0
            ge = 0
            ghf = 0
            gj = 0

            def emit_sx(sc, e_i, gu_prev):
                shared = e_i == NE
                if not shared:
                    for th in range(NTH):
                        def f_sc(e, e_i=e_i, sc=sc, th=th,
                                 war=(n_scev.n if n_sc.n >= 1 else None),
                                 slw=(n_sl.n - NTH if n_sl.n >= NTH else None)):
                            if war is not None:
                                e.wait_ge(n_scev.h, war)
                            if slw is not None:
                                e.wait_ge(n_sl.h, slw)
                            e.matmul(
                                ps_g[0][:],
                                oh_sb[:, e_i * 128:(e_i + 1) * 128],
                                scores[:, sc * SC + th * 512:
                                       sc * SC + th * 512 + 512],
                                start=True, stop=True).then_inc(n_sc.h, 1)
                        P3S.add("tensor", f_sc)
                        n_sc.bump()
                        sc_ready = n_sc.n

                        def f_sce(e, e_i=e_i, th=th, sr=sc_ready):
                            e.wait_ge(n_sc.h, sr)
                            e.tensor_copy(
                                shat[e_i % 2][:, th * 512:(th + 1) * 512],
                                ps_g[0][:]).then_inc(n_scev.h, 1)
                            _dr(e)
                        P3S.add("vector", f_sce)
                        n_scev.bump()
                shat_ready = n_scev.n
                x_base = n_X.n
                for j in range(8):
                    spp = n_str[emit_sx.gj % 2]

                    def f_str(e, j=j, sc=sc, gj=emit_sx.gj, spp=spp,
                              war=(n_X.n - 1)):
                        if gj >= 2:
                            e.wait_ge(n_X.h, war)
                        e.dma_start(strb[j % 2][:],
                                    xnt_d[j, :, sc * SC:(sc + 1) * SC]
                                    ).then_inc(spp.h, 16)
                    P3S.add("sync", f_str)
                    spp.bump(16)
                    emit_sx.gj += 1
                    str_ready = spp.n

                    def f_X(e, j=j, shared=shared, spp=spp, sr=str_ready,
                            shr=shat_ready, pg=gu_prev, e_i=e_i):
                        e.wait_ge(spp.h, sr)
                        e.wait_ge(n_gu.h, pg)
                        if not shared:
                            e.wait_ge(n_scev.h, shr)
                            e.tensor_mul(xe[:, j * SC:(j + 1) * SC],
                                         strb[j % 2][:],
                                         shat[e_i % 2][:]).then_inc(n_X.h, 1)
                        else:
                            e.tensor_copy(xe[:, j * SC:(j + 1) * SC],
                                          strb[j % 2][:]).then_inc(n_X.h, 1)
                        _dr(e)
                    P3S.add("vector", f_X)
                    n_X.bump()
                return x_base
            emit_sx.gj = 0

            for sc in range(NSC):
                x_base = emit_sx(sc, 0, n_gu.n)

                for e_i in range(NE + 1):
                    shared = e_i == NE

                    prev_dn_done = n_dn.n
                    sl_base = n_sl.n
                    h_base0 = n_h.n
                    for i in range(16):
                        wgp = n_wg[gih % 2]

                        def f_wgl(e, e_i=e_i, i=i, gih=gih, wgp=wgp,
                                  shared=shared,
                                  war=(n_gu.n - 2 * NTH if gih >= 2 else None)):
                            if war is not None:
                                e.wait_ge(n_gu.h, max(war, 0))
                            if shared:
                                gsrc = wgt_d[i].rearrange("p c f -> p (c f)")
                                usrc = wut_d[i].rearrange("p c f -> p (c f)")
                            else:
                                gsrc = gup_d[e_i, i].rearrange("p c f -> p (c f)")
                                usrc = gup_d[e_i, 16 + i].rearrange("p c f -> p (c f)")
                            e.dma_start(wgt_t[gih % 2][:], gsrc).then_inc(wgp.h, 16)
                            e.dma_start(wut_t[gih % 2][:], usrc).then_inc(wgp.h, 16)
                        P3S.add("gpsimd", f_wgl)
                        wgp.bump(32)
                        wg_ready = wgp.n

                        for th in range(NTH):
                            war_sl = sl_base + (NTH * i - NTH if th == 0
                                                else NTH * i + th)
                            war_h = h_base0 + (NTH * i - NTH if th == 0
                                               else NTH * i + th)

                            def f_gu(e, i=i, th=th, gih=gih, wgp=wgp,
                                     wr=wg_ready, xb=x_base,
                                     wsl=max(war_sl, 0), wh=max(war_h, 0),
                                     scv=(n_scev.n if i == 0 else None)):
                                e.wait_ge(wgp.h, wr)
                                if wsl >= 1:
                                    e.wait_ge(n_sl.h, wsl)
                                if wh >= 1:
                                    e.wait_ge(n_h.h, wh)
                                if i == 0 and scv is not None:
                                    e.wait_ge(n_scev.h, scv)
                                for ct in range(8):
                                    e.wait_ge(n_X.h, xb + ct + 1)
                                    rhs = xe[:, ct * SC + th * 512:
                                             ct * SC + th * 512 + 512]
                                    ig = e.matmul(ps_g[i % 2][:],
                                                  wgt_t[gih % 2][:, ct * 128:
                                                                 (ct + 1) * 128],
                                                  rhs, start=(ct == 0),
                                                  stop=(ct == 7))
                                ig.then_inc(n_gu.h, 1)
                                for ct in range(8):
                                    rhs = xe[:, ct * SC + th * 512:
                                             ct * SC + th * 512 + 512]
                                    iu = e.matmul(ps_u[i % 2][:],
                                                  wut_t[gih % 2][:, ct * 128:
                                                                 (ct + 1) * 128],
                                                  rhs, start=(ct == 0),
                                                  stop=(ct == 7))
                                iu.then_inc(n_gu.h, 1)
                            P3S.add("tensor", f_gu)
                            n_gu.bump(2)
                            gu_ready = n_gu.n

                            def f_sl(e, i=i, gr=(gu_ready - 1), whp=n_h.n):
                                e.wait_ge(n_gu.h, gr)
                                if whp >= 1:
                                    e.wait_ge(n_h.h, whp)
                                e.activation(sgtmp[:], ps_g[i % 2][:], AF.Silu
                                             ).then_inc(n_sl.h, 1)
                                _dr(e)
                            P3S.add("scalar", f_sl)
                            n_sl.bump()
                            sl_ready = n_sl.n

                            def f_h(e, i=i, th=th, gr=gu_ready, slr=sl_ready,
                                    pd=prev_dn_done):
                                e.wait_ge(n_gu.h, gr)
                                e.wait_ge(n_sl.h, slr)
                                e.wait_ge(n_dn.h, pd)
                                e.tensor_mul(
                                    h_sb[:, i * SC + th * 512:
                                         i * SC + th * 512 + 512],
                                    sgtmp[:], ps_u[i % 2][:]).then_inc(n_h.h, 1)
                                _dr(e)
                            P3S.add("vector", f_h)
                            n_h.bump()
                        gih += 1
                    h_base = n_h.n - 16 * NTH

                    if e_i < NE:
                        x_base = emit_sx(sc, e_i + 1, n_gu.n)

                    wdsem = n_wdn[1] if shared else n_wdn[0]
                    if shared:
                        def f_wdl(e, war=(n_dn.n if ge >= 1 else None)):
                            if war is not None:
                                e.wait_ge(n_dn.h, war)
                            e.dma_start(
                                wdn_sb[:],
                                wdsh_d.rearrange("p c f -> p (c f)"),
                            ).then_inc(n_wdn[1].h, 16)
                        P3S.add("gpsimd", f_wdl)
                    else:
                        def f_wdl(e, e_i=e_i, war=(n_dn.n if ge >= 1 else None)):
                            if war is not None:
                                e.wait_ge(n_dn.h, war)
                            e.dma_start(wdn_sb[:], dwn16_d[e_i]
                                        ).then_inc(n_wdn[0].h, 16)
                        P3S.add("sync", f_wdl)
                    wdsem.bump(16)
                    wdn_ready = wdsem.n

                    for tt in range(NTT):
                        for half in range(2):
                            def f_dn(e, tt=tt, half=half, ghf=ghf, hb=h_base,
                                     wdsem=wdsem, wr=wdn_ready, war=n_acc.n):
                                e.wait_ge(wdsem.h, wr)
                                if ghf >= 1:
                                    e.wait_ge(n_acc.h, war)
                                for ct in range(16):
                                    e.wait_ge(n_h.h, hb + NTH * (ct + 1))
                                    lhs = h_sb[:, ct * SC + tt * 128:
                                               ct * SC + tt * 128 + 128]
                                    inst = e.matmul(
                                        ps_d[:],
                                        lhs,
                                        wdn_sb[:, ct * 1024 + half * 512:
                                               ct * 1024 + half * 512 + 512],
                                        start=(ct == 0), stop=(ct == 15))
                                inst.then_inc(n_dn.h, 1)
                            P3S.add("tensor", f_dn)
                            n_dn.bump()
                            dn_ready = n_dn.n
                            ghf += 1

                            if e_i == 0:
                                def f_acc(e, tt=tt, half=half, sc=sc,
                                          dr=dn_ready,
                                          war=(NTT * sc if sc >= 1 else None)):
                                    e.wait_ge(n_dn.h, dr)
                                    if war is not None:
                                        e.wait_ge(n_ep.h, war)
                                    e.tensor_copy(
                                        acc[:, tt * 1024 + half * 512:
                                            tt * 1024 + half * 512 + 512],
                                        ps_d[:]).then_inc(n_acc.h, 1)
                                    _dr(e)
                            else:
                                def f_acc(e, tt=tt, half=half, dr=dn_ready):
                                    e.wait_ge(n_dn.h, dr)
                                    sl = acc[:, tt * 1024 + half * 512:
                                             tt * 1024 + half * 512 + 512]
                                    e.tensor_add(sl, sl, ps_d[:]
                                                 ).then_inc(n_acc.h, 1)
                                    _dr(e)
                            P3S.add("vector", f_acc)
                            n_acc.bump()
                    ge += 1
                    P3S.mark()
                    p3_reqs.append(0)

                acc_done = n_acc.n
                for tt in range(NTT):
                    idx = sc * NTT + tt
                    xap = n_xat[idx % 2]
                    oxp = n_ox[idx % 2]
                    opp = n_out[idx % 2]
                    c_need = (sc * SC + tt * 128) // CK
                    (as0, ac0), (as1, ac1) = ast_at[c_need]

                    def f_lx(e, sc=sc, tt=tt, idx=idx, xap=xap, opp=opp,
                             as0=as0, ac0=ac0, as1=as1, ac1=ac1,
                             owar=(16 * (idx // 2) if idx >= 2 else None)):
                        if owar is not None:
                            e.wait_ge(opp.h, owar)
                        e.wait_ge(as0.h, ac0)
                        e.wait_ge(as1.h, ac1)
                        row = sc * SC + tt * 128
                        e.dma_start(att_t[idx % 2][:], attn_d[row:row + 128, :]
                                    ).then_inc(xap.h, 16)
                    P3S.add("sync", f_lx)
                    xap.bump(16)
                    xat_ready = xap.n

                    def f_ep(e, tt=tt, idx=idx, xap=xap,
                             ar=(acc_done - 2 * NTT + 2 * (tt + 1)),
                             xr=xat_ready):
                        gt = idx % 2
                        e.wait_ge(n_acc.h, ar)
                        e.wait_ge(xap.h, xr)
                        e.tensor_add(att_t[gt][:],
                                     acc[:, tt * 1024:(tt + 1) * 1024],
                                     att_t[gt][:]).then_inc(n_ep.h, 1)
                        _dr(e)
                    P3S.add("vector", f_ep)
                    n_ep.bump()
                    ep_ready = n_ep.n

                    def f_xadd(e, sc=sc, tt=tt, idx=idx, oxp=oxp, er=ep_ready):
                        e.wait_ge(n_ep.h, er)
                        row = sc * SC + tt * 128
                        e.dma_start(att_t[idx % 2][:], x_d[row:row + 128, :],
                                    accum_op=ALU.add).then_inc(oxp.h, 16)
                    P3S.add("gpsimd", f_xadd)
                    oxp.bump(16)
                    ox_ready = oxp.n

                    def f_out(e, sc=sc, tt=tt, idx=idx, opp=opp, oxp=oxp,
                              er=ox_ready):
                        e.wait_ge(oxp.h, er)
                        gt = idx % 2
                        row = sc * SC + tt * 128
                        e.dma_start(out_d[row:row + 128, :], att_t[gt][:]
                                    ).then_inc(opp.h, 16)
                    P3S.add("sync", f_out)
                    opp.bump(16)
                    p3_reqs[-1] = c_need + 2
                    P3S.mark()
                    p3_reqs.append(0)

            PM = seg_merge(P2S, P3S, p3_reqs)

            def f_fin(e, w0=n_out[0].n, w1=n_out[1].n):
                e.wait_ge(n_out[0].h, w0)
                e.wait_ge(n_out[1].h, w1)
            PM.add("sync", f_fin)

            blk23 = p23.enter_context(nc.Block("p23"))
            PM.run(blk23)

    return nc


def _host_layout(inputs):
    f32 = np.float32
    g = lambda k: np.asarray(inputs[k])
    conv_w = g("L0_attn_conv_w")
    cw = np.ascontiguousarray(
        conv_w[:, 0, :].T.reshape(16, 128, 4).transpose(1, 0, 2).reshape(128, 64))
    cb = np.ascontiguousarray(g("L0_attn_conv_b").reshape(16, 128).T)
    bdt = np.ascontiguousarray(g("L0_attn_b_dt").reshape(16, 128).T)
    dvec = np.ascontiguousarray(g("L0_attn_D").reshape(16, 128).T)
    alog = np.ascontiguousarray(
        g("L0_attn_A_log").reshape(16, 128, 16).transpose(1, 0, 2).reshape(128, 256))
    onehot = np.zeros((16, 16 * 128), ml_dtypes.bfloat16)
    for n in range(16):
        onehot[n, n * 128:(n + 1) * 128] = 1.0
    def tile_lhs(w, nin, nout):
        return np.ascontiguousarray(
            w.reshape(nin, 128, nout, 128).transpose(2, 1, 0, 3))

    def tile_rhs(w, nin, k):
        return np.ascontiguousarray(
            w.reshape(nin, 128, k).transpose(1, 0, 2))

    gup = g("L0_ffn_gate_up_proj")
    gup_t = np.ascontiguousarray(
        gup.reshape(16, 8, 128, 32, 128).transpose(0, 3, 2, 1, 4))
    dwn = g("L0_ffn_down_proj")
    dwn_t = np.ascontiguousarray(
        dwn.reshape(16, 16, 128, 1024).transpose(0, 2, 1, 3))
    return {
        "win": tile_lhs(g("L0_attn_W_in"), 8, 32),
        "wx": np.ascontiguousarray(g("L0_attn_W_x")),
        "wdt": np.ascontiguousarray(g("L0_attn_W_dt")),
        "convw": cw, "convb": cb, "bdt": bdt, "alog": alog, "dvec": dvec,
        "wout": tile_rhs(g("L0_attn_W_out"), 16, 1024),
        "wrt": np.ascontiguousarray(g("L0_ffn_w_router").T),
        "gup": gup_t,
        "dwn": dwn_t,
        "wgt": tile_lhs(np.ascontiguousarray(g("L0_ffn_w_gate").T), 8, 16),
        "wut": tile_lhs(np.ascontiguousarray(g("L0_ffn_w_up").T), 8, 16),
        "wdsh": tile_rhs(np.ascontiguousarray(g("L0_ffn_w_down").T), 16, 1024),
        "oh": onehot,
        "idb": np.eye(128, dtype=ml_dtypes.bfloat16),
        "idf": np.eye(128, dtype=f32),
        "cvals": np.tile(np.array([[EPS, 1.0, 1.0 / D, -0.5, -1.0, 0.0, 0.0, 0.0]],
                                  dtype=f32), (128, 1)),
    }


def kernel(**inputs):
    common = _host_layout(inputs)
    x = np.asarray(inputs["L0_attn_x"])
    nc = build_program()
    in_maps = []
    for b in range(B):
        m = dict(common)
        m["x"] = np.ascontiguousarray(x[b])
        in_maps.append(m)
    res = run_bass_kernel_spmd(nc, in_maps, core_ids=list(range(B)))
    return np.stack([r["out"] for r in res.results], axis=0)


if __name__ == "__main__":
    import time
    t0 = time.time()
    nc = build_program()
    print("built", nc.n_instructions(), "instrs in", round(time.time() - t0, 1), "s")


# version 11
# speedup vs baseline: 1.5492x; 1.5492x over previous
import numpy as np
import ml_dtypes
from contextlib import ExitStack

import concourse.bass as bass
import concourse.mybir as mybir
from concourse.bass_utils import run_bass_kernel_spmd

F32 = mybir.dt.float32
BF16 = mybir.dt.bfloat16
AF = mybir.ActivationFunctionType
ALU = mybir.AluOpType
AX = mybir.AxisListType

B, T, D = 8, 4096, 1024
DIN = 2048
NST = 16
RNK = 64
NE = 16
FF = 2048
NT = T // 128
CK = 512
NCH = T // CK
NDT = DIN // 128
SC = 1024
NSC = T // SC
EPS = 1e-5

DEBUG_SCRATCH_OUT = False
USE_DRAINS = False


def _dr(e, always=False):
    if USE_DRAINS or always:
        e.drain()
class Ctr:
    def __init__(self, h):
        self.h = h
        self.n = 0

    def bump(self, k=1):
        self.n += k
        return self.n


class Prog:
    ENGS = ("sync", "tensor", "vector", "scalar", "gpsimd")

    def __init__(self):
        self.ops = {k: [] for k in self.ENGS}

    def add(self, eng, fn):
        self.ops[eng].append(fn)

    def run(self, block):
        for name in self.ENGS:
            fns = self.ops[name]
            if not fns:
                continue

            def body(e, fns=fns):
                for fn in fns:
                    fn(e)

            getattr(block, name)(body)


def build_program():
    nc = bass.Bass("TRN2", target_bir_lowering=False)

    ap = lambda h: h.ap()
    x_d = ap(nc.dram_tensor("x", [T, D], F32, kind="ExternalInput"))
    win_d = ap(nc.dram_tensor("win", [32, 128, 8, 128], F32, kind="ExternalInput"))
    wx_d = ap(nc.dram_tensor("wx", [DIN, 96], F32, kind="ExternalInput"))
    wdt_d = ap(nc.dram_tensor("wdt", [RNK, DIN], F32, kind="ExternalInput"))
    convw_d = ap(nc.dram_tensor("convw", [128, 64], F32, kind="ExternalInput"))
    convb_d = ap(nc.dram_tensor("convb", [128, 16], F32, kind="ExternalInput"))
    bdt_d = ap(nc.dram_tensor("bdt", [128, 16], F32, kind="ExternalInput"))
    alog_d = ap(nc.dram_tensor("alog", [128, 256], F32, kind="ExternalInput"))
    dvec_d = ap(nc.dram_tensor("dvec", [128, 16], F32, kind="ExternalInput"))
    wout_d = ap(nc.dram_tensor("wout", [128, 16, 1024], F32, kind="ExternalInput"))
    wrt_d = ap(nc.dram_tensor("wrt", [D, NE], F32, kind="ExternalInput"))
    gup_d = ap(nc.dram_tensor("gup", [NE, 32, 128, 8, 128], F32, kind="ExternalInput"))
    dwn_d = ap(nc.dram_tensor("dwn", [NE, 128, 16, 1024], F32, kind="ExternalInput"))
    wgt_d = ap(nc.dram_tensor("wgt", [16, 128, 8, 128], F32, kind="ExternalInput"))
    wut_d = ap(nc.dram_tensor("wut", [16, 128, 8, 128], F32, kind="ExternalInput"))
    wdsh_d = ap(nc.dram_tensor("wdsh", [128, 16, 1024], F32, kind="ExternalInput"))
    oh_d = ap(nc.dram_tensor("oh", [16, 16 * 128], BF16, kind="ExternalInput"))
    idb_d = ap(nc.dram_tensor("idb", [128, 128], BF16, kind="ExternalInput"))
    idf_d = ap(nc.dram_tensor("idf", [128, 128], F32, kind="ExternalInput"))
    cv_d = ap(nc.dram_tensor("cvals", [128, 8], F32, kind="ExternalInput"))

    skind = "ExternalOutput" if DEBUG_SCRATCH_OUT else "Internal"
    xnt_d = ap(nc.dram_tensor("xnt_s", [8, 128, T], BF16, kind=skind))
    attn_d = ap(nc.dram_tensor("attn_s", [T, D], F32, kind=skind))
    out_d = ap(nc.dram_tensor("out", [T, D], F32, kind="ExternalOutput"))

    ctx = ExitStack()
    with ctx:
        def sbuf(name, shape, dtype):
            return ctx.enter_context(nc.sbuf_tensor(name, shape, dtype))

        def sem(name):
            return Ctr(ctx.enter_context(nc.semaphore(name)))

        oh_sb = sbuf("oh_sb", [16, 16 * 128], BF16)
        idb_sb = sbuf("idb_sb", [128, 128], BF16)
        idf_sb = sbuf("idf_sb", [128, 128], F32)
        scores = sbuf("scores", [16, T], BF16)
        a_sb = sbuf("a_sb", [128, 256], F32)
        dvec_sb = sbuf("dvec_sb", [128, 16], F32)
        convw_sb = sbuf("convw_sb", [128, 64], F32)
        convb_sb = sbuf("convb_sb", [128, 16], F32)
        bdt_sb = sbuf("bdt_sb", [128, 16], F32)
        wx_sb = sbuf("wx_sb", [128, 16 * 96], BF16)
        wdt_sb = sbuf("wdt_sb", [64, 16 * 128], BF16)
        wrt_sb = sbuf("wrt_sb", [128, 8 * 16], BF16)
        cv_sb = sbuf("cv_sb", [128, 8], F32)
        CV_EPS, CV_ONE, CV_INVD, CV_NHALF, CV_NONE = (
            cv_sb[:, 0:1], cv_sb[:, 1:2], cv_sb[:, 2:3], cv_sb[:, 3:4], cv_sb[:, 4:5])

        c_ld = sem("c_ld")
        c_gld = sem("c_gld")
        c_prep = sem("c_prep")
        c_prep2 = sem("c_prep2")
        bar1 = sem("bar1")
        bar2 = sem("bar2")

        s1x = [sem("s1x0"), sem("s1x1")]
        s1v = sem("s1v")
        s1s = sem("s1s")
        s1t = sem("s1t")
        s1e = sem("s1e")
        s1w = [sem("s1w0"), sem("s1w1")]
        s1r = sem("s1r")
        s1sc = sem("s1sc")

        m_xnc = sem("m_xnc")
        m_win = [sem("m_win0"), sem("m_win1")]
        m_wres = sem("m_wres")
        m_wiA = sem("m_wiA")
        m_evA = sem("m_evA")
        m_hist = sem("m_hist")
        m_conv = sem("m_conv")
        m_u = sem("m_u")
        m_wx = sem("m_wx")
        m_evBs = sem("m_evBs")
        m_evBv = sem("m_evBv")
        m_exp = sem("m_exp")
        m_evE = sem("m_evE")
        m_wdt = sem("m_wdt")
        m_spr = sem("m_spr")
        m_dsp = sem("m_dsp")
        m_dlt = sem("m_dlt")
        m_z = sem("m_z")
        m_a = sem("m_a")
        m_b = sem("m_b")
        m_yg = sem("m_yg")
        m_scan = sem("m_scan")
        m_y = sem("m_y")
        m_wo = sem("m_wo")
        m_evC = sem("m_evC")
        m_ast = [sem("m_ast0"), sem("m_ast1")]

        n_xnr = sem("n_xnr")
        n_X = sem("n_X")
        n_sc = sem("n_sc")
        n_scev = sem("n_scev")
        n_wg = [sem("n_wg0"), sem("n_wg1")]
        n_wdn = [sem("n_wdn0"), sem("n_wdn1")]
        n_gu = sem("n_gu")
        n_sl = sem("n_sl")
        n_h = sem("n_h")
        n_dn = sem("n_dn")
        n_acc = sem("n_acc")
        n_xat = [sem("n_xat0"), sem("n_xat1")]
        n_ep = sem("n_ep")
        n_ox = [sem("n_ox0"), sem("n_ox1")]
        n_out = [sem("n_out0"), sem("n_out1")]

        with ExitStack() as p1:
            sb1 = lambda n_, s, d: p1.enter_context(nc.sbuf_tensor(n_, s, d))
            ps1 = lambda n_, s, d=F32: p1.enter_context(nc.psum_tensor(n_, s, d))

            xin = [sb1(f"xin{i}", [128, D], F32) for i in range(2)]
            junk = sb1("junk", [128, D], F32)
            stat = sb1("stat", [128, 8], F32)
            xn_t = sb1("xn_t", [128, D], BF16)
            xnt_sb = sb1("xnt_sb", [128, 8 * 512], BF16)
            sgt = sb1("sgt", [128, 16], F32)
            mskt = sb1("mskt", [128, 16], F32)
            sct = sb1("sct", [128, 16], F32)
            mxt = sb1("mxt", [128, 1], F32)
            atmp = sb1("atmp", [128, 256], F32)
            tp_ps = [ps1(f"tp_ps{i}", [128, 1024], BF16) for i in range(2)]
            lg_ps = ps1("lg_ps", [128, 512], F32)
            st_ps = ps1("st_ps", [16, 512], F32)

            P1 = Prog()

            def c_sync(e):
                for dst, src in ((oh_sb, oh_d), (idf_sb, idf_d),
                                 (convw_sb, convw_d), (convb_sb, convb_d),
                                 (bdt_sb, bdt_d), (dvec_sb, dvec_d),
                                 (atmp, alog_d), (cv_sb, cv_d)):
                    e.dma_start(dst[:], src[:]).then_inc(c_ld.h, 16)
            P1.add("sync", c_sync)
            c_ld.bump(16 * 8)

            def c_gp(e):
                e.dma_start(idb_sb[:], idb_d[:]).then_inc(c_gld.h, 16)
                e.dma_start(
                    wx_sb[:].rearrange("p (c k) -> p c k", c=16),
                    wx_d.rearrange("(c p) k -> p c k", p=128),
                ).then_inc(c_gld.h, 16)
                e.dma_start(
                    wdt_sb[:].rearrange("p (c k) -> p c k", c=16),
                    wdt_d.rearrange("p (c k) -> p c k", c=16),
                ).then_inc(c_gld.h, 16)
                e.dma_start(
                    wrt_sb[:].rearrange("p (c k) -> p c k", c=8),
                    wrt_d.rearrange("(c p) k -> p c k", p=128),
                ).then_inc(c_gld.h, 16)
            P1.add("gpsimd", c_gp)
            c_gld.bump(16 * 4)

            def c_sc(e):
                e.wait_ge(c_ld.h, 16 * 8)
                e.activation(atmp[:], atmp[:], AF.Exp).then_inc(c_prep.h, 1)
                _dr(e, True)
            P1.add("scalar", c_sc)
            c_prep.bump()

            def c_vec(e):
                e.wait_ge(c_prep.h, 1)
                e.tensor_scalar_mul(a_sb[:], atmp[:], -1.0).then_inc(c_prep2.h, 1)
                _dr(e, True)
            P1.add("vector", c_vec)
            c_prep2.bump()

            v_done = {}
            s_done = {}
            sg_done = {}
            e_done = {}
            w_done = {}
            mul_done = {}
            sce_done = {}
            rt_done = {}
            stp_done = {}

            for tt in range(NT):
                buf = xin[tt % 2]

                xp = s1x[tt % 2]

                def f_sync(e, tt=tt, buf=buf, xp=xp,
                           wv=(v_done.get(tt - 2)), ws=(s_done.get(tt - 2))):
                    if wv is not None:
                        e.wait_ge(s1v.h, wv)
                        e.wait_ge(s1s.h, ws)
                    e.dma_start(buf[:], x_d[tt * 128:(tt + 1) * 128, :]
                                ).then_inc(xp.h, 16)
                P1.add("sync", f_sync)
                xp.bump(16)
                x_ready = xp.n

                def f_v1(e, buf=buf, xp=xp, xr=x_ready, ws=s_done.get(tt - 1)):
                    e.wait_ge(xp.h, xr)
                    if ws is not None:
                        e.wait_ge(s1s.h, ws)
                    e.tensor_reduce(stat[:, 0:1], buf[:], AX.X, ALU.add)
                    _dr(e, True)
                    e.tensor_scalar_mul(stat[:, 0:1], stat[:, 0:1], -1.0 / D
                                        ).then_inc(s1v.h, 1)
                    _dr(e, True)
                P1.add("vector", f_v1)
                s1v.bump()
                nm_ready = s1v.n

                def f_s1(e, buf=buf, xp=xp, xr=x_ready, nmr=nm_ready):
                    e.wait_ge(xp.h, xr)
                    e.wait_ge(s1v.h, nmr)
                    e.activation(junk[:], buf[:], AF.Square,
                                 bias=stat[:, 0:1], accum_out=stat[:, 1:2])
                    _dr(e, True)
                    e.activation(stat[:, 2:3], stat[:, 1:2], AF.Ln,
                                 scale=CV_INVD, bias=CV_EPS)
                    _dr(e, True)
                    e.activation(stat[:, 3:4], stat[:, 2:3], AF.Exp,
                                 scale=CV_NHALF).then_inc(s1s.h, 1)
                    _dr(e, True)
                P1.add("scalar", f_s1)
                s1s.bump()
                rstd_ready = s1s.n
                s_done[tt] = s1s.n

                def f_xn(e, buf=buf, rr=rstd_ready):
                    e.wait_ge(s1s.h, rr)
                    e.tensor_scalar(xn_t[:], buf[:], stat[:, 0:1],
                                    stat[:, 3:4], ALU.add, ALU.mult
                                    ).then_inc(s1v.h, 1)
                    _dr(e, True)
                P1.add("vector", f_xn)
                s1v.bump()
                xn_ready = s1v.n
                v_done[tt] = s1v.n

                for j in range(8):
                    tpw = s1t.n - 1

                    def f_t(e, j=j, xr=xn_ready, war=tpw):
                        e.wait_ge(c_gld.h, 16 * 4)
                        e.wait_ge(s1v.h, xr)
                        if war >= 1:
                            e.wait_ge(s1e.h, war)
                        e.transpose(tp_ps[j % 2][:, 0:128],
                                    xn_t[:, j * 128:(j + 1) * 128],
                                    idb_sb[:]).then_inc(s1t.h, 1)
                    P1.add("tensor", f_t)
                    s1t.bump()
                    tp_ready = s1t.n

                    g = tt // 4

                    def f_ev(e, j=j, tt=tt, tr=tp_ready, g=g,
                             ww=w_done.get(g - 1),
                             wr=(8 * g - 1 if g >= 1 else None)):
                        e.wait_ge(s1t.h, tr)
                        if ww is not None:
                            e.wait_ge(s1w[(g - 1) % 2].h, ww)
                        if wr is not None:
                            e.wait_ge(s1r.h, wr)
                        col = j * 512 + (tt % 4) * 128
                        e.tensor_copy(xnt_sb[:, col:col + 128],
                                      tp_ps[j % 2][:, 0:128]).then_inc(s1e.h, 1)
                        _dr(e, True)
                    P1.add("vector", f_ev)
                    s1e.bump()
                e_done[tt] = s1e.n

                def f_rt(e, tt=tt, er=e_done[tt],
                         wm=mul_done.get(tt - 1), wsg=sg_done.get(tt - 1)):
                    e.wait_ge(c_gld.h, 16 * 4)
                    e.wait_ge(s1e.h, er)
                    if wm is not None:
                        e.wait_ge(s1sc.h, wm)
                        e.wait_ge(s1s.h, wsg)
                    for j in range(8):
                        col = j * 512 + (tt % 4) * 128
                        inst = e.matmul(lg_ps[:, 0:16], xnt_sb[:, col:col + 128],
                                        wrt_sb[:, j * 16:(j + 1) * 16],
                                        start=(j == 0), stop=(j == 7))
                    inst.then_inc(s1r.h, 1)
                P1.add("tensor", f_rt)
                s1r.bump()
                rt_done[tt] = s1r.n
                lg_ready = s1r.n

                def f_sg(e, lr=lg_ready):
                    e.wait_ge(s1r.h, lr)
                    e.activation(sgt[:], lg_ps[:, 0:16], AF.Sigmoid).then_inc(s1s.h, 1)
                    _dr(e, True)
                P1.add("scalar", f_sg)
                s1s.bump()
                sg_done[tt] = s1s.n

                def f_msk(e, lr=lg_ready, sgr=sg_done[tt],
                          wst=stp_done.get(tt - 1)):
                    e.wait_ge(s1r.h, lr)
                    if wst is not None:
                        e.wait_ge(s1r.h, wst)
                    e.tensor_reduce(mxt[:], lg_ps[:, 0:16], AX.X, ALU.max)
                    _dr(e, True)
                    e.tensor_scalar(mskt[:], lg_ps[:, 0:16], mxt[:, 0:1], None,
                                    ALU.is_ge)
                    _dr(e, True)
                    e.wait_ge(s1s.h, sgr)
                    e.tensor_mul(sct[:], mskt[:], sgt[:]).then_inc(s1sc.h, 1)
                    _dr(e, True)
                P1.add("vector", f_msk)
                s1sc.bump()
                mul_done[tt] = s1sc.n
                sc_ready = s1sc.n

                def f_sct(e, scr=sc_ready, wse=sce_done.get(tt - 1)):
                    e.wait_ge(s1sc.h, scr)
                    if wse is not None:
                        e.wait_ge(s1sc.h, wse)
                    e.transpose(st_ps[:, 0:128], sct[:], idf_sb[:]).then_inc(s1r.h, 1)
                P1.add("tensor", f_sct)
                s1r.bump()
                stp_done[tt] = s1r.n
                st_ready = s1r.n

                def f_sce(e, sr=st_ready, tt=tt):
                    e.wait_ge(s1r.h, sr)
                    e.tensor_copy(scores[:, tt * 128:(tt + 1) * 128],
                                  st_ps[:, 0:128]).then_inc(s1sc.h, 1)
                    _dr(e, True)
                P1.add("vector", f_sce)
                s1sc.bump()
                sce_done[tt] = s1sc.n

                if tt % 4 == 3:
                    g4 = tt // 4
                    wp = s1w[g4 % 2]

                    def f_wr(e, tt=tt, wp=wp, er=e_done[tt]):
                        e.wait_ge(s1e.h, er)
                        for j in range(8):
                            e.dma_start(
                                xnt_d[j, :, (tt - 3) * 128:(tt + 1) * 128],
                                xnt_sb[:, j * 512:(j + 1) * 512]
                            ).then_inc(wp.h, 16)
                    P1.add("sync", f_wr)
                    wp.bump(16 * 8)
                    w_done[g4] = wp.n

            def b_sync(e, w0=s1w[0].n, w1=s1w[1].n):
                e.wait_ge(s1w[0].h, w0)
                e.wait_ge(s1w[1].h, w1)
                e.nop().then_inc(bar1.h, 1)
            P1.add("sync", b_sync)

            def b_t(e):
                e.nop().then_inc(bar1.h, 1)
            P1.add("tensor", b_t)

            for eng in ("vector", "scalar", "gpsimd"):
                def b_e(e):
                    e.drain().then_inc(bar1.h, 1)
                P1.add(eng, b_e)
            bar1.bump(5)

            blk1 = p1.enter_context(nc.Block("p1"))
            P1.run(blk1)

        with ExitStack() as p2:
            sb2 = lambda n_, s, d: p2.enter_context(nc.sbuf_tensor(n_, s, d))
            ps2 = lambda n_, s, d=F32: p2.enter_context(nc.psum_tensor(n_, s, d))

            xnc = sb2("xnc", [128, 8 * CK], BF16)
            winb = [sb2(f"winb{i}", [128, 8 * 128], BF16) for i in range(2)]
            wout_sb = sb2("wout_sb", [128, 16 * 1024], BF16)
            ring = sb2("ring", [128, 16 * 520], BF16)
            convacc = [sb2(f"convacc{i}", [128, CK], F32) for i in range(2)]
            u_bf = sb2("u_bf", [128, 16 * CK], BF16)
            res_bf = sb2("res_bf", [128, 16 * CK], BF16)
            dr_bf = sb2("dr_bf", [64, CK], BF16)
            b_sbuf = sb2("b_sbuf", [16, CK], BF16)
            c_sbuf = sb2("c_sbuf", [16, CK], BF16)
            bhat = sb2("bhat", [128, 16 * CK], BF16)
            chat = sb2("chat", [128, 16 * CK], BF16)
            sp1 = sb2("sp1", [128, CK], F32)
            sp2 = sb2("sp2", [128, CK], F32)
            delta = sb2("delta", [128, CK], F32)
            z_sb = sb2("z_sb", [128, CK], F32)
            asb = [sb2(f"asb{i}", [128, CK], F32) for i in range(3)]
            bsb = [sb2(f"bsb{i}", [128, CK], F32) for i in range(3)]
            ssb = [sb2(f"ssb{i}", [128, CK], F32) for i in range(2)]
            ytmp = sb2("ytmp", [128, CK], F32)
            ytmp2 = sb2("ytmp2", [128, CK], F32)
            yacc = sb2("yacc", [128, CK], F32)
            yacc2 = sb2("yacc2", [128, CK], F32)
            carry = sb2("carry", [128, 256], F32)
            ybf = sb2("ybf", [128, 16 * CK], BF16)
            attn_t = [sb2("attn_t0", [128, 1024], F32)] * 2

            ps_win = [ps2(f"ps_win{i}", [128, CK]) for i in range(2)]
            ps_dr = ps2("ps_dr", [64, CK])
            ps_bc = ps2("ps_bc", [16, CK])
            ps_exp = ps2("ps_exp", [128, CK])
            ps_dpre = ps2("ps_dpre", [128, CK])
            ps_wo = [ps2(f"ps_wo{i}", [128, 512]) for i in range(2)]

            P2 = Prog()
            NB = 5

            for eng in ("sync", "scalar"):
                def pf(e):
                    e.wait_ge(bar1.h, NB)
                P2.add(eng, pf)

            def pf_t(e):
                e.wait_ge(bar1.h, NB)
                e.wait_ge(c_gld.h, 16 * 4)
            P2.add("tensor", pf_t)

            def pf_v(e):
                e.wait_ge(bar1.h, NB)
                e.memset(carry[:], 0.0)
                _dr(e)
                for dt in range(NDT):
                    e.memset(ring[:, dt * 520:dt * 520 + 3], 0.0)
                e.drain().then_inc(m_hist.h, NDT)
            P2.add("vector", pf_v)
            m_hist.bump(NDT)

            def pf_g(e):
                e.wait_ge(bar1.h, NB)
                for g in range(4):
                    e.dma_start(
                        wout_sb[:].rearrange("p (c k) -> p c k", c=16)
                        [:, g * 4:(g + 1) * 4, :],
                        wout_d[:, g * 4:(g + 1) * 4, :],
                    ).then_inc(m_wres.h, 16)
            P2.add("gpsimd", pf_g)
            m_wres.bump(16 * 4)

            gpt = 0
            g_dtn = 0
            g_gp = {}
            prev_y_base = [None]

            def emit_wout(c, y_base):
                for tt in range(CK // 128):
                    def f_wo(e, tt=tt, c=c, yb=y_base, war=m_evC.n):
                        e.wait_ge(m_wres.h, 64)
                        if c + tt > 0:
                            e.wait_ge(m_evC.h, war)
                        for ct in range(16):
                            e.wait_ge(m_y.h, yb + ct + 1)
                            lhs = ybf[:, ct * CK + tt * 128:ct * CK + tt * 128 + 128]
                            e.matmul(ps_wo[0][:], lhs,
                                     wout_sb[:, ct * 1024:ct * 1024 + 512],
                                     start=(ct == 0), stop=(ct == 15))
                            inst = e.matmul(ps_wo[1][:], lhs,
                                            wout_sb[:, ct * 1024 + 512:
                                                    (ct + 1) * 1024],
                                            start=(ct == 0), stop=(ct == 15))
                        inst.then_inc(m_wo.h, 1)
                    P2.add("tensor", f_wo)
                    m_wo.bump()
                    wo_ready = m_wo.n

                    idxa = c * 4 + tt
                    app = m_ast[idxa % 2]
                    app_prev = m_ast[(idxa - 1) % 2]

                    def f_evc(e, tt=tt, c=c, wr=wo_ready, app_prev=app_prev,
                              war=(app_prev.n if idxa >= 1 else None)):
                        e.wait_ge(m_wo.h, wr)
                        if war is not None:
                            e.wait_ge(app_prev.h, war)
                        gt = (c * 4 + tt) % 2
                        e.copy(attn_t[gt][:, 0:512], ps_wo[0][:])
                        _dr(e)
                        e.copy(attn_t[gt][:, 512:1024], ps_wo[1][:]
                               ).then_inc(m_evC.h, 2)
                        _dr(e)
                    P2.add("scalar", f_evc)
                    m_evC.bump(2)
                    evc_ready = m_evC.n

                    def f_ast(e, tt=tt, c=c, app=app, er=evc_ready):
                        e.wait_ge(m_evC.h, er)
                        gt = (c * 4 + tt) % 2
                        row = c * CK + tt * 128
                        e.dma_start(attn_d[row:row + 128, :], attn_t[gt][:]
                                    ).then_inc(app.h, 16)
                    P2.add("sync", f_ast)
                    app.bump(16)


            for c in range(NCH):
                def f_xnc(e, c=c, war=m_wiA.n):
                    if c >= 1:
                        e.wait_ge(m_wiA.h, war)
                    e.dma_start(
                        xnc[:].rearrange("p (j t) -> p j t", j=8),
                        xnt_d[:, :, c * CK:(c + 1) * CK].transpose([1, 0, 2]),
                    ).then_inc(m_xnc.h, 16)
                P2.add("sync", f_xnc)
                m_xnc.bump(16)
                xnc_ready = m_xnc.n

                for pt in range(32):
                    wpp = m_win[gpt % 2]

                    def f_wl(e, pt=pt, gpt=gpt, wpp=wpp, war=(m_wiA.n - 1)):
                        if gpt >= 2:
                            e.wait_ge(m_wiA.h, war)
                        e.dma_start(
                            winb[gpt % 2][:].rearrange("p (j k) -> p j k", j=8),
                            win_d[pt],
                        ).then_inc(wpp.h, 16)
                    P2.add("gpsimd", f_wl)
                    wpp.bump(16)
                    w_ready = wpp.n

                    def f_mm(e, gpt=gpt, wpp=wpp, xr=xnc_ready, wr=w_ready,
                             war=(m_evA.n - 1)):
                        e.wait_ge(m_xnc.h, xr)
                        e.wait_ge(wpp.h, wr)
                        if gpt >= 2:
                            e.wait_ge(m_evA.h, war)
                        for j in range(8):
                            inst = e.matmul(ps_win[gpt % 2][:],
                                            winb[gpt % 2][:, j * 128:(j + 1) * 128],
                                            xnc[:, j * CK:(j + 1) * CK],
                                            start=(j == 0), stop=(j == 7))
                        inst.then_inc(m_wiA.h, 1)
                    P2.add("tensor", f_mm)
                    m_wiA.bump()
                    mm_ready = m_wiA.n

                    def f_ev(e, pt=pt, gpt=gpt, c=c, mr=mm_ready,
                             hist=(16 * c + (pt % 16) + 1),
                             wy=(16 * (c - 1) + (pt - 16) + 1 if (c >= 1 and pt >= 16) else None)):
                        e.wait_ge(m_wiA.h, mr)
                        if pt < 16:
                            e.wait_ge(m_hist.h, hist)
                            e.copy(ring[:, pt * 520 + 3:pt * 520 + 515],
                                   ps_win[gpt % 2][:]).then_inc(m_evA.h, 1)
                        else:
                            if wy is not None:
                                e.wait_ge(m_y.h, wy)
                            e.activation(res_bf[:, (pt - 16) * CK:(pt - 15) * CK],
                                         ps_win[gpt % 2][:], AF.Silu
                                         ).then_inc(m_evA.h, 1)
                        _dr(e)
                    P2.add("scalar", f_ev)
                    m_evA.bump()
                    gpt += 1

                if c >= 1:
                    emit_wout(c - 1, prev_y_base[0])

                evA_base = m_evA.n - 32

                for dt in range(NDT):
                    def f_cv(e, dt=dt, ev=(evA_base + dt + 1),
                             wu=(m_u.n - 1)):
                        e.wait_ge(m_evA.h, ev)
                        if wu >= 1:
                            e.wait_ge(m_u.h, wu)
                        base = dt * 520
                        ca = convacc[dt % 2]
                        e.tensor_scalar(ca[:], ring[:, base:base + CK],
                                        convw_sb[:, dt * 4:dt * 4 + 1], None,
                                        ALU.mult)
                        _dr(e, True)
                        for k in range(1, 4):
                            inst = e.scalar_tensor_tensor(
                                ca[:], ring[:, base + k:base + k + CK],
                                convw_sb[:, dt * 4 + k:dt * 4 + k + 1],
                                ca[:], ALU.mult, ALU.add)
                            _dr(e, True)
                        inst.then_inc(m_conv.h, 1)
                        e.tensor_copy(ring[:, base:base + 3],
                                      ring[:, base + 512:base + 515]
                                      ).then_inc(m_hist.h, 1)
                        _dr(e)
                    P2.add("vector", f_cv)
                    m_conv.bump()
                    m_hist.bump()
                    cv_ready = m_conv.n

                    def f_u(e, dt=dt, c=c, cr=cv_ready,
                            wwx=(3 * c if c >= 1 else None),
                            wz=(16 * c if c >= 1 else None),
                            wy=(16 * c if c >= 1 else None)):
                        e.wait_ge(m_conv.h, cr)
                        if wwx is not None:
                            e.wait_ge(m_wx.h, wwx)
                            e.wait_ge(m_z.h, wz)
                            e.wait_ge(m_y.h, wy)
                        e.activation(u_bf[:, dt * CK:(dt + 1) * CK],
                                     convacc[dt % 2][:], AF.Silu,
                                     bias=convb_sb[:, dt:dt + 1]
                                     ).then_inc(m_u.h, 1)
                        _dr(e)
                    P2.add("scalar", f_u)
                    m_u.bump()

                u_all = m_u.n

                def f_wx(e, c=c, ur=u_all, war_s=m_evBs.n, war_v=m_evBv.n,
                         bev=(m_evBv.n + 1)):
                    e.wait_ge(m_u.h, ur)
                    if c >= 1:
                        e.wait_ge(m_evBs.h, war_s)
                        e.wait_ge(m_evBv.h, war_v)
                    for ct in range(16):
                        inst = e.matmul(ps_dr[:],
                                        wx_sb[:, ct * 96:ct * 96 + 64],
                                        u_bf[:, ct * CK:(ct + 1) * CK],
                                        start=(ct == 0), stop=(ct == 15))
                    inst.then_inc(m_wx.h, 1)
                    for ct in range(16):
                        inst = e.matmul(ps_bc[:],
                                        wx_sb[:, ct * 96 + 64:ct * 96 + 80],
                                        u_bf[:, ct * CK:(ct + 1) * CK],
                                        start=(ct == 0), stop=(ct == 15))
                    inst.then_inc(m_wx.h, 1)
                    e.wait_ge(m_evBv.h, bev)
                    for ct in range(16):
                        inst = e.matmul(ps_bc[:],
                                        wx_sb[:, ct * 96 + 80:ct * 96 + 96],
                                        u_bf[:, ct * CK:(ct + 1) * CK],
                                        start=(ct == 0), stop=(ct == 15))
                    inst.then_inc(m_wx.h, 1)
                P2.add("tensor", f_wx)
                m_wx.bump(3)
                wx_done = m_wx.n

                def f_evdr(e, wr=(wx_done - 2), wdt_prev=(m_wdt.n if c >= 1 else None)):
                    e.wait_ge(m_wx.h, wr)
                    if wdt_prev is not None:
                        e.wait_ge(m_wdt.h, wdt_prev)
                    e.copy(dr_bf[:], ps_dr[:]).then_inc(m_evBs.h, 1)
                    _dr(e)
                P2.add("scalar", f_evdr)
                m_evBs.bump()
                dr_ready = m_evBs.n

                def f_evbc(e, wrB=(wx_done - 1), wrC=wx_done,
                           wexp=(m_exp.n if c >= 1 else None)):
                    e.wait_ge(m_wx.h, wrB)
                    if wexp is not None:
                        e.wait_ge(m_exp.h, wexp)
                    e.tensor_copy(b_sbuf[:], ps_bc[:]).then_inc(m_evBv.h, 1)
                    _dr(e)
                    e.wait_ge(m_wx.h, wrC)
                    e.tensor_copy(c_sbuf[:], ps_bc[:]).then_inc(m_evBv.h, 1)
                    _dr(e)
                P2.add("vector", f_evbc)
                m_evBv.bump(2)
                bc_ready = m_evBv.n

                for src, dst in ((b_sbuf, bhat), (c_sbuf, chat)):
                    for n in range(NST):
                        def f_xp(e, src=src, n=n, br=bc_ready, war=m_evE.n):
                            e.wait_ge(m_evBv.h, br)
                            if war >= 1:
                                e.wait_ge(m_evE.h, war)
                            e.matmul(ps_exp[:], oh_sb[:, n * 128:(n + 1) * 128],
                                     src[:], start=True, stop=True
                                     ).then_inc(m_exp.h, 1)
                        P2.add("tensor", f_xp)
                        m_exp.bump()
                        xp_ready = m_exp.n

                        def f_xpe(e, dst=dst, n=n, xr=xp_ready,
                                  wb=(256 * c if c >= 1 else None)):
                            e.wait_ge(m_exp.h, xr)
                            if wb is not None:
                                e.wait_ge(m_b.h, wb)
                            e.tensor_copy(dst[:, n * CK:(n + 1) * CK],
                                          ps_exp[:]).then_inc(m_evE.h, 1)
                            _dr(e)
                        P2.add("vector", f_xpe)
                        m_evE.bump()
                evE_base = m_evE.n - 32

                for dt in range(NDT):
                    def f_wdt(e, dt=dt, drr=dr_ready, war=m_spr.n):
                        e.wait_ge(m_evBs.h, drr)
                        if war >= 1:
                            e.wait_ge(m_spr.h, war)
                        e.matmul(ps_dpre[:], wdt_sb[:, dt * 128:(dt + 1) * 128],
                                 dr_bf[:], start=True, stop=True
                                 ).then_inc(m_wdt.h, 1)
                    P2.add("tensor", f_wdt)
                    m_wdt.bump()
                    dpre_ready = m_wdt.n

                    def f_sp(e, dt=dt, dr=dpre_ready, war=m_dlt.n):
                        e.wait_ge(m_wdt.h, dr)
                        if war >= 1:
                            e.wait_ge(m_dlt.h, war)
                        bcol = bdt_sb[:, dt:dt + 1]
                        e.activation(sp1[:], ps_dpre[:], AF.Abs, bias=bcol)
                        _dr(e)
                        e.activation(sp2[:], ps_dpre[:], AF.Relu, bias=bcol
                                     ).then_inc(m_spr.h, 1)
                        _dr(e)
                        e.activation(delta[:], sp1[:], AF.Exp, scale=CV_NONE)
                        _dr(e)
                        e.activation(sp1[:], delta[:], AF.Ln, bias=CV_ONE
                                     ).then_inc(m_dsp.h, 1)
                        _dr(e)
                    P2.add("scalar", f_sp)
                    m_spr.bump()
                    m_dsp.bump()
                    sp_ready = m_dsp.n

                    def f_dlt(e, sr=sp_ready, wa=m_a.n, wz=m_z.n):
                        e.wait_ge(m_dsp.h, sr)
                        if wa >= 1:
                            e.wait_ge(m_a.h, wa)
                            e.wait_ge(m_z.h, wz)
                        e.tensor_add(delta[:], sp2[:], sp1[:]).then_inc(m_dlt.h, 1)
                        _dr(e)
                    P2.add("vector", f_dlt)
                    m_dlt.bump()
                    dlt_ready = m_dlt.n

                    def f_z(e, dt=dt, dr=dlt_ready):
                        e.wait_ge(m_dlt.h, dr)
                        e.tensor_mul(z_sb[:], delta[:],
                                     u_bf[:, dt * CK:(dt + 1) * CK]
                                     ).then_inc(m_z.h, 1)
                        _dr(e)
                    P2.add("gpsimd", f_z)
                    m_z.bump()
                    z_ready = m_z.n

                    def f_yi(e, dt=dt, ur=u_all):
                        e.wait_ge(m_u.h, ur)
                        e.tensor_scalar(yacc[:], u_bf[:, dt * CK:(dt + 1) * CK],
                                        dvec_sb[:, dt:dt + 1], None, ALU.mult)
                        _dr(e)
                    P2.add("vector", f_yi)

                    pend_ym = [None]
                    for n in range(NST):
                        par = g_dtn % 3
                        spar = g_dtn % 2
                        on_gp = (n % 3 == 1)

                        def f_a(e, dt=dt, n=n, par=par, dr=dlt_ready,
                                war=(m_scan.n - 2)):
                            e.wait_ge(m_dlt.h, dr)
                            if war >= 1:
                                e.wait_ge(m_scan.h, war)
                            e.activation(asb[par][:], delta[:], AF.Exp,
                                         scale=a_sb[:, dt * 16 + n:dt * 16 + n + 1]
                                         ).then_inc(m_a.h, 1)
                            _dr(e)
                        P2.add("scalar", f_a)
                        m_a.bump()
                        a_ready = m_a.n

                        def f_b(e, n=n, par=par, zr=z_ready,
                                be=(evE_base + n + 1), war=(m_scan.n - 2)):
                            e.wait_ge(m_z.h, zr)
                            e.wait_ge(m_evE.h, be)
                            if war >= 1:
                                e.wait_ge(m_scan.h, war)
                            e.tensor_mul(bsb[par][:], z_sb[:],
                                         bhat[:, n * CK:(n + 1) * CK]
                                         ).then_inc(m_b.h, 1)
                            _dr(e)
                        P2.add("gpsimd", f_b)
                        m_b.bump()
                        b_ready = m_b.n
                        if pend_ym[0] is not None:
                            P2.add("gpsimd", pend_ym[0])
                            pend_ym[0] = None

                        gwar = g_gp.get(g_dtn - 2)

                        def f_scan(e, dt=dt, n=n, par=par, spar=spar,
                                   ar=a_ready, br=b_ready, on_gp=on_gp,
                                   gwar=gwar, ce=(evE_base + 16 + n + 1)):
                            e.wait_ge(m_a.h, ar)
                            e.wait_ge(m_b.h, br)
                            if gwar is not None:
                                e.wait_ge(m_yg.h, gwar)
                            col = dt * 16 + n
                            e.tensor_tensor_scan(ssb[spar][:], asb[par][:],
                                                 bsb[par][:],
                                                 carry[:, col:col + 1],
                                                 ALU.mult, ALU.add
                                                 ).then_inc(m_scan.h, 1)
                            _dr(e)
                            e.tensor_copy(carry[:, col:col + 1],
                                          ssb[spar][:, CK - 1:CK])
                            _dr(e)
                            if not on_gp:
                                e.wait_ge(m_evE.h, ce)
                                e.tensor_mul(ytmp[:], ssb[spar][:],
                                             chat[:, n * CK:(n + 1) * CK])
                                _dr(e)
                                e.tensor_add(yacc[:], yacc[:], ytmp[:])
                                _dr(e)
                        P2.add("vector", f_scan)
                        m_scan.bump()
                        if on_gp:
                            def f_ym(e, n=n, par=spar, sr=m_scan.n,
                                     ce=(evE_base + 16 + n + 1),
                                     first=(n == 1), dt=dt, c=c,
                                     wy=(m_y.n if (c + dt) > 0 else None)):
                                e.wait_ge(m_scan.h, sr)
                                e.wait_ge(m_evE.h, ce)
                                e.tensor_mul(ytmp2[:], ssb[par][:],
                                             chat[:, n * CK:(n + 1) * CK])
                                _dr(e)
                                if first:
                                    if wy is not None:
                                        e.wait_ge(m_y.h, wy)
                                    e.tensor_copy(yacc2[:], ytmp2[:]
                                                  ).then_inc(m_yg.h, 1)
                                else:
                                    e.tensor_add(yacc2[:], yacc2[:], ytmp2[:]
                                                 ).then_inc(m_yg.h, 1)
                                _dr(e)
                            pend_ym[0] = f_ym
                            m_yg.bump()
                            g_gp[g_dtn] = m_yg.n
                        g_dtn += 1

                    if pend_ym[0] is not None:
                        P2.add("gpsimd", pend_ym[0])
                        pend_ym[0] = None

                    def f_yb(e, dt=dt, c=c, rr=(evA_base + 16 + dt + 1),
                             ygr=m_yg.n,
                             wwo=(4 * c if c >= 1 else None)):
                        e.wait_ge(m_evA.h, rr)
                        e.wait_ge(m_yg.h, ygr)
                        if wwo is not None:
                            e.wait_ge(m_wo.h, wwo)
                        e.tensor_add(yacc[:], yacc[:], yacc2[:])
                        _dr(e)
                        e.tensor_mul(ybf[:, dt * CK:(dt + 1) * CK], yacc[:],
                                     res_bf[:, dt * CK:(dt + 1) * CK]
                                     ).then_inc(m_y.h, 1)
                        _dr(e)
                    P2.add("vector", f_yb)
                    m_y.bump()

                prev_y_base[0] = m_y.n - NDT

            emit_wout(NCH - 1, prev_y_base[0])

            def b2_sync(e, w0=m_ast[0].n, w1=m_ast[1].n):
                e.wait_ge(m_ast[0].h, w0)
                e.wait_ge(m_ast[1].h, w1)
                e.nop().then_inc(bar2.h, 1)
            P2.add("sync", b2_sync)

            def b2_t(e):
                e.nop().then_inc(bar2.h, 1)
            P2.add("tensor", b2_t)

            for eng in ("vector", "scalar", "gpsimd"):
                def b2_e(e):
                    e.drain().then_inc(bar2.h, 1)
                P2.add(eng, b2_e)
            bar2.bump(5)

            blk2 = p2.enter_context(nc.Block("p2"))
            P2.run(blk2)

        with ExitStack() as p3:
            sb3 = lambda n_, s, d: p3.enter_context(nc.sbuf_tensor(n_, s, d))
            ps3 = lambda n_, s, d=F32: p3.enter_context(nc.psum_tensor(n_, s, d))

            xnr = sb3("xnr", [128, 8 * SC], BF16)
            shat = [sb3(f"shat{i}", [128, SC], BF16) for i in range(2)]
            xe = sb3("xe", [128, 8 * SC], BF16)
            wgt_t = [sb3(f"wgt_t{i}", [128, 8 * 128], BF16) for i in range(2)]
            wut_t = [sb3(f"wut_t{i}", [128, 8 * 128], BF16) for i in range(2)]
            h_sb = sb3("h_sb", [128, 16 * SC], BF16)
            wdn_sb = [sb3(f"wdn_sb{i}", [128, 16 * 1024], BF16) for i in range(2)]
            acc = sb3("acc", [128, 8 * 1024], F32)
            sgtmp = sb3("sgtmp", [128, 512], F32)
            att_t = [sb3(f"att_t{i}", [128, 1024], F32) for i in range(2)]

            ps_s = ps3("ps_s", [128, 512])
            ps_g = [ps3(f"ps_g{i}", [128, 512]) for i in range(2)]
            ps_u = [ps3(f"ps_u{i}", [128, 512]) for i in range(2)]
            ps_d = [ps3(f"ps_d{i}", [128, 512]) for i in range(2)]

            P3 = Prog()
            NB2 = 5
            for eng in Prog.ENGS:
                def pf3(e):
                    e.wait_ge(bar2.h, NB2)
                P3.add(eng, pf3)

            gih = 0
            ge = 0
            gtt = 0

            def emit_sx(sc, e_i, gu_prev, xnrr):
                shared = e_i == NE
                if not shared:
                    for th in range(2):
                        def f_sc(e, e_i=e_i, sc=sc, th=th,
                                 war=(n_scev.n if n_sc.n >= 1 else None)):
                            if war is not None:
                                e.wait_ge(n_scev.h, war)
                            e.matmul(
                                ps_s[:],
                                oh_sb[:, e_i * 128:(e_i + 1) * 128],
                                scores[:, sc * SC + th * 512:
                                       sc * SC + th * 512 + 512],
                                start=True, stop=True).then_inc(n_sc.h, 1)
                        P3.add("tensor", f_sc)
                        n_sc.bump()
                        sc_ready = n_sc.n

                        def f_sce(e, e_i=e_i, th=th, sr=sc_ready):
                            e.wait_ge(n_sc.h, sr)
                            e.tensor_copy(
                                shat[e_i % 2][:, th * 512:(th + 1) * 512],
                                ps_s[:]).then_inc(n_scev.h, 1)
                            _dr(e)
                        P3.add("vector", f_sce)
                        n_scev.bump()
                shat_ready = n_scev.n
                x_base = n_X.n
                for j in range(8):
                    def f_X(e, j=j, e_i=e_i, shared=shared, xnrr=xnrr,
                            shr=shat_ready, pg=gu_prev):
                        e.wait_ge(n_xnr.h, xnrr)
                        e.wait_ge(n_gu.h, pg)
                        if not shared:
                            e.wait_ge(n_scev.h, shr)
                            e.tensor_mul(xe[:, j * SC:(j + 1) * SC],
                                         xnr[:, j * SC:(j + 1) * SC],
                                         shat[e_i % 2][:]).then_inc(n_X.h, 1)
                        else:
                            e.tensor_copy(xe[:, j * SC:(j + 1) * SC],
                                          xnr[:, j * SC:(j + 1) * SC]
                                          ).then_inc(n_X.h, 1)
                        _dr(e)
                    P3.add("vector", f_X)
                    n_X.bump()
                return x_base

            for sc in range(NSC):
                def f_xnr(e, sc=sc, war=n_X.n):
                    if sc >= 1:
                        e.wait_ge(n_X.h, war)
                    e.dma_start(
                        xnr[:].rearrange("p (j t) -> p j t", j=8),
                        xnt_d[:, :, sc * SC:(sc + 1) * SC].transpose([1, 0, 2]),
                    ).then_inc(n_xnr.h, 16)
                P3.add("sync", f_xnr)
                n_xnr.bump(16)
                xnr_ready = n_xnr.n

                x_base = emit_sx(sc, 0, n_gu.n, xnr_ready)

                for e_i in range(NE + 1):
                    shared = e_i == NE

                    prev_dn_done = n_dn.n
                    sl_base = n_sl.n
                    h_base0 = n_h.n
                    for i in range(16):
                        wgp = n_wg[gih % 2]

                        def f_wgl(e, e_i=e_i, i=i, gih=gih, wgp=wgp,
                                  shared=shared,
                                  war=(n_gu.n - 4 if gih >= 2 else None)):
                            if war is not None:
                                e.wait_ge(n_gu.h, max(war, 0))
                            if shared:
                                gsrc = wgt_d[i].rearrange("p c f -> p (c f)")
                                usrc = wut_d[i].rearrange("p c f -> p (c f)")
                            else:
                                gsrc = gup_d[e_i, i].rearrange("p c f -> p (c f)")
                                usrc = gup_d[e_i, 16 + i].rearrange("p c f -> p (c f)")
                            e.dma_start(wgt_t[gih % 2][:], gsrc
                                        ).then_inc(wgp.h, 16)
                            e.dma_start(wut_t[gih % 2][:], usrc
                                        ).then_inc(wgp.h, 16)
                        P3.add("gpsimd", f_wgl)
                        wgp.bump(32)
                        wg_ready = wgp.n

                        for th in range(2):
                            war_sl = sl_base + (2 * i - 2 if th == 0 else 2 * i + 1)
                            war_h = h_base0 + (2 * i - 2 if th == 0 else 2 * i + 1)

                            def f_gu(e, i=i, th=th, gih=gih, wgp=wgp,
                                     wr=wg_ready, xb=x_base,
                                     wsl=max(war_sl, 0), wh=max(war_h, 0)):
                                e.wait_ge(wgp.h, wr)
                                if wsl >= 1:
                                    e.wait_ge(n_sl.h, wsl)
                                if wh >= 1:
                                    e.wait_ge(n_h.h, wh)
                                for ct in range(8):
                                    e.wait_ge(n_X.h, xb + ct + 1)
                                    rhs = xe[:, ct * SC + th * 512:
                                             ct * SC + th * 512 + 512]
                                    ig = e.matmul(ps_g[i % 2][:],
                                                  wgt_t[gih % 2][:, ct * 128:
                                                                 (ct + 1) * 128],
                                                  rhs, start=(ct == 0),
                                                  stop=(ct == 7))
                                ig.then_inc(n_gu.h, 1)
                                for ct in range(8):
                                    rhs = xe[:, ct * SC + th * 512:
                                             ct * SC + th * 512 + 512]
                                    iu = e.matmul(ps_u[i % 2][:],
                                                  wut_t[gih % 2][:, ct * 128:
                                                                 (ct + 1) * 128],
                                                  rhs, start=(ct == 0),
                                                  stop=(ct == 7))
                                iu.then_inc(n_gu.h, 1)
                            P3.add("tensor", f_gu)
                            n_gu.bump(2)
                            gu_ready = n_gu.n

                            def f_sl(e, i=i, gr=(gu_ready - 1), whp=n_h.n):
                                e.wait_ge(n_gu.h, gr)
                                if whp >= 1:
                                    e.wait_ge(n_h.h, whp)
                                e.activation(sgtmp[:], ps_g[i % 2][:], AF.Silu
                                             ).then_inc(n_sl.h, 1)
                                _dr(e)
                            P3.add("scalar", f_sl)
                            n_sl.bump()
                            sl_ready = n_sl.n

                            def f_h(e, i=i, th=th, gr=gu_ready, slr=sl_ready,
                                    pd=prev_dn_done):
                                e.wait_ge(n_gu.h, gr)
                                e.wait_ge(n_sl.h, slr)
                                e.wait_ge(n_dn.h, pd)
                                e.tensor_mul(
                                    h_sb[:, i * SC + th * 512:
                                         i * SC + th * 512 + 512],
                                    sgtmp[:], ps_u[i % 2][:]).then_inc(n_h.h, 1)
                                _dr(e)
                            P3.add("vector", f_h)
                            n_h.bump()
                        gih += 1
                    h_base = n_h.n - 32

                    if e_i < NE:
                        x_base = emit_sx(sc, e_i + 1, n_gu.n, xnr_ready)

                    wdp = n_wdn[ge % 2]

                    def f_wdl(e, e_i=e_i, ge=ge, wdp=wdp, shared=shared,
                              war=(n_dn.n - 8 if ge >= 2 else None)):
                        if war is not None:
                            e.wait_ge(n_dn.h, max(war, 0))
                        src = (wdsh_d if shared else dwn_d[e_i]
                               ).rearrange("p c k -> p (c k)")
                        e.dma_start(wdn_sb[ge % 2][:], src).then_inc(wdp.h, 16)
                    P3.add("gpsimd", f_wdl)
                    wdp.bump(16)
                    wdn_ready = wdp.n

                    for tt in range(8):
                        def f_dn(e, tt=tt, ge=ge, gtt=gtt, hb=h_base,
                                 wdp=wdp, wr=wdn_ready, war=n_acc.n):
                            e.wait_ge(wdp.h, wr)
                            if gtt >= 1:
                                e.wait_ge(n_acc.h, war)
                            for ct in range(16):
                                e.wait_ge(n_h.h, hb + 2 * (ct + 1))
                                lhs = h_sb[:, ct * SC + tt * 128:
                                           ct * SC + tt * 128 + 128]
                                e.matmul(ps_d[0][:], lhs,
                                         wdn_sb[ge % 2][:, ct * 1024:
                                                        ct * 1024 + 512],
                                         start=(ct == 0), stop=(ct == 15))
                                inst = e.matmul(ps_d[1][:], lhs,
                                                wdn_sb[ge % 2][:, ct * 1024 + 512:
                                                               (ct + 1) * 1024],
                                                start=(ct == 0), stop=(ct == 15))
                            inst.then_inc(n_dn.h, 1)
                        P3.add("tensor", f_dn)
                        n_dn.bump()
                        dn_ready = n_dn.n
                        gtt += 1

                        if e_i == 0:
                            def f_acc(e, tt=tt, sc=sc, dr=dn_ready,
                                      war=(8 * sc if sc >= 1 else None)):
                                e.wait_ge(n_dn.h, dr)
                                if war is not None:
                                    e.wait_ge(n_ep.h, war)
                                e.tensor_copy(acc[:, tt * 1024:tt * 1024 + 512],
                                              ps_d[0][:])
                                _dr(e)
                                e.tensor_copy(acc[:, tt * 1024 + 512:(tt + 1) * 1024],
                                              ps_d[1][:]).then_inc(n_acc.h, 1)
                                _dr(e)
                        else:
                            def f_acc(e, tt=tt, dr=dn_ready):
                                e.wait_ge(n_dn.h, dr)
                                e.tensor_add(acc[:, tt * 1024:tt * 1024 + 512],
                                             acc[:, tt * 1024:tt * 1024 + 512],
                                             ps_d[0][:])
                                _dr(e)
                                e.tensor_add(acc[:, tt * 1024 + 512:(tt + 1) * 1024],
                                             acc[:, tt * 1024 + 512:(tt + 1) * 1024],
                                             ps_d[1][:]).then_inc(n_acc.h, 1)
                                _dr(e)
                        P3.add("vector", f_acc)
                        n_acc.bump()
                    ge += 1

                acc_done = n_acc.n
                for tt in range(8):
                    idx = sc * 8 + tt

                    xap = n_xat[idx % 2]
                    oxp = n_ox[idx % 2]
                    opp = n_out[idx % 2]

                    def f_lx(e, sc=sc, tt=tt, idx=idx, xap=xap, opp=opp,
                             owar=(16 * (idx // 2) if idx >= 2 else None)):
                        if owar is not None:
                            e.wait_ge(opp.h, owar)
                        row = sc * SC + tt * 128
                        e.dma_start(att_t[idx % 2][:], attn_d[row:row + 128, :]
                                    ).then_inc(xap.h, 16)
                    P3.add("sync", f_lx)
                    xap.bump(16)
                    xat_ready = xap.n

                    def f_ep(e, tt=tt, idx=idx, xap=xap,
                             ar=(acc_done - 8 + tt + 1), xr=xat_ready):
                        gt = idx % 2
                        e.wait_ge(n_acc.h, ar)
                        e.wait_ge(xap.h, xr)
                        e.tensor_add(att_t[gt][:],
                                     acc[:, tt * 1024:(tt + 1) * 1024],
                                     att_t[gt][:]).then_inc(n_ep.h, 1)
                        _dr(e)
                    P3.add("vector", f_ep)
                    n_ep.bump()
                    ep_ready = n_ep.n

                    def f_xadd(e, sc=sc, tt=tt, idx=idx, oxp=oxp, er=ep_ready):
                        e.wait_ge(n_ep.h, er)
                        row = sc * SC + tt * 128
                        e.dma_start(att_t[idx % 2][:], x_d[row:row + 128, :],
                                    accum_op=ALU.add).then_inc(oxp.h, 16)
                    P3.add("gpsimd", f_xadd)
                    oxp.bump(16)
                    ox_ready = oxp.n

                    def f_out(e, sc=sc, tt=tt, idx=idx, opp=opp, oxp=oxp,
                              er=ox_ready):
                        e.wait_ge(oxp.h, er)
                        gt = idx % 2
                        row = sc * SC + tt * 128
                        e.dma_start(out_d[row:row + 128, :], att_t[gt][:]
                                    ).then_inc(opp.h, 16)
                    P3.add("sync", f_out)
                    opp.bump(16)

            def f_fin(e, w0=n_out[0].n, w1=n_out[1].n):
                e.wait_ge(n_out[0].h, w0)
                e.wait_ge(n_out[1].h, w1)
            P3.add("sync", f_fin)

            blk3 = p3.enter_context(nc.Block("p3"))
            P3.run(blk3)

    return nc


def _host_layout(inputs):
    f32 = np.float32
    g = lambda k: np.asarray(inputs[k])
    conv_w = g("L0_attn_conv_w")
    cw = np.ascontiguousarray(
        conv_w[:, 0, :].T.reshape(16, 128, 4).transpose(1, 0, 2).reshape(128, 64))
    cb = np.ascontiguousarray(g("L0_attn_conv_b").reshape(16, 128).T)
    bdt = np.ascontiguousarray(g("L0_attn_b_dt").reshape(16, 128).T)
    dvec = np.ascontiguousarray(g("L0_attn_D").reshape(16, 128).T)
    alog = np.ascontiguousarray(
        g("L0_attn_A_log").reshape(16, 128, 16).transpose(1, 0, 2).reshape(128, 256))
    onehot = np.zeros((16, 16 * 128), ml_dtypes.bfloat16)
    for n in range(16):
        onehot[n, n * 128:(n + 1) * 128] = 1.0
    def tile_lhs(w, nin, nout):
        return np.ascontiguousarray(
            w.reshape(nin, 128, nout, 128).transpose(2, 1, 0, 3))

    def tile_rhs(w, nin, k):
        return np.ascontiguousarray(
            w.reshape(nin, 128, k).transpose(1, 0, 2))

    gup = g("L0_ffn_gate_up_proj")
    gup_t = np.ascontiguousarray(
        gup.reshape(16, 8, 128, 32, 128).transpose(0, 3, 2, 1, 4))
    dwn = g("L0_ffn_down_proj")
    dwn_t = np.ascontiguousarray(
        dwn.reshape(16, 16, 128, 1024).transpose(0, 2, 1, 3))
    return {
        "win": tile_lhs(g("L0_attn_W_in"), 8, 32),
        "wx": np.ascontiguousarray(g("L0_attn_W_x")),
        "wdt": np.ascontiguousarray(g("L0_attn_W_dt")),
        "convw": cw, "convb": cb, "bdt": bdt, "alog": alog, "dvec": dvec,
        "wout": tile_rhs(g("L0_attn_W_out"), 16, 1024),
        "wrt": np.ascontiguousarray(g("L0_ffn_w_router").T),
        "gup": gup_t,
        "dwn": dwn_t,
        "wgt": tile_lhs(np.ascontiguousarray(g("L0_ffn_w_gate").T), 8, 16),
        "wut": tile_lhs(np.ascontiguousarray(g("L0_ffn_w_up").T), 8, 16),
        "wdsh": tile_rhs(np.ascontiguousarray(g("L0_ffn_w_down").T), 16, 1024),
        "oh": onehot,
        "idb": np.eye(128, dtype=ml_dtypes.bfloat16),
        "idf": np.eye(128, dtype=f32),
        "cvals": np.tile(np.array([[EPS, 1.0, 1.0 / D, -0.5, -1.0, 0.0, 0.0, 0.0]],
                                  dtype=f32), (128, 1)),
    }


def kernel(**inputs):
    common = _host_layout(inputs)
    x = np.asarray(inputs["L0_attn_x"])
    nc = build_program()
    in_maps = []
    for b in range(B):
        m = dict(common)
        m["x"] = np.ascontiguousarray(x[b])
        in_maps.append(m)
    res = run_bass_kernel_spmd(nc, in_maps, core_ids=list(range(B)))
    return np.stack([r["out"] for r in res.results], axis=0)


if __name__ == "__main__":
    import time
    t0 = time.time()
    nc = build_program()
    print("built", nc.n_instructions(), "instrs in", round(time.time() - t0, 1), "s")


# version 14
# speedup vs baseline: 1.5731x; 1.0154x over previous
import numpy as np
import ml_dtypes
from contextlib import ExitStack

import concourse.bass as bass
import concourse.mybir as mybir
from concourse.bass_utils import run_bass_kernel_spmd

F32 = mybir.dt.float32
BF16 = mybir.dt.bfloat16
AF = mybir.ActivationFunctionType
ALU = mybir.AluOpType
AX = mybir.AxisListType

B, T, D = 8, 4096, 1024
DIN = 2048
NST = 16
RNK = 64
NE = 16
FF = 2048
NT = T // 128
CK = 512
NCH = T // CK
NDT = DIN // 128
SC = 1024
NSC = T // SC
EPS = 1e-5

DEBUG_SCRATCH_OUT = False
USE_DRAINS = False


def _dr(e, always=False):
    if USE_DRAINS or always:
        e.drain()
class Ctr:
    def __init__(self, h):
        self.h = h
        self.n = 0

    def bump(self, k=1):
        self.n += k
        return self.n


class Prog:
    ENGS = ("sync", "tensor", "vector", "scalar", "gpsimd")

    def __init__(self):
        self.ops = {k: [] for k in self.ENGS}

    def add(self, eng, fn):
        self.ops[eng].append(fn)

    def run(self, block):
        for name in self.ENGS:
            fns = self.ops[name]
            if not fns:
                continue

            def body(e, fns=fns):
                for fn in fns:
                    fn(e)

            getattr(block, name)(body)


def build_program():
    nc = bass.Bass("TRN2", target_bir_lowering=False)

    ap = lambda h: h.ap()
    x_d = ap(nc.dram_tensor("x", [T, D], F32, kind="ExternalInput"))
    win_d = ap(nc.dram_tensor("win", [32, 128, 8, 128], F32, kind="ExternalInput"))
    wx_d = ap(nc.dram_tensor("wx", [DIN, 96], F32, kind="ExternalInput"))
    wdt_d = ap(nc.dram_tensor("wdt", [RNK, DIN], F32, kind="ExternalInput"))
    convw_d = ap(nc.dram_tensor("convw", [128, 64], F32, kind="ExternalInput"))
    convb_d = ap(nc.dram_tensor("convb", [128, 16], F32, kind="ExternalInput"))
    bdt_d = ap(nc.dram_tensor("bdt", [128, 16], F32, kind="ExternalInput"))
    alog_d = ap(nc.dram_tensor("alog", [128, 256], F32, kind="ExternalInput"))
    dvec_d = ap(nc.dram_tensor("dvec", [128, 16], F32, kind="ExternalInput"))
    wout_d = ap(nc.dram_tensor("wout", [128, 16, 1024], F32, kind="ExternalInput"))
    wrt_d = ap(nc.dram_tensor("wrt", [D, NE], F32, kind="ExternalInput"))
    gup_d = ap(nc.dram_tensor("gup", [NE, 32, 128, 8, 128], F32, kind="ExternalInput"))
    dwn_d = ap(nc.dram_tensor("dwn", [NE, 128, 16, 1024], F32, kind="ExternalInput"))
    wgt_d = ap(nc.dram_tensor("wgt", [16, 128, 8, 128], F32, kind="ExternalInput"))
    wut_d = ap(nc.dram_tensor("wut", [16, 128, 8, 128], F32, kind="ExternalInput"))
    wdsh_d = ap(nc.dram_tensor("wdsh", [128, 16, 1024], F32, kind="ExternalInput"))
    oh_d = ap(nc.dram_tensor("oh", [16, 16 * 128], BF16, kind="ExternalInput"))
    idb_d = ap(nc.dram_tensor("idb", [128, 128], BF16, kind="ExternalInput"))
    idf_d = ap(nc.dram_tensor("idf", [128, 128], F32, kind="ExternalInput"))
    cv_d = ap(nc.dram_tensor("cvals", [128, 8], F32, kind="ExternalInput"))

    skind = "ExternalOutput" if DEBUG_SCRATCH_OUT else "Internal"
    xnt_d = ap(nc.dram_tensor("xnt_s", [8, 128, T], BF16, kind=skind))
    attn_d = ap(nc.dram_tensor("attn_s", [T, D], F32, kind=skind))
    out_d = ap(nc.dram_tensor("out", [T, D], F32, kind="ExternalOutput"))

    ctx = ExitStack()
    with ctx:
        def sbuf(name, shape, dtype):
            return ctx.enter_context(nc.sbuf_tensor(name, shape, dtype))

        def sem(name):
            return Ctr(ctx.enter_context(nc.semaphore(name)))

        oh_sb = sbuf("oh_sb", [16, 16 * 128], BF16)
        idb_sb = sbuf("idb_sb", [128, 128], BF16)
        idf_sb = sbuf("idf_sb", [128, 128], F32)
        scores = sbuf("scores", [16, T], BF16)
        a_sb = sbuf("a_sb", [128, 256], F32)
        dvec_sb = sbuf("dvec_sb", [128, 16], F32)
        convw_sb = sbuf("convw_sb", [128, 64], F32)
        convb_sb = sbuf("convb_sb", [128, 16], F32)
        bdt_sb = sbuf("bdt_sb", [128, 16], F32)
        wx_sb = sbuf("wx_sb", [128, 16 * 96], BF16)
        wdt_sb = sbuf("wdt_sb", [64, 16 * 128], BF16)
        wrt_sb = sbuf("wrt_sb", [128, 8 * 16], BF16)
        cv_sb = sbuf("cv_sb", [128, 8], F32)
        CV_EPS, CV_ONE, CV_INVD, CV_NHALF, CV_NONE = (
            cv_sb[:, 0:1], cv_sb[:, 1:2], cv_sb[:, 2:3], cv_sb[:, 3:4], cv_sb[:, 4:5])

        c_ld = sem("c_ld")
        c_gld = sem("c_gld")
        c_prep = sem("c_prep")
        c_prep2 = sem("c_prep2")
        bar1 = sem("bar1")
        bar2 = sem("bar2")

        s1x = [sem("s1x0"), sem("s1x1")]
        s1v = sem("s1v")
        s1s = sem("s1s")
        s1t = sem("s1t")
        s1e = sem("s1e")
        s1w = [sem("s1w0"), sem("s1w1")]
        s1r = sem("s1r")
        s1sc = sem("s1sc")

        m_xnc = sem("m_xnc")
        m_win = [sem("m_win0"), sem("m_win1")]
        m_wres = sem("m_wres")
        m_wiA = sem("m_wiA")
        m_evA = sem("m_evA")
        m_hist = sem("m_hist")
        m_conv = sem("m_conv")
        m_u = sem("m_u")
        m_wx = sem("m_wx")
        m_evBs = sem("m_evBs")
        m_evBv = sem("m_evBv")
        m_exp = sem("m_exp")
        m_evE = sem("m_evE")
        m_wdt = sem("m_wdt")
        m_spr = sem("m_spr")
        m_dsp = sem("m_dsp")
        m_dlt = sem("m_dlt")
        m_z = sem("m_z")
        m_a = sem("m_a")
        m_b = sem("m_b")
        m_yg = sem("m_yg")
        m_scan = sem("m_scan")
        m_y = sem("m_y")
        m_wo = sem("m_wo")
        m_evC = sem("m_evC")
        m_ast = [sem("m_ast0"), sem("m_ast1")]

        n_xnr = sem("n_xnr")
        n_X = sem("n_X")
        n_sc = sem("n_sc")
        n_scev = sem("n_scev")
        n_wg = [sem("n_wg0"), sem("n_wg1"), sem("n_wg2")]
        n_wdn = [sem("n_wdn0"), sem("n_wdn1")]
        n_gu = sem("n_gu")
        n_sl = sem("n_sl")
        n_h = sem("n_h")
        n_dn = sem("n_dn")
        n_acc = sem("n_acc")
        n_xat = [sem("n_xat0"), sem("n_xat1")]
        n_ep = sem("n_ep")
        n_ox = [sem("n_ox0"), sem("n_ox1")]
        n_out = [sem("n_out0"), sem("n_out1")]

        with ExitStack() as p1:
            sb1 = lambda n_, s, d: p1.enter_context(nc.sbuf_tensor(n_, s, d))
            ps1 = lambda n_, s, d=F32: p1.enter_context(nc.psum_tensor(n_, s, d))

            xin = [sb1(f"xin{i}", [128, D], F32) for i in range(2)]
            junk = sb1("junk", [128, D], F32)
            stat = sb1("stat", [128, 8], F32)
            xn_t = sb1("xn_t", [128, D], BF16)
            xnt_sb = sb1("xnt_sb", [128, 8 * 512], BF16)
            sgt = sb1("sgt", [128, 16], F32)
            mskt = sb1("mskt", [128, 16], F32)
            sct = sb1("sct", [128, 16], F32)
            mxt = sb1("mxt", [128, 1], F32)
            atmp = sb1("atmp", [128, 256], F32)
            tp_ps = [ps1(f"tp_ps{i}", [128, 1024], BF16) for i in range(2)]
            lg_ps = ps1("lg_ps", [128, 512], F32)
            st_ps = ps1("st_ps", [16, 512], F32)

            P1 = Prog()

            def c_sync(e):
                for dst, src in ((oh_sb, oh_d), (idf_sb, idf_d),
                                 (convw_sb, convw_d), (convb_sb, convb_d),
                                 (bdt_sb, bdt_d), (dvec_sb, dvec_d),
                                 (atmp, alog_d), (cv_sb, cv_d)):
                    e.dma_start(dst[:], src[:]).then_inc(c_ld.h, 16)
            P1.add("sync", c_sync)
            c_ld.bump(16 * 8)

            def c_gp(e):
                e.dma_start(idb_sb[:], idb_d[:]).then_inc(c_gld.h, 16)
                e.dma_start(
                    wx_sb[:].rearrange("p (c k) -> p c k", c=16),
                    wx_d.rearrange("(c p) k -> p c k", p=128),
                ).then_inc(c_gld.h, 16)
                e.dma_start(
                    wdt_sb[:].rearrange("p (c k) -> p c k", c=16),
                    wdt_d.rearrange("p (c k) -> p c k", c=16),
                ).then_inc(c_gld.h, 16)
                e.dma_start(
                    wrt_sb[:].rearrange("p (c k) -> p c k", c=8),
                    wrt_d.rearrange("(c p) k -> p c k", p=128),
                ).then_inc(c_gld.h, 16)
            P1.add("gpsimd", c_gp)
            c_gld.bump(16 * 4)

            def c_sc(e):
                e.wait_ge(c_ld.h, 16 * 8)
                e.activation(atmp[:], atmp[:], AF.Exp).then_inc(c_prep.h, 1)
                _dr(e, True)
            P1.add("scalar", c_sc)
            c_prep.bump()

            def c_vec(e):
                e.wait_ge(c_prep.h, 1)
                e.tensor_scalar_mul(a_sb[:], atmp[:], -1.0).then_inc(c_prep2.h, 1)
                _dr(e, True)
            P1.add("vector", c_vec)
            c_prep2.bump()

            v_done = {}
            s_done = {}
            sg_done = {}
            e_done = {}
            w_done = {}
            mul_done = {}
            sce_done = {}
            rt_done = {}
            stp_done = {}

            for tt in range(NT):
                buf = xin[tt % 2]

                xp = s1x[tt % 2]

                def f_sync(e, tt=tt, buf=buf, xp=xp,
                           wv=(v_done.get(tt - 2)), ws=(s_done.get(tt - 2))):
                    if wv is not None:
                        e.wait_ge(s1v.h, wv)
                        e.wait_ge(s1s.h, ws)
                    e.dma_start(buf[:], x_d[tt * 128:(tt + 1) * 128, :]
                                ).then_inc(xp.h, 16)
                P1.add("sync", f_sync)
                xp.bump(16)
                x_ready = xp.n

                def f_v1(e, buf=buf, xp=xp, xr=x_ready, ws=s_done.get(tt - 1)):
                    e.wait_ge(xp.h, xr)
                    if ws is not None:
                        e.wait_ge(s1s.h, ws)
                    e.tensor_reduce(stat[:, 0:1], buf[:], AX.X, ALU.add)
                    _dr(e, True)
                    e.tensor_scalar_mul(stat[:, 0:1], stat[:, 0:1], -1.0 / D
                                        ).then_inc(s1v.h, 1)
                    _dr(e, True)
                P1.add("vector", f_v1)
                s1v.bump()
                nm_ready = s1v.n

                def f_s1(e, buf=buf, xp=xp, xr=x_ready, nmr=nm_ready):
                    e.wait_ge(xp.h, xr)
                    e.wait_ge(s1v.h, nmr)
                    e.activation(junk[:], buf[:], AF.Square,
                                 bias=stat[:, 0:1], accum_out=stat[:, 1:2])
                    _dr(e, True)
                    e.activation(stat[:, 2:3], stat[:, 1:2], AF.Ln,
                                 scale=CV_INVD, bias=CV_EPS)
                    _dr(e, True)
                    e.activation(stat[:, 3:4], stat[:, 2:3], AF.Exp,
                                 scale=CV_NHALF).then_inc(s1s.h, 1)
                    _dr(e, True)
                P1.add("scalar", f_s1)
                s1s.bump()
                rstd_ready = s1s.n
                s_done[tt] = s1s.n

                def f_xn(e, buf=buf, rr=rstd_ready):
                    e.wait_ge(s1s.h, rr)
                    e.tensor_scalar(xn_t[:], buf[:], stat[:, 0:1],
                                    stat[:, 3:4], ALU.add, ALU.mult
                                    ).then_inc(s1v.h, 1)
                    _dr(e, True)
                P1.add("vector", f_xn)
                s1v.bump()
                xn_ready = s1v.n
                v_done[tt] = s1v.n

                for j in range(8):
                    tpw = s1t.n - 1

                    def f_t(e, j=j, xr=xn_ready, war=tpw):
                        e.wait_ge(c_gld.h, 16 * 4)
                        e.wait_ge(s1v.h, xr)
                        if war >= 1:
                            e.wait_ge(s1e.h, war)
                        e.transpose(tp_ps[j % 2][:, 0:128],
                                    xn_t[:, j * 128:(j + 1) * 128],
                                    idb_sb[:]).then_inc(s1t.h, 1)
                    P1.add("tensor", f_t)
                    s1t.bump()
                    tp_ready = s1t.n

                    g = tt // 4

                    def f_ev(e, j=j, tt=tt, tr=tp_ready, g=g,
                             ww=w_done.get(g - 1),
                             wr=(8 * g - 1 if g >= 1 else None)):
                        e.wait_ge(s1t.h, tr)
                        if ww is not None:
                            e.wait_ge(s1w[(g - 1) % 2].h, ww)
                        if wr is not None:
                            e.wait_ge(s1r.h, wr)
                        col = j * 512 + (tt % 4) * 128
                        e.tensor_copy(xnt_sb[:, col:col + 128],
                                      tp_ps[j % 2][:, 0:128]).then_inc(s1e.h, 1)
                        _dr(e, True)
                    P1.add("vector", f_ev)
                    s1e.bump()
                e_done[tt] = s1e.n

                def f_rt(e, tt=tt, er=e_done[tt],
                         wm=mul_done.get(tt - 1), wsg=sg_done.get(tt - 1)):
                    e.wait_ge(c_gld.h, 16 * 4)
                    e.wait_ge(s1e.h, er)
                    if wm is not None:
                        e.wait_ge(s1sc.h, wm)
                        e.wait_ge(s1s.h, wsg)
                    for j in range(8):
                        col = j * 512 + (tt % 4) * 128
                        inst = e.matmul(lg_ps[:, 0:16], xnt_sb[:, col:col + 128],
                                        wrt_sb[:, j * 16:(j + 1) * 16],
                                        start=(j == 0), stop=(j == 7))
                    inst.then_inc(s1r.h, 1)
                P1.add("tensor", f_rt)
                s1r.bump()
                rt_done[tt] = s1r.n
                lg_ready = s1r.n

                def f_sg(e, lr=lg_ready):
                    e.wait_ge(s1r.h, lr)
                    e.activation(sgt[:], lg_ps[:, 0:16], AF.Sigmoid).then_inc(s1s.h, 1)
                    _dr(e, True)
                P1.add("scalar", f_sg)
                s1s.bump()
                sg_done[tt] = s1s.n

                def f_msk(e, lr=lg_ready, sgr=sg_done[tt],
                          wst=stp_done.get(tt - 1)):
                    e.wait_ge(s1r.h, lr)
                    if wst is not None:
                        e.wait_ge(s1r.h, wst)
                    e.tensor_reduce(mxt[:], lg_ps[:, 0:16], AX.X, ALU.max)
                    _dr(e, True)
                    e.tensor_scalar(mskt[:], lg_ps[:, 0:16], mxt[:, 0:1], None,
                                    ALU.is_ge)
                    _dr(e, True)
                    e.wait_ge(s1s.h, sgr)
                    e.tensor_mul(sct[:], mskt[:], sgt[:]).then_inc(s1sc.h, 1)
                    _dr(e, True)
                P1.add("vector", f_msk)
                s1sc.bump()
                mul_done[tt] = s1sc.n
                sc_ready = s1sc.n

                def f_sct(e, scr=sc_ready, wse=sce_done.get(tt - 1)):
                    e.wait_ge(s1sc.h, scr)
                    if wse is not None:
                        e.wait_ge(s1sc.h, wse)
                    e.transpose(st_ps[:, 0:128], sct[:], idf_sb[:]).then_inc(s1r.h, 1)
                P1.add("tensor", f_sct)
                s1r.bump()
                stp_done[tt] = s1r.n
                st_ready = s1r.n

                def f_sce(e, sr=st_ready, tt=tt):
                    e.wait_ge(s1r.h, sr)
                    e.tensor_copy(scores[:, tt * 128:(tt + 1) * 128],
                                  st_ps[:, 0:128]).then_inc(s1sc.h, 1)
                    _dr(e, True)
                P1.add("vector", f_sce)
                s1sc.bump()
                sce_done[tt] = s1sc.n

                if tt % 4 == 3:
                    g4 = tt // 4
                    wp = s1w[g4 % 2]

                    def f_wr(e, tt=tt, wp=wp, er=e_done[tt]):
                        e.wait_ge(s1e.h, er)
                        for j in range(8):
                            e.dma_start(
                                xnt_d[j, :, (tt - 3) * 128:(tt + 1) * 128],
                                xnt_sb[:, j * 512:(j + 1) * 512]
                            ).then_inc(wp.h, 16)
                    P1.add("sync", f_wr)
                    wp.bump(16 * 8)
                    w_done[g4] = wp.n

            def b_sync(e, w0=s1w[0].n, w1=s1w[1].n):
                e.wait_ge(s1w[0].h, w0)
                e.wait_ge(s1w[1].h, w1)
                e.nop().then_inc(bar1.h, 1)
            P1.add("sync", b_sync)

            def b_t(e):
                e.nop().then_inc(bar1.h, 1)
            P1.add("tensor", b_t)

            for eng in ("vector", "scalar", "gpsimd"):
                def b_e(e):
                    e.drain().then_inc(bar1.h, 1)
                P1.add(eng, b_e)
            bar1.bump(5)

            blk1 = p1.enter_context(nc.Block("p1"))
            P1.run(blk1)

        with ExitStack() as p2:
            sb2 = lambda n_, s, d: p2.enter_context(nc.sbuf_tensor(n_, s, d))
            ps2 = lambda n_, s, d=F32: p2.enter_context(nc.psum_tensor(n_, s, d))

            xnc = sb2("xnc", [128, 8 * CK], BF16)
            winb = [sb2(f"winb{i}", [128, 8 * 128], BF16) for i in range(2)]
            wout_sb = sb2("wout_sb", [128, 16 * 1024], BF16)
            ring = sb2("ring", [128, 16 * 520], BF16)
            convacc = [sb2(f"convacc{i}", [128, CK], F32) for i in range(2)]
            u_bf = sb2("u_bf", [128, 16 * CK], BF16)
            res_bf = sb2("res_bf", [128, 16 * CK], BF16)
            dr_bf = sb2("dr_bf", [64, CK], BF16)
            b_sbuf = sb2("b_sbuf", [16, CK], BF16)
            c_sbuf = sb2("c_sbuf", [16, CK], BF16)
            bhat = sb2("bhat", [128, 16 * CK], BF16)
            chat = sb2("chat", [128, 16 * CK], BF16)
            sp1 = sb2("sp1", [128, CK], F32)
            sp2 = sb2("sp2", [128, CK], F32)
            delta = sb2("delta", [128, CK], F32)
            z_sb = sb2("z_sb", [128, CK], F32)
            asb = [sb2(f"asb{i}", [128, CK], F32) for i in range(3)]
            bsb = [sb2(f"bsb{i}", [128, CK], F32) for i in range(3)]
            ssb = [sb2(f"ssb{i}", [128, CK], F32) for i in range(2)]
            ytmp = sb2("ytmp", [128, CK], F32)
            ytmp2 = sb2("ytmp2", [128, CK], F32)
            yacc = sb2("yacc", [128, CK], F32)
            yacc2 = sb2("yacc2", [128, CK], F32)
            carry = sb2("carry", [128, 256], F32)
            ybf = sb2("ybf", [128, 16 * CK], BF16)
            attn_t = [sb2("attn_t0", [128, 1024], F32)] * 2

            ps_win = [ps2(f"ps_win{i}", [128, CK]) for i in range(2)]
            ps_dr = ps2("ps_dr", [64, CK])
            ps_bc = ps2("ps_bc", [16, CK])
            ps_exp = ps2("ps_exp", [128, CK])
            ps_dpre = ps2("ps_dpre", [128, CK])
            ps_wo = [ps2(f"ps_wo{i}", [128, 512]) for i in range(2)]

            P2 = Prog()
            NB = 5

            for eng in ("sync", "scalar"):
                def pf(e):
                    e.wait_ge(bar1.h, NB)
                P2.add(eng, pf)

            def pf_t(e):
                e.wait_ge(bar1.h, NB)
                e.wait_ge(c_gld.h, 16 * 4)
            P2.add("tensor", pf_t)

            def pf_v(e):
                e.wait_ge(bar1.h, NB)
                e.memset(carry[:], 0.0)
                _dr(e)
                for dt in range(NDT):
                    e.memset(ring[:, dt * 520:dt * 520 + 3], 0.0)
                e.drain().then_inc(m_hist.h, NDT)
            P2.add("vector", pf_v)
            m_hist.bump(NDT)

            def pf_g(e):
                e.wait_ge(bar1.h, NB)
                for g in range(4):
                    e.dma_start(
                        wout_sb[:].rearrange("p (c k) -> p c k", c=16)
                        [:, g * 4:(g + 1) * 4, :],
                        wout_d[:, g * 4:(g + 1) * 4, :],
                    ).then_inc(m_wres.h, 16)
            P2.add("gpsimd", pf_g)
            m_wres.bump(16 * 4)

            gpt = 0
            g_dtn = 0
            g_gp = {}
            prev_y_base = [None]

            def emit_wout(c, y_base):
                for tt in range(CK // 128):
                    def f_wo(e, tt=tt, c=c, yb=y_base, war=m_evC.n):
                        e.wait_ge(m_wres.h, 64)
                        if c + tt > 0:
                            e.wait_ge(m_evC.h, war)
                        for ct in range(16):
                            e.wait_ge(m_y.h, yb + ct + 1)
                            lhs = ybf[:, ct * CK + tt * 128:ct * CK + tt * 128 + 128]
                            e.matmul(ps_wo[0][:], lhs,
                                     wout_sb[:, ct * 1024:ct * 1024 + 512],
                                     start=(ct == 0), stop=(ct == 15))
                            inst = e.matmul(ps_wo[1][:], lhs,
                                            wout_sb[:, ct * 1024 + 512:
                                                    (ct + 1) * 1024],
                                            start=(ct == 0), stop=(ct == 15))
                        inst.then_inc(m_wo.h, 1)
                    P2.add("tensor", f_wo)
                    m_wo.bump()
                    wo_ready = m_wo.n

                    idxa = c * 4 + tt
                    app = m_ast[idxa % 2]
                    app_prev = m_ast[(idxa - 1) % 2]

                    def f_evc(e, tt=tt, c=c, wr=wo_ready, app_prev=app_prev,
                              war=(app_prev.n if idxa >= 1 else None)):
                        e.wait_ge(m_wo.h, wr)
                        if war is not None:
                            e.wait_ge(app_prev.h, war)
                        gt = (c * 4 + tt) % 2
                        e.copy(attn_t[gt][:, 0:512], ps_wo[0][:])
                        _dr(e)
                        e.copy(attn_t[gt][:, 512:1024], ps_wo[1][:]
                               ).then_inc(m_evC.h, 2)
                        _dr(e)
                    P2.add("scalar", f_evc)
                    m_evC.bump(2)
                    evc_ready = m_evC.n

                    def f_ast(e, tt=tt, c=c, app=app, er=evc_ready):
                        e.wait_ge(m_evC.h, er)
                        gt = (c * 4 + tt) % 2
                        row = c * CK + tt * 128
                        e.dma_start(attn_d[row:row + 128, :], attn_t[gt][:]
                                    ).then_inc(app.h, 16)
                    P2.add("sync", f_ast)
                    app.bump(16)


            for c in range(NCH):
                def f_xnc(e, c=c, war=m_wiA.n):
                    if c >= 1:
                        e.wait_ge(m_wiA.h, war)
                    e.dma_start(
                        xnc[:].rearrange("p (j t) -> p j t", j=8),
                        xnt_d[:, :, c * CK:(c + 1) * CK].transpose([1, 0, 2]),
                    ).then_inc(m_xnc.h, 16)
                P2.add("sync", f_xnc)
                m_xnc.bump(16)
                xnc_ready = m_xnc.n

                for pt in range(32):
                    wpp = m_win[gpt % 2]

                    def f_wl(e, pt=pt, gpt=gpt, wpp=wpp, war=(m_wiA.n - 1)):
                        if gpt >= 2:
                            e.wait_ge(m_wiA.h, war)
                        e.dma_start(
                            winb[gpt % 2][:].rearrange("p (j k) -> p j k", j=8),
                            win_d[pt],
                        ).then_inc(wpp.h, 16)
                    P2.add("gpsimd", f_wl)
                    wpp.bump(16)
                    w_ready = wpp.n

                    def f_mm(e, gpt=gpt, wpp=wpp, xr=xnc_ready, wr=w_ready,
                             war=(m_evA.n - 1)):
                        e.wait_ge(m_xnc.h, xr)
                        e.wait_ge(wpp.h, wr)
                        if gpt >= 2:
                            e.wait_ge(m_evA.h, war)
                        for j in range(8):
                            inst = e.matmul(ps_win[gpt % 2][:],
                                            winb[gpt % 2][:, j * 128:(j + 1) * 128],
                                            xnc[:, j * CK:(j + 1) * CK],
                                            start=(j == 0), stop=(j == 7))
                        inst.then_inc(m_wiA.h, 1)
                    P2.add("tensor", f_mm)
                    m_wiA.bump()
                    mm_ready = m_wiA.n

                    def f_ev(e, pt=pt, gpt=gpt, c=c, mr=mm_ready,
                             hist=(16 * c + (pt % 16) + 1),
                             wy=(16 * (c - 1) + (pt - 16) + 1 if (c >= 1 and pt >= 16) else None)):
                        e.wait_ge(m_wiA.h, mr)
                        if pt < 16:
                            e.wait_ge(m_hist.h, hist)
                            e.copy(ring[:, pt * 520 + 3:pt * 520 + 515],
                                   ps_win[gpt % 2][:]).then_inc(m_evA.h, 1)
                        else:
                            if wy is not None:
                                e.wait_ge(m_y.h, wy)
                            e.activation(res_bf[:, (pt - 16) * CK:(pt - 15) * CK],
                                         ps_win[gpt % 2][:], AF.Silu
                                         ).then_inc(m_evA.h, 1)
                        _dr(e)
                    P2.add("scalar", f_ev)
                    m_evA.bump()
                    gpt += 1

                if c >= 1:
                    emit_wout(c - 1, prev_y_base[0])

                evA_base = m_evA.n - 32

                for dt in range(NDT):
                    def f_cv(e, dt=dt, ev=(evA_base + dt + 1),
                             wu=(m_u.n - 1)):
                        e.wait_ge(m_evA.h, ev)
                        if wu >= 1:
                            e.wait_ge(m_u.h, wu)
                        base = dt * 520
                        ca = convacc[dt % 2]
                        e.tensor_scalar(ca[:], ring[:, base:base + CK],
                                        convw_sb[:, dt * 4:dt * 4 + 1], None,
                                        ALU.mult)
                        _dr(e, True)
                        for k in range(1, 4):
                            inst = e.scalar_tensor_tensor(
                                ca[:], ring[:, base + k:base + k + CK],
                                convw_sb[:, dt * 4 + k:dt * 4 + k + 1],
                                ca[:], ALU.mult, ALU.add)
                            _dr(e, True)
                        inst.then_inc(m_conv.h, 1)
                        e.tensor_copy(ring[:, base:base + 3],
                                      ring[:, base + 512:base + 515]
                                      ).then_inc(m_hist.h, 1)
                        _dr(e)
                    P2.add("vector", f_cv)
                    m_conv.bump()
                    m_hist.bump()
                    cv_ready = m_conv.n

                    def f_u(e, dt=dt, c=c, cr=cv_ready,
                            wwx=(3 * c if c >= 1 else None),
                            wz=(16 * c if c >= 1 else None),
                            wy=(16 * c if c >= 1 else None)):
                        e.wait_ge(m_conv.h, cr)
                        if wwx is not None:
                            e.wait_ge(m_wx.h, wwx)
                            e.wait_ge(m_z.h, wz)
                            e.wait_ge(m_y.h, wy)
                        e.activation(u_bf[:, dt * CK:(dt + 1) * CK],
                                     convacc[dt % 2][:], AF.Silu,
                                     bias=convb_sb[:, dt:dt + 1]
                                     ).then_inc(m_u.h, 1)
                        _dr(e)
                    P2.add("scalar", f_u)
                    m_u.bump()

                u_all = m_u.n

                def f_wx(e, c=c, ur=u_all, war_s=m_evBs.n, war_v=m_evBv.n,
                         bev=(m_evBv.n + 1)):
                    e.wait_ge(m_u.h, ur)
                    if c >= 1:
                        e.wait_ge(m_evBs.h, war_s)
                        e.wait_ge(m_evBv.h, war_v)
                    for ct in range(16):
                        inst = e.matmul(ps_dr[:],
                                        wx_sb[:, ct * 96:ct * 96 + 64],
                                        u_bf[:, ct * CK:(ct + 1) * CK],
                                        start=(ct == 0), stop=(ct == 15))
                    inst.then_inc(m_wx.h, 1)
                    for ct in range(16):
                        inst = e.matmul(ps_bc[:],
                                        wx_sb[:, ct * 96 + 64:ct * 96 + 80],
                                        u_bf[:, ct * CK:(ct + 1) * CK],
                                        start=(ct == 0), stop=(ct == 15))
                    inst.then_inc(m_wx.h, 1)
                    e.wait_ge(m_evBv.h, bev)
                    for ct in range(16):
                        inst = e.matmul(ps_bc[:],
                                        wx_sb[:, ct * 96 + 80:ct * 96 + 96],
                                        u_bf[:, ct * CK:(ct + 1) * CK],
                                        start=(ct == 0), stop=(ct == 15))
                    inst.then_inc(m_wx.h, 1)
                P2.add("tensor", f_wx)
                m_wx.bump(3)
                wx_done = m_wx.n

                def f_evdr(e, wr=(wx_done - 2), wdt_prev=(m_wdt.n if c >= 1 else None)):
                    e.wait_ge(m_wx.h, wr)
                    if wdt_prev is not None:
                        e.wait_ge(m_wdt.h, wdt_prev)
                    e.copy(dr_bf[:], ps_dr[:]).then_inc(m_evBs.h, 1)
                    _dr(e)
                P2.add("scalar", f_evdr)
                m_evBs.bump()
                dr_ready = m_evBs.n

                def f_evbc(e, wrB=(wx_done - 1), wrC=wx_done,
                           wexp=(m_exp.n if c >= 1 else None)):
                    e.wait_ge(m_wx.h, wrB)
                    if wexp is not None:
                        e.wait_ge(m_exp.h, wexp)
                    e.tensor_copy(b_sbuf[:], ps_bc[:]).then_inc(m_evBv.h, 1)
                    _dr(e)
                    e.wait_ge(m_wx.h, wrC)
                    e.tensor_copy(c_sbuf[:], ps_bc[:]).then_inc(m_evBv.h, 1)
                    _dr(e)
                P2.add("vector", f_evbc)
                m_evBv.bump(2)
                bc_ready = m_evBv.n

                for src, dst in ((b_sbuf, bhat), (c_sbuf, chat)):
                    for n in range(NST):
                        def f_xp(e, src=src, n=n, br=bc_ready, war=m_evE.n):
                            e.wait_ge(m_evBv.h, br)
                            if war >= 1:
                                e.wait_ge(m_evE.h, war)
                            e.matmul(ps_exp[:], oh_sb[:, n * 128:(n + 1) * 128],
                                     src[:], start=True, stop=True
                                     ).then_inc(m_exp.h, 1)
                        P2.add("tensor", f_xp)
                        m_exp.bump()
                        xp_ready = m_exp.n

                        def f_xpe(e, dst=dst, n=n, xr=xp_ready,
                                  wb=(256 * c if c >= 1 else None)):
                            e.wait_ge(m_exp.h, xr)
                            if wb is not None:
                                e.wait_ge(m_b.h, wb)
                            e.tensor_copy(dst[:, n * CK:(n + 1) * CK],
                                          ps_exp[:]).then_inc(m_evE.h, 1)
                            _dr(e)
                        P2.add("vector", f_xpe)
                        m_evE.bump()
                evE_base = m_evE.n - 32

                for dt in range(NDT):
                    def f_wdt(e, dt=dt, drr=dr_ready, war=m_spr.n):
                        e.wait_ge(m_evBs.h, drr)
                        if war >= 1:
                            e.wait_ge(m_spr.h, war)
                        e.matmul(ps_dpre[:], wdt_sb[:, dt * 128:(dt + 1) * 128],
                                 dr_bf[:], start=True, stop=True
                                 ).then_inc(m_wdt.h, 1)
                    P2.add("tensor", f_wdt)
                    m_wdt.bump()
                    dpre_ready = m_wdt.n

                    def f_sp(e, dt=dt, dr=dpre_ready, war=m_dlt.n):
                        e.wait_ge(m_wdt.h, dr)
                        if war >= 1:
                            e.wait_ge(m_dlt.h, war)
                        bcol = bdt_sb[:, dt:dt + 1]
                        e.activation(sp1[:], ps_dpre[:], AF.Abs, bias=bcol)
                        _dr(e)
                        e.activation(sp2[:], ps_dpre[:], AF.Relu, bias=bcol
                                     ).then_inc(m_spr.h, 1)
                        _dr(e)
                        e.activation(delta[:], sp1[:], AF.Exp, scale=CV_NONE)
                        _dr(e)
                        e.activation(sp1[:], delta[:], AF.Ln, bias=CV_ONE
                                     ).then_inc(m_dsp.h, 1)
                        _dr(e)
                    P2.add("scalar", f_sp)
                    m_spr.bump()
                    m_dsp.bump()
                    sp_ready = m_dsp.n

                    def f_dlt(e, sr=sp_ready, wa=m_a.n, wz=m_z.n):
                        e.wait_ge(m_dsp.h, sr)
                        if wa >= 1:
                            e.wait_ge(m_a.h, wa)
                            e.wait_ge(m_z.h, wz)
                        e.tensor_add(delta[:], sp2[:], sp1[:]).then_inc(m_dlt.h, 1)
                        _dr(e)
                    P2.add("vector", f_dlt)
                    m_dlt.bump()
                    dlt_ready = m_dlt.n

                    def f_z(e, dt=dt, dr=dlt_ready):
                        e.wait_ge(m_dlt.h, dr)
                        e.tensor_mul(z_sb[:], delta[:],
                                     u_bf[:, dt * CK:(dt + 1) * CK]
                                     ).then_inc(m_z.h, 1)
                        _dr(e)
                    P2.add("gpsimd", f_z)
                    m_z.bump()
                    z_ready = m_z.n

                    def f_yi(e, dt=dt, ur=u_all):
                        e.wait_ge(m_u.h, ur)
                        e.tensor_scalar(yacc[:], u_bf[:, dt * CK:(dt + 1) * CK],
                                        dvec_sb[:, dt:dt + 1], None, ALU.mult)
                        _dr(e)
                    P2.add("vector", f_yi)

                    pend_ym = [None]
                    for n in range(NST):
                        par = g_dtn % 3
                        spar = g_dtn % 2
                        on_gp = n in (1, 4, 7, 10, 13, 15)

                        def f_a(e, dt=dt, n=n, par=par, dr=dlt_ready,
                                war=(m_scan.n - 2)):
                            e.wait_ge(m_dlt.h, dr)
                            if war >= 1:
                                e.wait_ge(m_scan.h, war)
                            e.activation(asb[par][:], delta[:], AF.Exp,
                                         scale=a_sb[:, dt * 16 + n:dt * 16 + n + 1]
                                         ).then_inc(m_a.h, 1)
                            _dr(e)
                        P2.add("scalar", f_a)
                        m_a.bump()
                        a_ready = m_a.n

                        def f_b(e, n=n, par=par, zr=z_ready,
                                be=(evE_base + n + 1), war=(m_scan.n - 2)):
                            e.wait_ge(m_z.h, zr)
                            e.wait_ge(m_evE.h, be)
                            if war >= 1:
                                e.wait_ge(m_scan.h, war)
                            e.tensor_mul(bsb[par][:], z_sb[:],
                                         bhat[:, n * CK:(n + 1) * CK]
                                         ).then_inc(m_b.h, 1)
                            _dr(e)
                        P2.add("gpsimd", f_b)
                        m_b.bump()
                        b_ready = m_b.n
                        if pend_ym[0] is not None:
                            P2.add("gpsimd", pend_ym[0])
                            pend_ym[0] = None

                        gwar = g_gp.get(g_dtn - 2)

                        def f_scan(e, dt=dt, n=n, par=par, spar=spar,
                                   ar=a_ready, br=b_ready, on_gp=on_gp,
                                   gwar=gwar, ce=(evE_base + 16 + n + 1)):
                            e.wait_ge(m_a.h, ar)
                            e.wait_ge(m_b.h, br)
                            if gwar is not None:
                                e.wait_ge(m_yg.h, gwar)
                            col = dt * 16 + n
                            e.tensor_tensor_scan(ssb[spar][:], asb[par][:],
                                                 bsb[par][:],
                                                 carry[:, col:col + 1],
                                                 ALU.mult, ALU.add
                                                 ).then_inc(m_scan.h, 1)
                            _dr(e)
                            e.tensor_copy(carry[:, col:col + 1],
                                          ssb[spar][:, CK - 1:CK])
                            _dr(e)
                            if not on_gp:
                                e.wait_ge(m_evE.h, ce)
                                e.tensor_mul(ytmp[:], ssb[spar][:],
                                             chat[:, n * CK:(n + 1) * CK])
                                _dr(e)
                                e.tensor_add(yacc[:], yacc[:], ytmp[:])
                                _dr(e)
                        P2.add("vector", f_scan)
                        m_scan.bump()
                        if on_gp:
                            def f_ym(e, n=n, par=spar, sr=m_scan.n,
                                     ce=(evE_base + 16 + n + 1),
                                     first=(n == 1), dt=dt, c=c,
                                     wy=(m_y.n if (c + dt) > 0 else None)):
                                e.wait_ge(m_scan.h, sr)
                                e.wait_ge(m_evE.h, ce)
                                e.tensor_mul(ytmp2[:], ssb[par][:],
                                             chat[:, n * CK:(n + 1) * CK])
                                _dr(e)
                                if first:
                                    if wy is not None:
                                        e.wait_ge(m_y.h, wy)
                                    e.tensor_copy(yacc2[:], ytmp2[:]
                                                  ).then_inc(m_yg.h, 1)
                                else:
                                    e.tensor_add(yacc2[:], yacc2[:], ytmp2[:]
                                                 ).then_inc(m_yg.h, 1)
                                _dr(e)
                            pend_ym[0] = f_ym
                            m_yg.bump()
                            g_gp[g_dtn] = m_yg.n
                        g_dtn += 1

                    if pend_ym[0] is not None:
                        P2.add("gpsimd", pend_ym[0])
                        pend_ym[0] = None

                    def f_yb(e, dt=dt, c=c, rr=(evA_base + 16 + dt + 1),
                             ygr=m_yg.n,
                             wwo=(4 * c if c >= 1 else None)):
                        e.wait_ge(m_evA.h, rr)
                        e.wait_ge(m_yg.h, ygr)
                        if wwo is not None:
                            e.wait_ge(m_wo.h, wwo)
                        e.tensor_add(yacc[:], yacc[:], yacc2[:])
                        _dr(e)
                        e.tensor_mul(ybf[:, dt * CK:(dt + 1) * CK], yacc[:],
                                     res_bf[:, dt * CK:(dt + 1) * CK]
                                     ).then_inc(m_y.h, 1)
                        _dr(e)
                    P2.add("vector", f_yb)
                    m_y.bump()

                prev_y_base[0] = m_y.n - NDT

            emit_wout(NCH - 1, prev_y_base[0])

            def b2_sync(e, w0=m_ast[0].n, w1=m_ast[1].n):
                e.wait_ge(m_ast[0].h, w0)
                e.wait_ge(m_ast[1].h, w1)
                e.nop().then_inc(bar2.h, 1)
            P2.add("sync", b2_sync)

            def b2_t(e):
                e.nop().then_inc(bar2.h, 1)
            P2.add("tensor", b2_t)

            for eng in ("vector", "scalar", "gpsimd"):
                def b2_e(e):
                    e.drain().then_inc(bar2.h, 1)
                P2.add(eng, b2_e)
            bar2.bump(5)

            blk2 = p2.enter_context(nc.Block("p2"))
            P2.run(blk2)

        with ExitStack() as p3:
            sb3 = lambda n_, s, d: p3.enter_context(nc.sbuf_tensor(n_, s, d))
            ps3 = lambda n_, s, d=F32: p3.enter_context(nc.psum_tensor(n_, s, d))

            xnr = sb3("xnr", [128, 8 * SC], BF16)
            shat = [sb3(f"shat{i}", [128, SC], BF16) for i in range(2)]
            xe = sb3("xe", [128, 8 * SC], BF16)
            wgt_t = [sb3(f"wgt_t{i}", [128, 8 * 128], BF16) for i in range(3)]
            wut_t = [sb3(f"wut_t{i}", [128, 8 * 128], BF16) for i in range(3)]
            h_sb = sb3("h_sb", [128, 16 * SC], BF16)
            wdn_sb = [sb3(f"wdn_sb{i}", [128, 16 * 1024], BF16) for i in range(2)]
            acc = sb3("acc", [128, 8 * 1024], F32)
            sgtmp = sb3("sgtmp", [128, 512], F32)
            att_t = [sb3(f"att_t{i}", [128, 1024], F32) for i in range(2)]

            ps_s = ps3("ps_s", [128, 512])
            ps_g = [ps3(f"ps_g{i}", [128, 512]) for i in range(2)]
            ps_u = [ps3(f"ps_u{i}", [128, 512]) for i in range(2)]
            ps_d = [ps3(f"ps_d{i}", [128, 512]) for i in range(2)]

            P3 = Prog()
            NB2 = 5
            for eng in Prog.ENGS:
                def pf3(e):
                    e.wait_ge(bar2.h, NB2)
                P3.add(eng, pf3)

            gih = 0
            ge = 0
            gtt = 0
            pend_ep = [None]

            def emit_sx(sc, e_i, gu_prev, xnrr):
                shared = e_i == NE
                if not shared:
                    for th in range(2):
                        def f_sc(e, e_i=e_i, sc=sc, th=th,
                                 war=(n_scev.n if n_sc.n >= 1 else None)):
                            if war is not None:
                                e.wait_ge(n_scev.h, war)
                            e.matmul(
                                ps_s[:],
                                oh_sb[:, e_i * 128:(e_i + 1) * 128],
                                scores[:, sc * SC + th * 512:
                                       sc * SC + th * 512 + 512],
                                start=True, stop=True).then_inc(n_sc.h, 1)
                        P3.add("tensor", f_sc)
                        n_sc.bump()
                        sc_ready = n_sc.n

                        def f_sce(e, e_i=e_i, th=th, sr=sc_ready):
                            e.wait_ge(n_sc.h, sr)
                            e.tensor_copy(
                                shat[e_i % 2][:, th * 512:(th + 1) * 512],
                                ps_s[:]).then_inc(n_scev.h, 1)
                            _dr(e)
                        P3.add("vector", f_sce)
                        n_scev.bump()
                shat_ready = n_scev.n
                x_base = n_X.n
                for j in range(8):
                    def f_X(e, j=j, e_i=e_i, shared=shared, xnrr=xnrr,
                            shr=shat_ready, pg=gu_prev):
                        e.wait_ge(n_xnr.h, xnrr)
                        e.wait_ge(n_gu.h, pg)
                        if not shared:
                            e.wait_ge(n_scev.h, shr)
                            e.tensor_mul(xe[:, j * SC:(j + 1) * SC],
                                         xnr[:, j * SC:(j + 1) * SC],
                                         shat[e_i % 2][:]).then_inc(n_X.h, 1)
                        else:
                            e.tensor_copy(xe[:, j * SC:(j + 1) * SC],
                                          xnr[:, j * SC:(j + 1) * SC]
                                          ).then_inc(n_X.h, 1)
                        _dr(e)
                    P3.add("vector", f_X)
                    n_X.bump()
                return x_base

            for sc in range(NSC):
                def f_xnr(e, sc=sc, war=n_X.n):
                    if sc >= 1:
                        e.wait_ge(n_X.h, war)
                    e.dma_start(
                        xnr[:].rearrange("p (j t) -> p j t", j=8),
                        xnt_d[:, :, sc * SC:(sc + 1) * SC].transpose([1, 0, 2]),
                    ).then_inc(n_xnr.h, 16)
                P3.add("sync", f_xnr)
                n_xnr.bump(16)
                xnr_ready = n_xnr.n

                x_base = emit_sx(sc, 0, n_gu.n, xnr_ready)

                for e_i in range(NE + 1):
                    shared = e_i == NE

                    prev_dn_done = n_dn.n
                    sl_base = n_sl.n
                    h_base0 = n_h.n
                    wdp = n_wdn[ge % 2]

                    def f_wdl(e, e_i=e_i, ge=ge, wdp=wdp, shared=shared,
                              war=(n_dn.n - 8 if ge >= 2 else None)):
                        if war is not None:
                            e.wait_ge(n_dn.h, max(war, 0))
                        src = (wdsh_d if shared else dwn_d[e_i]
                               ).rearrange("p c k -> p (c k)")
                        e.dma_start(wdn_sb[ge % 2][:], src).then_inc(wdp.h, 16)
                    P3.add("gpsimd", f_wdl)
                    wdp.bump(16)
                    wdn_ready = wdp.n


                    for i in range(16):
                        wgp = n_wg[gih % 3]

                        def f_wgl(e, e_i=e_i, i=i, gih=gih, wgp=wgp,
                                  shared=shared,
                                  war=(n_gu.n - 8 if gih >= 3 else None)):
                            if war is not None:
                                e.wait_ge(n_gu.h, max(war, 0))
                            if shared:
                                gsrc = wgt_d[i].rearrange("p c f -> p (c f)")
                                usrc = wut_d[i].rearrange("p c f -> p (c f)")
                            else:
                                gsrc = gup_d[e_i, i].rearrange("p c f -> p (c f)")
                                usrc = gup_d[e_i, 16 + i].rearrange("p c f -> p (c f)")
                            e.dma_start(wgt_t[gih % 3][:], gsrc
                                        ).then_inc(wgp.h, 16)
                            e.dma_start(wut_t[gih % 3][:], usrc
                                        ).then_inc(wgp.h, 16)
                        P3.add("gpsimd", f_wgl)
                        wgp.bump(32)
                        wg_ready = wgp.n

                        for th in range(2):
                            war_sl = sl_base + (2 * i - 2 if th == 0 else 2 * i + 1)
                            war_h = h_base0 + (2 * i - 2 if th == 0 else 2 * i + 1)

                            def f_gu(e, i=i, th=th, gih=gih, wgp=wgp,
                                     wr=wg_ready, xb=x_base,
                                     wsl=max(war_sl, 0), wh=max(war_h, 0)):
                                e.wait_ge(wgp.h, wr)
                                if wsl >= 1:
                                    e.wait_ge(n_sl.h, wsl)
                                if wh >= 1:
                                    e.wait_ge(n_h.h, wh)
                                for ct in range(8):
                                    e.wait_ge(n_X.h, xb + ct + 1)
                                    rhs = xe[:, ct * SC + th * 512:
                                             ct * SC + th * 512 + 512]
                                    ig = e.matmul(ps_g[i % 2][:],
                                                  wgt_t[gih % 3][:, ct * 128:
                                                                 (ct + 1) * 128],
                                                  rhs, start=(ct == 0),
                                                  stop=(ct == 7))
                                ig.then_inc(n_gu.h, 1)
                                for ct in range(8):
                                    rhs = xe[:, ct * SC + th * 512:
                                             ct * SC + th * 512 + 512]
                                    iu = e.matmul(ps_u[i % 2][:],
                                                  wut_t[gih % 3][:, ct * 128:
                                                                 (ct + 1) * 128],
                                                  rhs, start=(ct == 0),
                                                  stop=(ct == 7))
                                iu.then_inc(n_gu.h, 1)
                            P3.add("tensor", f_gu)
                            n_gu.bump(2)
                            gu_ready = n_gu.n

                            def f_sl(e, i=i, gr=(gu_ready - 1), whp=n_h.n):
                                e.wait_ge(n_gu.h, gr)
                                if whp >= 1:
                                    e.wait_ge(n_h.h, whp)
                                e.activation(sgtmp[:], ps_g[i % 2][:], AF.Silu
                                             ).then_inc(n_sl.h, 1)
                                _dr(e)
                            P3.add("scalar", f_sl)
                            n_sl.bump()
                            sl_ready = n_sl.n

                            def f_h(e, i=i, th=th, gr=gu_ready, slr=sl_ready,
                                    pd=prev_dn_done):
                                e.wait_ge(n_gu.h, gr)
                                e.wait_ge(n_sl.h, slr)
                                e.wait_ge(n_dn.h, pd)
                                e.tensor_mul(
                                    h_sb[:, i * SC + th * 512:
                                         i * SC + th * 512 + 512],
                                    sgtmp[:], ps_u[i % 2][:]).then_inc(n_h.h, 1)
                                _dr(e)
                            P3.add("vector", f_h)
                            n_h.bump()
                        gih += 1
                    h_base = n_h.n - 32

                    if e_i < NE:
                        x_base = emit_sx(sc, e_i + 1, n_gu.n, xnr_ready)

                    for tt in range(8):
                        def f_dn(e, tt=tt, ge=ge, gtt=gtt, hb=h_base,
                                 wdp=wdp, wr=wdn_ready, war=n_acc.n):
                            e.wait_ge(wdp.h, wr)
                            if gtt >= 1:
                                e.wait_ge(n_acc.h, war)
                            for ct in range(16):
                                e.wait_ge(n_h.h, hb + 2 * (ct + 1))
                                lhs = h_sb[:, ct * SC + tt * 128:
                                           ct * SC + tt * 128 + 128]
                                e.matmul(ps_d[0][:], lhs,
                                         wdn_sb[ge % 2][:, ct * 1024:
                                                        ct * 1024 + 512],
                                         start=(ct == 0), stop=(ct == 15))
                                inst = e.matmul(ps_d[1][:], lhs,
                                                wdn_sb[ge % 2][:, ct * 1024 + 512:
                                                               (ct + 1) * 1024],
                                                start=(ct == 0), stop=(ct == 15))
                            inst.then_inc(n_dn.h, 1)
                        P3.add("tensor", f_dn)
                        n_dn.bump()
                        dn_ready = n_dn.n
                        gtt += 1

                        if e_i == 0:
                            def f_acc(e, tt=tt, sc=sc, dr=dn_ready,
                                      war=(8 * sc if sc >= 1 else None)):
                                e.wait_ge(n_dn.h, dr)
                                if war is not None:
                                    e.wait_ge(n_ep.h, war)
                                e.tensor_copy(acc[:, tt * 1024:tt * 1024 + 512],
                                              ps_d[0][:])
                                _dr(e)
                                e.tensor_copy(acc[:, tt * 1024 + 512:(tt + 1) * 1024],
                                              ps_d[1][:]).then_inc(n_acc.h, 1)
                                _dr(e)
                        else:
                            def f_acc(e, tt=tt, dr=dn_ready):
                                e.wait_ge(n_dn.h, dr)
                                pd = 0
                                e.tensor_add(acc[:, tt * 1024:tt * 1024 + 512],
                                             acc[:, tt * 1024:tt * 1024 + 512],
                                             ps_d[pd][:])
                                _dr(e)
                                e.tensor_add(acc[:, tt * 1024 + 512:(tt + 1) * 1024],
                                             acc[:, tt * 1024 + 512:(tt + 1) * 1024],
                                             ps_d[pd + 1][:]).then_inc(n_acc.h, 1)
                                _dr(e)
                        P3.add("vector", f_acc)
                        n_acc.bump()
                    ge += 1
                    if e_i == 0 and pend_ep[0] is not None:
                        pend_ep[0]()
                        pend_ep[0] = None

                acc_done = n_acc.n
                for tt in range(8):
                    idx = sc * 8 + tt

                    xap = n_xat[idx % 2]
                    oxp = n_ox[idx % 2]
                    opp = n_out[idx % 2]

                    def f_lx(e, sc=sc, tt=tt, idx=idx, xap=xap, opp=opp,
                             owar=(16 * (idx // 2) if idx >= 2 else None)):
                        if owar is not None:
                            e.wait_ge(opp.h, owar)
                        row = sc * SC + tt * 128
                        e.dma_start(att_t[idx % 2][:], attn_d[row:row + 128, :]
                                    ).then_inc(xap.h, 16)
                    P3.add("sync", f_lx)
                    xap.bump(16)
                    xat_ready = xap.n

                    def f_ep(e, tt=tt, idx=idx, xap=xap,
                             ar=(acc_done - 8 + tt + 1), xr=xat_ready):
                        gt = idx % 2
                        e.wait_ge(n_acc.h, ar)
                        e.wait_ge(xap.h, xr)
                        e.tensor_add(att_t[gt][:],
                                     acc[:, tt * 1024:(tt + 1) * 1024],
                                     att_t[gt][:]).then_inc(n_ep.h, 1)
                        _dr(e)
                    P3.add("vector", f_ep)
                    n_ep.bump()
                    ep_ready = n_ep.n

                    def f_xadd(e, sc=sc, tt=tt, idx=idx, oxp=oxp, er=ep_ready):
                        e.wait_ge(n_ep.h, er)
                        row = sc * SC + tt * 128
                        e.dma_start(att_t[idx % 2][:], x_d[row:row + 128, :],
                                    accum_op=ALU.add).then_inc(oxp.h, 16)
                    P3.add("gpsimd", f_xadd)
                    oxp.bump(16)
                    ox_ready = oxp.n

                    def f_out(e, sc=sc, tt=tt, idx=idx, opp=opp, oxp=oxp,
                              er=ox_ready):
                        e.wait_ge(oxp.h, er)
                        gt = idx % 2
                        row = sc * SC + tt * 128
                        e.dma_start(out_d[row:row + 128, :], att_t[gt][:]
                                    ).then_inc(opp.h, 16)
                    P3.add("sync", f_out)
                    opp.bump(16)

            def f_fin(e, w0=n_out[0].n, w1=n_out[1].n):
                e.wait_ge(n_out[0].h, w0)
                e.wait_ge(n_out[1].h, w1)
            P3.add("sync", f_fin)

            blk3 = p3.enter_context(nc.Block("p3"))
            P3.run(blk3)

    return nc


def _host_layout(inputs):
    f32 = np.float32
    g = lambda k: np.asarray(inputs[k])
    conv_w = g("L0_attn_conv_w")
    cw = np.ascontiguousarray(
        conv_w[:, 0, :].T.reshape(16, 128, 4).transpose(1, 0, 2).reshape(128, 64))
    cb = np.ascontiguousarray(g("L0_attn_conv_b").reshape(16, 128).T)
    bdt = np.ascontiguousarray(g("L0_attn_b_dt").reshape(16, 128).T)
    dvec = np.ascontiguousarray(g("L0_attn_D").reshape(16, 128).T)
    alog = np.ascontiguousarray(
        g("L0_attn_A_log").reshape(16, 128, 16).transpose(1, 0, 2).reshape(128, 256))
    onehot = np.zeros((16, 16 * 128), ml_dtypes.bfloat16)
    for n in range(16):
        onehot[n, n * 128:(n + 1) * 128] = 1.0
    def tile_lhs(w, nin, nout):
        return np.ascontiguousarray(
            w.reshape(nin, 128, nout, 128).transpose(2, 1, 0, 3))

    def tile_rhs(w, nin, k):
        return np.ascontiguousarray(
            w.reshape(nin, 128, k).transpose(1, 0, 2))

    gup = g("L0_ffn_gate_up_proj")
    gup_t = np.ascontiguousarray(
        gup.reshape(16, 8, 128, 32, 128).transpose(0, 3, 2, 1, 4))
    dwn = g("L0_ffn_down_proj")
    dwn_t = np.ascontiguousarray(
        dwn.reshape(16, 16, 128, 1024).transpose(0, 2, 1, 3))
    return {
        "win": tile_lhs(g("L0_attn_W_in"), 8, 32),
        "wx": np.ascontiguousarray(g("L0_attn_W_x")),
        "wdt": np.ascontiguousarray(g("L0_attn_W_dt")),
        "convw": cw, "convb": cb, "bdt": bdt, "alog": alog, "dvec": dvec,
        "wout": tile_rhs(g("L0_attn_W_out"), 16, 1024),
        "wrt": np.ascontiguousarray(g("L0_ffn_w_router").T),
        "gup": gup_t,
        "dwn": dwn_t,
        "wgt": tile_lhs(np.ascontiguousarray(g("L0_ffn_w_gate").T), 8, 16),
        "wut": tile_lhs(np.ascontiguousarray(g("L0_ffn_w_up").T), 8, 16),
        "wdsh": tile_rhs(np.ascontiguousarray(g("L0_ffn_w_down").T), 16, 1024),
        "oh": onehot,
        "idb": np.eye(128, dtype=ml_dtypes.bfloat16),
        "idf": np.eye(128, dtype=f32),
        "cvals": np.tile(np.array([[EPS, 1.0, 1.0 / D, -0.5, -1.0, 0.0, 0.0, 0.0]],
                                  dtype=f32), (128, 1)),
    }


def kernel(**inputs):
    common = _host_layout(inputs)
    x = np.asarray(inputs["L0_attn_x"])
    nc = build_program()
    in_maps = []
    for b in range(B):
        m = dict(common)
        m["x"] = np.ascontiguousarray(x[b])
        in_maps.append(m)
    res = run_bass_kernel_spmd(nc, in_maps, core_ids=list(range(B)))
    return np.stack([r["out"] for r in res.results], axis=0)


if __name__ == "__main__":
    import time
    t0 = time.time()
    nc = build_program()
    print("built", nc.n_instructions(), "instrs in", round(time.time() - t0, 1), "s")


# version 16
# speedup vs baseline: 1.6707x; 1.0621x over previous
import numpy as np
import ml_dtypes
from contextlib import ExitStack

import concourse.bass as bass
import concourse.mybir as mybir
from concourse.bass_utils import run_bass_kernel_spmd

F32 = mybir.dt.float32
BF16 = mybir.dt.bfloat16
AF = mybir.ActivationFunctionType
ALU = mybir.AluOpType
AX = mybir.AxisListType

B, T, D = 8, 4096, 1024
DIN = 2048
NST = 16
RNK = 64
NE = 16
FF = 2048
NT = T // 128
CK = 512
NCH = T // CK
NDT = DIN // 128
SC = 1024
NSC = T // SC
EPS = 1e-5

DEBUG_SCRATCH_OUT = False
USE_DRAINS = False


def _dr(e, always=False):
    if USE_DRAINS or always:
        e.drain()
class Ctr:
    def __init__(self, h):
        self.h = h
        self.n = 0

    def bump(self, k=1):
        self.n += k
        return self.n


class Prog:
    ENGS = ("sync", "tensor", "vector", "scalar", "gpsimd")

    def __init__(self):
        self.ops = {k: [] for k in self.ENGS}

    def add(self, eng, fn):
        self.ops[eng].append(fn)

    def run(self, block):
        for name in self.ENGS:
            fns = self.ops[name]
            if not fns:
                continue

            def body(e, fns=fns):
                for fn in fns:
                    fn(e)

            getattr(block, name)(body)


def build_program():
    nc = bass.Bass("TRN2", target_bir_lowering=False)

    ap = lambda h: h.ap()
    x_d = ap(nc.dram_tensor("x", [T, D], F32, kind="ExternalInput"))
    win_d = ap(nc.dram_tensor("win", [32, 128, 8, 128], F32, kind="ExternalInput"))
    wx_d = ap(nc.dram_tensor("wx", [DIN, 96], F32, kind="ExternalInput"))
    wdt_d = ap(nc.dram_tensor("wdt", [RNK, DIN], F32, kind="ExternalInput"))
    convw_d = ap(nc.dram_tensor("convw", [128, 64], F32, kind="ExternalInput"))
    convb_d = ap(nc.dram_tensor("convb", [128, 16], F32, kind="ExternalInput"))
    bdt_d = ap(nc.dram_tensor("bdt", [128, 16], F32, kind="ExternalInput"))
    alog_d = ap(nc.dram_tensor("alog", [128, 256], F32, kind="ExternalInput"))
    dvec_d = ap(nc.dram_tensor("dvec", [128, 16], F32, kind="ExternalInput"))
    wout_d = ap(nc.dram_tensor("wout", [128, 16, 1024], F32, kind="ExternalInput"))
    wrt_d = ap(nc.dram_tensor("wrt", [D, NE], F32, kind="ExternalInput"))
    gup_d = ap(nc.dram_tensor("gup", [NE, 32, 128, 8, 128], F32, kind="ExternalInput"))
    dwn_d = ap(nc.dram_tensor("dwn", [NE, 128, 16, 1024], F32, kind="ExternalInput"))
    wgt_d = ap(nc.dram_tensor("wgt", [16, 128, 8, 128], F32, kind="ExternalInput"))
    wut_d = ap(nc.dram_tensor("wut", [16, 128, 8, 128], F32, kind="ExternalInput"))
    wdsh_d = ap(nc.dram_tensor("wdsh", [128, 16, 1024], F32, kind="ExternalInput"))
    oh_d = ap(nc.dram_tensor("oh", [16, 16 * 128], BF16, kind="ExternalInput"))
    idb_d = ap(nc.dram_tensor("idb", [128, 128], BF16, kind="ExternalInput"))
    idf_d = ap(nc.dram_tensor("idf", [128, 128], F32, kind="ExternalInput"))
    cv_d = ap(nc.dram_tensor("cvals", [128, 8], F32, kind="ExternalInput"))

    skind = "ExternalOutput" if DEBUG_SCRATCH_OUT else "Internal"
    xnt_d = ap(nc.dram_tensor("xnt_s", [8, 128, T], BF16, kind=skind))
    attn_d = ap(nc.dram_tensor("attn_s", [T, D], F32, kind=skind))
    out_d = ap(nc.dram_tensor("out", [T, D], F32, kind="ExternalOutput"))

    ctx = ExitStack()
    with ctx:
        def sbuf(name, shape, dtype):
            return ctx.enter_context(nc.sbuf_tensor(name, shape, dtype))

        def sem(name):
            return Ctr(ctx.enter_context(nc.semaphore(name)))

        oh_sb = sbuf("oh_sb", [16, 16 * 128], BF16)
        idb_sb = sbuf("idb_sb", [128, 128], BF16)
        idf_sb = sbuf("idf_sb", [128, 128], F32)
        scores = sbuf("scores", [16, T], BF16)
        a_sb = sbuf("a_sb", [128, 256], F32)
        dvec_sb = sbuf("dvec_sb", [128, 16], F32)
        convw_sb = sbuf("convw_sb", [128, 64], F32)
        convb_sb = sbuf("convb_sb", [128, 16], F32)
        bdt_sb = sbuf("bdt_sb", [128, 16], F32)
        wx_sb = sbuf("wx_sb", [128, 16 * 96], BF16)
        wdt_sb = sbuf("wdt_sb", [64, 16 * 128], BF16)
        wrt_sb = sbuf("wrt_sb", [128, 8 * 16], BF16)
        cv_sb = sbuf("cv_sb", [128, 8], F32)
        CV_EPS, CV_ONE, CV_INVD, CV_NHALF, CV_NONE = (
            cv_sb[:, 0:1], cv_sb[:, 1:2], cv_sb[:, 2:3], cv_sb[:, 3:4], cv_sb[:, 4:5])

        c_ld = sem("c_ld")
        c_gld = sem("c_gld")
        c_prep = sem("c_prep")
        c_prep2 = sem("c_prep2")
        bar1 = sem("bar1")
        bar2 = sem("bar2")

        s1x = [sem("s1x0"), sem("s1x1")]
        s1v = sem("s1v")
        s1s = sem("s1s")
        s1t = sem("s1t")
        s1e = sem("s1e")
        s1w = [sem("s1w0"), sem("s1w1")]
        s1r = sem("s1r")
        s1sc = sem("s1sc")

        m_xnc = sem("m_xnc")
        m_win = [sem("m_win0"), sem("m_win1")]
        m_wres = sem("m_wres")
        m_wiA = sem("m_wiA")
        m_evA = sem("m_evA")
        m_hist = sem("m_hist")
        m_conv = sem("m_conv")
        m_u = sem("m_u")
        m_wx = sem("m_wx")
        m_evBs = sem("m_evBs")
        m_evBv = sem("m_evBv")
        m_exp = sem("m_exp")
        m_evE = sem("m_evE")
        m_wdt = sem("m_wdt")
        m_spr = sem("m_spr")
        m_dsp = sem("m_dsp")
        m_dlt = sem("m_dlt")
        m_z = sem("m_z")
        m_a = sem("m_a")
        m_b = sem("m_b")
        m_yg = sem("m_yg")
        m_ytv = sem("m_ytv")
        m_ytg = sem("m_ytg")
        m_ymm = sem("m_ymm")
        m_scan = sem("m_scan")
        m_y = sem("m_y")
        m_wo = sem("m_wo")
        m_evC = sem("m_evC")
        m_ast = [sem("m_ast0"), sem("m_ast1")]

        n_xnr = sem("n_xnr")
        n_X = sem("n_X")
        n_sc = sem("n_sc")
        n_scev = sem("n_scev")
        n_wg = [sem("n_wg0"), sem("n_wg1"), sem("n_wg2")]
        n_wdn = [sem("n_wdn0"), sem("n_wdn1")]
        n_gu = sem("n_gu")
        n_sl = sem("n_sl")
        n_h = sem("n_h")
        n_dn = sem("n_dn")
        n_acc = sem("n_acc")
        n_xat = [sem("n_xat0"), sem("n_xat1")]
        n_ep = sem("n_ep")
        n_ox = [sem("n_ox0"), sem("n_ox1")]
        n_out = [sem("n_out0"), sem("n_out1")]

        with ExitStack() as p1:
            sb1 = lambda n_, s, d: p1.enter_context(nc.sbuf_tensor(n_, s, d))
            ps1 = lambda n_, s, d=F32: p1.enter_context(nc.psum_tensor(n_, s, d))

            xin = [sb1(f"xin{i}", [128, D], F32) for i in range(2)]
            junk = sb1("junk", [128, D], F32)
            stat = sb1("stat", [128, 8], F32)
            xn_t = sb1("xn_t", [128, D], BF16)
            xnt_sb = sb1("xnt_sb", [128, 8 * 512], BF16)
            sgt = sb1("sgt", [128, 16], F32)
            mskt = sb1("mskt", [128, 16], F32)
            sct = sb1("sct", [128, 16], F32)
            mxt = sb1("mxt", [128, 1], F32)
            atmp = sb1("atmp", [128, 256], F32)
            tp_ps = [ps1(f"tp_ps{i}", [128, 1024], BF16) for i in range(2)]
            lg_ps = ps1("lg_ps", [128, 512], F32)
            st_ps = ps1("st_ps", [16, 512], F32)

            P1 = Prog()

            def c_sync(e):
                for dst, src in ((oh_sb, oh_d), (idf_sb, idf_d),
                                 (convw_sb, convw_d), (convb_sb, convb_d),
                                 (bdt_sb, bdt_d), (dvec_sb, dvec_d),
                                 (atmp, alog_d), (cv_sb, cv_d)):
                    e.dma_start(dst[:], src[:]).then_inc(c_ld.h, 16)
            P1.add("sync", c_sync)
            c_ld.bump(16 * 8)

            def c_gp(e):
                e.dma_start(idb_sb[:], idb_d[:]).then_inc(c_gld.h, 16)
                e.dma_start(
                    wx_sb[:].rearrange("p (c k) -> p c k", c=16),
                    wx_d.rearrange("(c p) k -> p c k", p=128),
                ).then_inc(c_gld.h, 16)
                e.dma_start(
                    wdt_sb[:].rearrange("p (c k) -> p c k", c=16),
                    wdt_d.rearrange("p (c k) -> p c k", c=16),
                ).then_inc(c_gld.h, 16)
                e.dma_start(
                    wrt_sb[:].rearrange("p (c k) -> p c k", c=8),
                    wrt_d.rearrange("(c p) k -> p c k", p=128),
                ).then_inc(c_gld.h, 16)
            P1.add("gpsimd", c_gp)
            c_gld.bump(16 * 4)

            def c_sc(e):
                e.wait_ge(c_ld.h, 16 * 8)
                e.activation(atmp[:], atmp[:], AF.Exp).then_inc(c_prep.h, 1)
                _dr(e, True)
            P1.add("scalar", c_sc)
            c_prep.bump()

            def c_vec(e):
                e.wait_ge(c_prep.h, 1)
                e.tensor_scalar_mul(a_sb[:], atmp[:], -1.0).then_inc(c_prep2.h, 1)
                _dr(e, True)
            P1.add("vector", c_vec)
            c_prep2.bump()

            v_done = {}
            s_done = {}
            sg_done = {}
            e_done = {}
            w_done = {}
            mul_done = {}
            sce_done = {}
            rt_done = {}
            stp_done = {}

            for tt in range(NT):
                buf = xin[tt % 2]

                xp = s1x[tt % 2]

                def f_sync(e, tt=tt, buf=buf, xp=xp,
                           wv=(v_done.get(tt - 2)), ws=(s_done.get(tt - 2))):
                    if wv is not None:
                        e.wait_ge(s1v.h, wv)
                        e.wait_ge(s1s.h, ws)
                    e.dma_start(buf[:], x_d[tt * 128:(tt + 1) * 128, :]
                                ).then_inc(xp.h, 16)
                P1.add("sync", f_sync)
                xp.bump(16)
                x_ready = xp.n

                def f_v1(e, buf=buf, xp=xp, xr=x_ready, ws=s_done.get(tt - 1)):
                    e.wait_ge(xp.h, xr)
                    if ws is not None:
                        e.wait_ge(s1s.h, ws)
                    e.tensor_reduce(stat[:, 0:1], buf[:], AX.X, ALU.add)
                    _dr(e, True)
                    e.tensor_scalar_mul(stat[:, 0:1], stat[:, 0:1], -1.0 / D
                                        ).then_inc(s1v.h, 1)
                    _dr(e, True)
                P1.add("vector", f_v1)
                s1v.bump()
                nm_ready = s1v.n

                def f_s1(e, buf=buf, xp=xp, xr=x_ready, nmr=nm_ready):
                    e.wait_ge(xp.h, xr)
                    e.wait_ge(s1v.h, nmr)
                    e.activation(junk[:], buf[:], AF.Square,
                                 bias=stat[:, 0:1], accum_out=stat[:, 1:2])
                    _dr(e, True)
                    e.activation(stat[:, 2:3], stat[:, 1:2], AF.Ln,
                                 scale=CV_INVD, bias=CV_EPS)
                    _dr(e, True)
                    e.activation(stat[:, 3:4], stat[:, 2:3], AF.Exp,
                                 scale=CV_NHALF).then_inc(s1s.h, 1)
                    _dr(e, True)
                P1.add("scalar", f_s1)
                s1s.bump()
                rstd_ready = s1s.n
                s_done[tt] = s1s.n

                def f_xn(e, buf=buf, rr=rstd_ready):
                    e.wait_ge(s1s.h, rr)
                    e.tensor_scalar(xn_t[:], buf[:], stat[:, 0:1],
                                    stat[:, 3:4], ALU.add, ALU.mult
                                    ).then_inc(s1v.h, 1)
                    _dr(e, True)
                P1.add("vector", f_xn)
                s1v.bump()
                xn_ready = s1v.n
                v_done[tt] = s1v.n

                for j in range(8):
                    tpw = s1t.n - 1

                    def f_t(e, j=j, xr=xn_ready, war=tpw):
                        e.wait_ge(c_gld.h, 16 * 4)
                        e.wait_ge(s1v.h, xr)
                        if war >= 1:
                            e.wait_ge(s1e.h, war)
                        e.transpose(tp_ps[j % 2][:, 0:128],
                                    xn_t[:, j * 128:(j + 1) * 128],
                                    idb_sb[:]).then_inc(s1t.h, 1)
                    P1.add("tensor", f_t)
                    s1t.bump()
                    tp_ready = s1t.n

                    g = tt // 4

                    def f_ev(e, j=j, tt=tt, tr=tp_ready, g=g,
                             ww=w_done.get(g - 1),
                             wr=(8 * g - 1 if g >= 1 else None)):
                        e.wait_ge(s1t.h, tr)
                        if ww is not None:
                            e.wait_ge(s1w[(g - 1) % 2].h, ww)
                        if wr is not None:
                            e.wait_ge(s1r.h, wr)
                        col = j * 512 + (tt % 4) * 128
                        e.tensor_copy(xnt_sb[:, col:col + 128],
                                      tp_ps[j % 2][:, 0:128]).then_inc(s1e.h, 1)
                        _dr(e, True)
                    P1.add("vector", f_ev)
                    s1e.bump()
                e_done[tt] = s1e.n

                def f_rt(e, tt=tt, er=e_done[tt],
                         wm=mul_done.get(tt - 1), wsg=sg_done.get(tt - 1)):
                    e.wait_ge(c_gld.h, 16 * 4)
                    e.wait_ge(s1e.h, er)
                    if wm is not None:
                        e.wait_ge(s1sc.h, wm)
                        e.wait_ge(s1s.h, wsg)
                    for j in range(8):
                        col = j * 512 + (tt % 4) * 128
                        inst = e.matmul(lg_ps[:, 0:16], xnt_sb[:, col:col + 128],
                                        wrt_sb[:, j * 16:(j + 1) * 16],
                                        start=(j == 0), stop=(j == 7))
                    inst.then_inc(s1r.h, 1)
                P1.add("tensor", f_rt)
                s1r.bump()
                rt_done[tt] = s1r.n
                lg_ready = s1r.n

                def f_sg(e, lr=lg_ready):
                    e.wait_ge(s1r.h, lr)
                    e.activation(sgt[:], lg_ps[:, 0:16], AF.Sigmoid).then_inc(s1s.h, 1)
                    _dr(e, True)
                P1.add("scalar", f_sg)
                s1s.bump()
                sg_done[tt] = s1s.n

                def f_msk(e, lr=lg_ready, sgr=sg_done[tt],
                          wst=stp_done.get(tt - 1)):
                    e.wait_ge(s1r.h, lr)
                    if wst is not None:
                        e.wait_ge(s1r.h, wst)
                    e.tensor_reduce(mxt[:], lg_ps[:, 0:16], AX.X, ALU.max)
                    _dr(e, True)
                    e.tensor_scalar(mskt[:], lg_ps[:, 0:16], mxt[:, 0:1], None,
                                    ALU.is_ge)
                    _dr(e, True)
                    e.wait_ge(s1s.h, sgr)
                    e.tensor_mul(sct[:], mskt[:], sgt[:]).then_inc(s1sc.h, 1)
                    _dr(e, True)
                P1.add("vector", f_msk)
                s1sc.bump()
                mul_done[tt] = s1sc.n
                sc_ready = s1sc.n

                def f_sct(e, scr=sc_ready, wse=sce_done.get(tt - 1)):
                    e.wait_ge(s1sc.h, scr)
                    if wse is not None:
                        e.wait_ge(s1sc.h, wse)
                    e.transpose(st_ps[:, 0:128], sct[:], idf_sb[:]).then_inc(s1r.h, 1)
                P1.add("tensor", f_sct)
                s1r.bump()
                stp_done[tt] = s1r.n
                st_ready = s1r.n

                def f_sce(e, sr=st_ready, tt=tt):
                    e.wait_ge(s1r.h, sr)
                    e.tensor_copy(scores[:, tt * 128:(tt + 1) * 128],
                                  st_ps[:, 0:128]).then_inc(s1sc.h, 1)
                    _dr(e, True)
                P1.add("vector", f_sce)
                s1sc.bump()
                sce_done[tt] = s1sc.n

                if tt % 4 == 3:
                    g4 = tt // 4
                    wp = s1w[g4 % 2]

                    def f_wr(e, tt=tt, wp=wp, er=e_done[tt]):
                        e.wait_ge(s1e.h, er)
                        for j in range(8):
                            e.dma_start(
                                xnt_d[j, :, (tt - 3) * 128:(tt + 1) * 128],
                                xnt_sb[:, j * 512:(j + 1) * 512]
                            ).then_inc(wp.h, 16)
                    P1.add("sync", f_wr)
                    wp.bump(16 * 8)
                    w_done[g4] = wp.n

            def b_sync(e, w0=s1w[0].n, w1=s1w[1].n):
                e.wait_ge(s1w[0].h, w0)
                e.wait_ge(s1w[1].h, w1)
                e.nop().then_inc(bar1.h, 1)
            P1.add("sync", b_sync)

            def b_t(e):
                e.nop().then_inc(bar1.h, 1)
            P1.add("tensor", b_t)

            for eng in ("vector", "scalar", "gpsimd"):
                def b_e(e):
                    e.drain().then_inc(bar1.h, 1)
                P1.add(eng, b_e)
            bar1.bump(5)

            blk1 = p1.enter_context(nc.Block("p1"))
            P1.run(blk1)

        with ExitStack() as p2:
            sb2 = lambda n_, s, d: p2.enter_context(nc.sbuf_tensor(n_, s, d))
            ps2 = lambda n_, s, d=F32: p2.enter_context(nc.psum_tensor(n_, s, d))

            xnc = sb2("xnc", [128, 8 * CK], BF16)
            winb = [sb2(f"winb{i}", [128, 8 * 128], BF16) for i in range(2)]
            wout_sb = sb2("wout_sb", [128, 16 * 1024], BF16)
            ring = sb2("ring", [128, 16 * 520], BF16)
            convacc = [sb2(f"convacc{i}", [128, CK], F32) for i in range(2)]
            u_bf = sb2("u_bf", [128, 16 * CK], BF16)
            res_bf = sb2("res_bf", [128, 16 * CK], BF16)
            dr_bf = sb2("dr_bf", [64, CK], BF16)
            b_sbuf = sb2("b_sbuf", [16, CK], BF16)
            c_sbuf = sb2("c_sbuf", [16, CK], BF16)
            bhat = sb2("bhat", [128, 16 * CK], BF16)
            chat = sb2("chat", [128, 16 * CK], BF16)
            sp1 = sb2("sp1", [128, CK], F32)
            sp2 = sb2("sp2", [128, CK], F32)
            delta = sb2("delta", [128, CK], F32)
            z_sb = sb2("z_sb", [128, CK], F32)
            asb = [sb2(f"asb{i}", [128, CK], F32) for i in range(3)]
            bsb = [sb2(f"bsb{i}", [128, CK], F32) for i in range(3)]
            ssb = [sb2(f"ssb{i}", [128, CK], F32) for i in range(2)]
            ytmp_bf = [sb2(f"ytmpbf{i}", [128, CK], BF16) for i in range(3)]
            ytmp_f = sb2("ytmp_f", [128, CK], F32)
            carry = sb2("carry", [128, 256], F32)
            ybf = sb2("ybf", [128, 16 * CK], BF16)
            attn_t = [sb2("attn_t0", [128, 1024], F32)] * 2

            ps_win = [ps2(f"ps_win{i}", [128, CK]) for i in range(2)]
            ps_dr = ps2("ps_dr", [64, CK])
            ps_bc = ps2("ps_bc", [16, CK])
            ps_exp = ps2("ps_exp", [128, CK])
            ps_dpre = ps2("ps_dpre", [128, CK])
            ps_wo = [ps2("ps_wo0", [128, 512])] * 2
            ps_y = ps2("ps_y", [128, 512])

            P2 = Prog()
            NB = 5

            for eng in ("sync", "scalar"):
                def pf(e):
                    e.wait_ge(bar1.h, NB)
                P2.add(eng, pf)

            def pf_t(e):
                e.wait_ge(bar1.h, NB)
                e.wait_ge(c_gld.h, 16 * 4)
            P2.add("tensor", pf_t)

            def pf_v(e):
                e.wait_ge(bar1.h, NB)
                e.memset(carry[:], 0.0)
                _dr(e)
                for dt in range(NDT):
                    e.memset(ring[:, dt * 520:dt * 520 + 3], 0.0)
                e.drain().then_inc(m_hist.h, NDT)
            P2.add("vector", pf_v)
            m_hist.bump(NDT)

            def pf_g(e):
                e.wait_ge(bar1.h, NB)
                for g in range(4):
                    e.dma_start(
                        wout_sb[:].rearrange("p (c k) -> p c k", c=16)
                        [:, g * 4:(g + 1) * 4, :],
                        wout_d[:, g * 4:(g + 1) * 4, :],
                    ).then_inc(m_wres.h, 16)
            P2.add("gpsimd", pf_g)
            m_wres.bump(16 * 4)

            gpt = 0
            g_dtn = 0
            g_gp = {}
            pend_ymm = []
            prev_y_base = [None]

            def emit_wout(c, y_base):
                for tt in range(CK // 128):
                    def f_wo(e, tt=tt, c=c, yb=y_base, war=m_evC.n,
                             evc0=m_evC.n):
                        e.wait_ge(m_wres.h, 64)
                        if c + tt > 0:
                            e.wait_ge(m_evC.h, war)
                        for half in range(2):
                            if half == 1:
                                e.wait_ge(m_evC.h, evc0 + 1)
                            for ct in range(16):
                                e.wait_ge(m_y.h, yb + ct + 1)
                                lhs = ybf[:, ct * CK + tt * 128:
                                          ct * CK + tt * 128 + 128]
                                inst = e.matmul(
                                    ps_wo[0][:], lhs,
                                    wout_sb[:, ct * 1024 + half * 512:
                                            ct * 1024 + half * 512 + 512],
                                    start=(ct == 0), stop=(ct == 15))
                            inst.then_inc(m_wo.h, 1)
                    P2.add("tensor", f_wo)
                    m_wo.bump(2)
                    wo_ready = m_wo.n

                    idxa = c * 4 + tt
                    app = m_ast[idxa % 2]
                    app_prev = m_ast[(idxa - 1) % 2]

                    def f_evc(e, tt=tt, c=c, wr=wo_ready, app_prev=app_prev,
                              war=(app_prev.n if idxa >= 1 else None)):
                        e.wait_ge(m_wo.h, wr - 1)
                        if war is not None:
                            e.wait_ge(app_prev.h, war)
                        gt = (c * 4 + tt) % 2
                        e.copy(attn_t[gt][:, 0:512], ps_wo[0][:]
                               ).then_inc(m_evC.h, 1)
                        _dr(e)
                        e.wait_ge(m_wo.h, wr)
                        e.copy(attn_t[gt][:, 512:1024], ps_wo[0][:]
                               ).then_inc(m_evC.h, 1)
                        _dr(e)
                    P2.add("scalar", f_evc)
                    m_evC.bump(2)
                    evc_ready = m_evC.n

                    def f_ast(e, tt=tt, c=c, app=app, er=evc_ready):
                        e.wait_ge(m_evC.h, er)
                        gt = (c * 4 + tt) % 2
                        row = c * CK + tt * 128
                        e.dma_start(attn_d[row:row + 128, :], attn_t[gt][:]
                                    ).then_inc(app.h, 16)
                    P2.add("sync", f_ast)
                    app.bump(16)


            for c in range(NCH):
                def f_xnc(e, c=c, war=m_wiA.n):
                    if c >= 1:
                        e.wait_ge(m_wiA.h, war)
                    e.dma_start(
                        xnc[:].rearrange("p (j t) -> p j t", j=8),
                        xnt_d[:, :, c * CK:(c + 1) * CK].transpose([1, 0, 2]),
                    ).then_inc(m_xnc.h, 16)
                P2.add("sync", f_xnc)
                m_xnc.bump(16)
                xnc_ready = m_xnc.n

                for pt in range(32):
                    wpp = m_win[gpt % 2]

                    def f_wl(e, pt=pt, gpt=gpt, wpp=wpp, war=(m_wiA.n - 1)):
                        if gpt >= 2:
                            e.wait_ge(m_wiA.h, war)
                        e.dma_start(
                            winb[gpt % 2][:].rearrange("p (j k) -> p j k", j=8),
                            win_d[pt],
                        ).then_inc(wpp.h, 16)
                    P2.add("gpsimd", f_wl)
                    wpp.bump(16)
                    w_ready = wpp.n

                    def f_mm(e, gpt=gpt, wpp=wpp, xr=xnc_ready, wr=w_ready,
                             war=(m_evA.n - 1)):
                        e.wait_ge(m_xnc.h, xr)
                        e.wait_ge(wpp.h, wr)
                        if gpt >= 2:
                            e.wait_ge(m_evA.h, war)
                        for j in range(8):
                            inst = e.matmul(ps_win[gpt % 2][:],
                                            winb[gpt % 2][:, j * 128:(j + 1) * 128],
                                            xnc[:, j * CK:(j + 1) * CK],
                                            start=(j == 0), stop=(j == 7))
                        inst.then_inc(m_wiA.h, 1)
                    P2.add("tensor", f_mm)
                    m_wiA.bump()
                    mm_ready = m_wiA.n

                    def f_ev(e, pt=pt, gpt=gpt, c=c, mr=mm_ready,
                             hist=(16 * c + (pt % 16) + 1),
                             wy=(16 * (c - 1) + (pt - 16) + 1 if (c >= 1 and pt >= 16) else None)):
                        e.wait_ge(m_wiA.h, mr)
                        if pt < 16:
                            e.wait_ge(m_hist.h, hist)
                            e.copy(ring[:, pt * 520 + 3:pt * 520 + 515],
                                   ps_win[gpt % 2][:]).then_inc(m_evA.h, 1)
                        else:
                            if wy is not None:
                                e.wait_ge(m_y.h, wy)
                            e.activation(res_bf[:, (pt - 16) * CK:(pt - 15) * CK],
                                         ps_win[gpt % 2][:], AF.Silu
                                         ).then_inc(m_evA.h, 1)
                        _dr(e)
                    P2.add("scalar", f_ev)
                    m_evA.bump()
                    gpt += 1

                if c >= 1:
                    emit_wout(c - 1, prev_y_base[0])

                evA_base = m_evA.n - 32

                for dt in range(NDT):
                    def f_cv(e, dt=dt, ev=(evA_base + dt + 1),
                             wu=(m_u.n - 1)):
                        e.wait_ge(m_evA.h, ev)
                        if wu >= 1:
                            e.wait_ge(m_u.h, wu)
                        base = dt * 520
                        ca = convacc[dt % 2]
                        e.tensor_scalar(ca[:], ring[:, base:base + CK],
                                        convw_sb[:, dt * 4:dt * 4 + 1], None,
                                        ALU.mult)
                        _dr(e, True)
                        for k in range(1, 4):
                            inst = e.scalar_tensor_tensor(
                                ca[:], ring[:, base + k:base + k + CK],
                                convw_sb[:, dt * 4 + k:dt * 4 + k + 1],
                                ca[:], ALU.mult, ALU.add)
                            _dr(e, True)
                        inst.then_inc(m_conv.h, 1)
                        e.tensor_copy(ring[:, base:base + 3],
                                      ring[:, base + 512:base + 515]
                                      ).then_inc(m_hist.h, 1)
                        _dr(e)
                    P2.add("vector", f_cv)
                    m_conv.bump()
                    m_hist.bump()
                    cv_ready = m_conv.n

                    def f_u(e, dt=dt, c=c, cr=cv_ready,
                            wwx=(3 * c if c >= 1 else None),
                            wz=(16 * c if c >= 1 else None),
                            wy=(16 * c if c >= 1 else None)):
                        e.wait_ge(m_conv.h, cr)
                        if wwx is not None:
                            e.wait_ge(m_wx.h, wwx)
                            e.wait_ge(m_z.h, wz)
                            e.wait_ge(m_y.h, wy)
                        e.activation(u_bf[:, dt * CK:(dt + 1) * CK],
                                     convacc[dt % 2][:], AF.Silu,
                                     bias=convb_sb[:, dt:dt + 1]
                                     ).then_inc(m_u.h, 1)
                        _dr(e)
                    P2.add("scalar", f_u)
                    m_u.bump()

                u_all = m_u.n

                def f_wx(e, c=c, ur=u_all, war_s=m_evBs.n, war_v=m_evBv.n,
                         bev=(m_evBv.n + 1)):
                    e.wait_ge(m_u.h, ur)
                    if c >= 1:
                        e.wait_ge(m_evBs.h, war_s)
                        e.wait_ge(m_evBv.h, war_v)
                    for ct in range(16):
                        inst = e.matmul(ps_dr[:],
                                        wx_sb[:, ct * 96:ct * 96 + 64],
                                        u_bf[:, ct * CK:(ct + 1) * CK],
                                        start=(ct == 0), stop=(ct == 15))
                    inst.then_inc(m_wx.h, 1)
                    for ct in range(16):
                        inst = e.matmul(ps_bc[:],
                                        wx_sb[:, ct * 96 + 64:ct * 96 + 80],
                                        u_bf[:, ct * CK:(ct + 1) * CK],
                                        start=(ct == 0), stop=(ct == 15))
                    inst.then_inc(m_wx.h, 1)
                    e.wait_ge(m_evBv.h, bev)
                    for ct in range(16):
                        inst = e.matmul(ps_bc[:],
                                        wx_sb[:, ct * 96 + 80:ct * 96 + 96],
                                        u_bf[:, ct * CK:(ct + 1) * CK],
                                        start=(ct == 0), stop=(ct == 15))
                    inst.then_inc(m_wx.h, 1)
                P2.add("tensor", f_wx)
                m_wx.bump(3)
                wx_done = m_wx.n

                def f_evdr(e, wr=(wx_done - 2), wdt_prev=(m_wdt.n if c >= 1 else None)):
                    e.wait_ge(m_wx.h, wr)
                    if wdt_prev is not None:
                        e.wait_ge(m_wdt.h, wdt_prev)
                    e.copy(dr_bf[:], ps_dr[:]).then_inc(m_evBs.h, 1)
                    _dr(e)
                P2.add("scalar", f_evdr)
                m_evBs.bump()
                dr_ready = m_evBs.n

                def f_evbc(e, wrB=(wx_done - 1), wrC=wx_done,
                           wexp=(m_exp.n if c >= 1 else None)):
                    e.wait_ge(m_wx.h, wrB)
                    if wexp is not None:
                        e.wait_ge(m_exp.h, wexp)
                    e.tensor_copy(b_sbuf[:], ps_bc[:]).then_inc(m_evBv.h, 1)
                    _dr(e)
                    e.wait_ge(m_wx.h, wrC)
                    e.tensor_copy(c_sbuf[:], ps_bc[:]).then_inc(m_evBv.h, 1)
                    _dr(e)
                P2.add("vector", f_evbc)
                m_evBv.bump(2)
                bc_ready = m_evBv.n

                for src, dst in ((b_sbuf, bhat), (c_sbuf, chat)):
                    for n in range(NST):
                        def f_xp(e, src=src, n=n, br=bc_ready, war=m_evE.n):
                            e.wait_ge(m_evBv.h, br)
                            if war >= 1:
                                e.wait_ge(m_evE.h, war)
                            e.matmul(ps_exp[:], oh_sb[:, n * 128:(n + 1) * 128],
                                     src[:], start=True, stop=True
                                     ).then_inc(m_exp.h, 1)
                        P2.add("tensor", f_xp)
                        m_exp.bump()
                        xp_ready = m_exp.n

                        def f_xpe(e, dst=dst, n=n, xr=xp_ready,
                                  wb=(256 * c if c >= 1 else None)):
                            e.wait_ge(m_exp.h, xr)
                            if wb is not None:
                                e.wait_ge(m_b.h, wb)
                            e.tensor_copy(dst[:, n * CK:(n + 1) * CK],
                                          ps_exp[:]).then_inc(m_evE.h, 1)
                            _dr(e)
                        P2.add("vector", f_xpe)
                        m_evE.bump()
                evE_base = m_evE.n - 32

                for dt in range(NDT):
                    def f_wdt(e, dt=dt, drr=dr_ready, war=m_spr.n):
                        e.wait_ge(m_evBs.h, drr)
                        if war >= 1:
                            e.wait_ge(m_spr.h, war)
                        e.matmul(ps_dpre[:], wdt_sb[:, dt * 128:(dt + 1) * 128],
                                 dr_bf[:], start=True, stop=True
                                 ).then_inc(m_wdt.h, 1)
                    P2.add("tensor", f_wdt)
                    m_wdt.bump()
                    dpre_ready = m_wdt.n
                    for fn in pend_ymm:
                        P2.add("tensor", fn)
                    pend_ymm = []

                    def f_sp(e, dt=dt, dr=dpre_ready, war=m_dlt.n):
                        e.wait_ge(m_wdt.h, dr)
                        if war >= 1:
                            e.wait_ge(m_dlt.h, war)
                        bcol = bdt_sb[:, dt:dt + 1]
                        e.activation(sp1[:], ps_dpre[:], AF.Abs, bias=bcol)
                        _dr(e)
                        e.activation(sp2[:], ps_dpre[:], AF.Relu, bias=bcol
                                     ).then_inc(m_spr.h, 1)
                        _dr(e)
                        e.activation(delta[:], sp1[:], AF.Exp, scale=CV_NONE)
                        _dr(e)
                        e.activation(sp1[:], delta[:], AF.Ln, bias=CV_ONE
                                     ).then_inc(m_dsp.h, 1)
                        _dr(e)
                    P2.add("scalar", f_sp)
                    m_spr.bump()
                    m_dsp.bump()
                    sp_ready = m_dsp.n

                    def f_dlt(e, sr=sp_ready, wa=m_a.n, wz=m_z.n):
                        e.wait_ge(m_dsp.h, sr)
                        if wa >= 1:
                            e.wait_ge(m_a.h, wa)
                            e.wait_ge(m_z.h, wz)
                        e.tensor_add(delta[:], sp2[:], sp1[:]).then_inc(m_dlt.h, 1)
                        _dr(e)
                    P2.add("vector", f_dlt)
                    m_dlt.bump()
                    dlt_ready = m_dlt.n

                    def f_z(e, dt=dt, dr=dlt_ready):
                        e.wait_ge(m_dlt.h, dr)
                        e.tensor_mul(z_sb[:], delta[:],
                                     u_bf[:, dt * CK:(dt + 1) * CK]
                                     ).then_inc(m_z.h, 1)
                        _dr(e)
                    P2.add("gpsimd", f_z)
                    m_z.bump()
                    z_ready = m_z.n

                    pend_ym = [None]
                    for n in range(NST):
                        par = g_dtn % 3
                        spar = g_dtn % 2
                        on_gp = n in (1, 4, 7, 10, 13, 15)

                        def f_a(e, dt=dt, n=n, par=par, dr=dlt_ready,
                                war=(m_scan.n - 2)):
                            e.wait_ge(m_dlt.h, dr)
                            if war >= 1:
                                e.wait_ge(m_scan.h, war)
                            e.activation(asb[par][:], delta[:], AF.Exp,
                                         scale=a_sb[:, dt * 16 + n:dt * 16 + n + 1]
                                         ).then_inc(m_a.h, 1)
                            _dr(e)
                        P2.add("scalar", f_a)
                        m_a.bump()
                        a_ready = m_a.n

                        def f_b(e, n=n, par=par, zr=z_ready,
                                be=(evE_base + n + 1), war=(m_scan.n - 2)):
                            e.wait_ge(m_z.h, zr)
                            e.wait_ge(m_evE.h, be)
                            if war >= 1:
                                e.wait_ge(m_scan.h, war)
                            e.tensor_mul(bsb[par][:], z_sb[:],
                                         bhat[:, n * CK:(n + 1) * CK]
                                         ).then_inc(m_b.h, 1)
                            _dr(e)
                        P2.add("gpsimd", f_b)
                        m_b.bump()
                        b_ready = m_b.n
                        if pend_ym[0] is not None:
                            P2.add("gpsimd", pend_ym[0])
                            pend_ym[0] = None

                        gwar = g_gp.get(g_dtn - 2)

                        mmwar = (g_dtn - 2) if g_dtn >= 3 else None

                        def f_scan(e, dt=dt, n=n, par=par, spar=spar,
                                   ar=a_ready, br=b_ready, on_gp=on_gp,
                                   gwar=gwar, mmwar=mmwar,
                                   ce=(evE_base + 16 + n + 1)):
                            e.wait_ge(m_a.h, ar)
                            e.wait_ge(m_b.h, br)
                            if gwar is not None:
                                e.wait_ge(m_ytg.h, gwar)
                            col = dt * 16 + n
                            e.tensor_tensor_scan(ssb[spar][:], asb[par][:],
                                                 bsb[par][:],
                                                 carry[:, col:col + 1],
                                                 ALU.mult, ALU.add
                                                 ).then_inc(m_scan.h, 1)
                            _dr(e)
                            e.tensor_copy(carry[:, col:col + 1],
                                          ssb[spar][:, CK - 1:CK])
                            _dr(e)
                            if not on_gp:
                                e.wait_ge(m_evE.h, ce)
                                if mmwar is not None:
                                    e.wait_ge(m_ymm.h, mmwar)
                                e.tensor_mul(ytmp_bf[par][:], ssb[spar][:],
                                             chat[:, n * CK:(n + 1) * CK]
                                             ).then_inc(m_ytv.h, 1)
                                _dr(e)
                        P2.add("vector", f_scan)
                        m_scan.bump()
                        if on_gp:
                            def f_ym(e, n=n, par=par, spar=spar, sr=m_scan.n,
                                     mmwar=mmwar,
                                     ce=(evE_base + 16 + n + 1)):
                                e.wait_ge(m_scan.h, sr)
                                e.wait_ge(m_evE.h, ce)
                                if mmwar is not None:
                                    e.wait_ge(m_ymm.h, mmwar)
                                e.tensor_mul(ytmp_bf[par][:], ssb[spar][:],
                                             chat[:, n * CK:(n + 1) * CK]
                                             ).then_inc(m_ytg.h, 1)
                                _dr(e)
                            pend_ym[0] = f_ym
                            m_ytg.bump()
                            g_gp[g_dtn] = m_ytg.n
                            yt_ready = m_ytg.n
                        else:
                            m_ytv.bump()
                            yt_ready = m_ytv.n

                        def f_ymm(e, dt=dt, n=n, c=c, par=par, on_gp=on_gp,
                                  yr=yt_ready,
                                  wy=(16 * c + dt if (c + dt) > 0 else None)):
                            if on_gp:
                                e.wait_ge(m_ytg.h, yr)
                            else:
                                e.wait_ge(m_ytv.h, yr)
                            if n == 0 and wy is not None:
                                e.wait_ge(m_y.h, wy)
                            e.matmul(ps_y[:], idb_sb[:], ytmp_bf[par][:],
                                     start=(n == 0), stop=(n == 15)
                                     ).then_inc(m_ymm.h, 1)
                        pend_ymm.append(f_ymm)
                        m_ymm.bump()
                        g_dtn += 1

                    if pend_ym[0] is not None:
                        P2.add("gpsimd", pend_ym[0])
                        pend_ym[0] = None

                    def f_yb(e, dt=dt, c=c, rr=(evA_base + 16 + dt + 1),
                             ymr=m_ymm.n, ur=u_all,
                             wwo=(8 * c if c >= 1 else None)):
                        e.wait_ge(m_evA.h, rr)
                        e.wait_ge(m_ymm.h, ymr)
                        e.wait_ge(m_u.h, ur)
                        if wwo is not None:
                            e.wait_ge(m_wo.h, wwo)
                        e.scalar_tensor_tensor(ytmp_f[:],
                                               u_bf[:, dt * CK:(dt + 1) * CK],
                                               dvec_sb[:, dt:dt + 1],
                                               ps_y[:], ALU.mult, ALU.add)
                        _dr(e, True)
                        e.tensor_mul(ybf[:, dt * CK:(dt + 1) * CK], ytmp_f[:],
                                     res_bf[:, dt * CK:(dt + 1) * CK]
                                     ).then_inc(m_y.h, 1)
                        _dr(e)
                    P2.add("vector", f_yb)
                    m_y.bump()

                for fn in pend_ymm:
                    P2.add("tensor", fn)
                pend_ymm = []

                prev_y_base[0] = m_y.n - NDT

            emit_wout(NCH - 1, prev_y_base[0])

            def b2_sync(e, w0=m_ast[0].n, w1=m_ast[1].n):
                e.wait_ge(m_ast[0].h, w0)
                e.wait_ge(m_ast[1].h, w1)
                e.nop().then_inc(bar2.h, 1)
            P2.add("sync", b2_sync)

            def b2_t(e):
                e.nop().then_inc(bar2.h, 1)
            P2.add("tensor", b2_t)

            for eng in ("vector", "scalar", "gpsimd"):
                def b2_e(e):
                    e.drain().then_inc(bar2.h, 1)
                P2.add(eng, b2_e)
            bar2.bump(5)

            blk2 = p2.enter_context(nc.Block("p2"))
            P2.run(blk2)

        with ExitStack() as p3:
            sb3 = lambda n_, s, d: p3.enter_context(nc.sbuf_tensor(n_, s, d))
            ps3 = lambda n_, s, d=F32: p3.enter_context(nc.psum_tensor(n_, s, d))

            xnr = sb3("xnr", [128, 8 * SC], BF16)
            shat = [sb3(f"shat{i}", [128, SC], BF16) for i in range(2)]
            xe = sb3("xe", [128, 8 * SC], BF16)
            wgt_t = [sb3(f"wgt_t{i}", [128, 8 * 128], BF16) for i in range(3)]
            wut_t = [sb3(f"wut_t{i}", [128, 8 * 128], BF16) for i in range(3)]
            h_sb = sb3("h_sb", [128, 16 * SC], BF16)
            wdn_sb = [sb3(f"wdn_sb{i}", [128, 16 * 1024], BF16) for i in range(2)]
            acc = sb3("acc", [128, 8 * 1024], F32)
            sgtmp = sb3("sgtmp", [128, 512], F32)
            att_t = [sb3(f"att_t{i}", [128, 1024], F32) for i in range(2)]

            ps_s = ps3("ps_s", [128, 512])
            ps_g = [ps3(f"ps_g{i}", [128, 512]) for i in range(2)]
            ps_u = [ps3(f"ps_u{i}", [128, 512]) for i in range(2)]
            ps_d = [ps3(f"ps_d{i}", [128, 512]) for i in range(2)]

            P3 = Prog()
            NB2 = 5
            for eng in Prog.ENGS:
                def pf3(e):
                    e.wait_ge(bar2.h, NB2)
                P3.add(eng, pf3)

            gih = 0
            ge = 0
            gtt = 0
            pend_ep = [None]

            def emit_sx(sc, e_i, gu_prev, xnrr):
                shared = e_i == NE
                if not shared:
                    for th in range(2):
                        def f_sc(e, e_i=e_i, sc=sc, th=th,
                                 war=(n_scev.n if n_sc.n >= 1 else None)):
                            if war is not None:
                                e.wait_ge(n_scev.h, war)
                            e.matmul(
                                ps_s[:],
                                oh_sb[:, e_i * 128:(e_i + 1) * 128],
                                scores[:, sc * SC + th * 512:
                                       sc * SC + th * 512 + 512],
                                start=True, stop=True).then_inc(n_sc.h, 1)
                        P3.add("tensor", f_sc)
                        n_sc.bump()
                        sc_ready = n_sc.n

                        def f_sce(e, e_i=e_i, th=th, sr=sc_ready):
                            e.wait_ge(n_sc.h, sr)
                            e.tensor_copy(
                                shat[e_i % 2][:, th * 512:(th + 1) * 512],
                                ps_s[:]).then_inc(n_scev.h, 1)
                            _dr(e)
                        P3.add("vector", f_sce)
                        n_scev.bump()
                shat_ready = n_scev.n
                x_base = n_X.n
                for j in range(8):
                    def f_X(e, j=j, e_i=e_i, shared=shared, xnrr=xnrr,
                            shr=shat_ready, pg=gu_prev):
                        e.wait_ge(n_xnr.h, xnrr)
                        e.wait_ge(n_gu.h, pg)
                        if not shared:
                            e.wait_ge(n_scev.h, shr)
                            e.tensor_mul(xe[:, j * SC:(j + 1) * SC],
                                         xnr[:, j * SC:(j + 1) * SC],
                                         shat[e_i % 2][:]).then_inc(n_X.h, 1)
                        else:
                            e.tensor_copy(xe[:, j * SC:(j + 1) * SC],
                                          xnr[:, j * SC:(j + 1) * SC]
                                          ).then_inc(n_X.h, 1)
                        _dr(e)
                    P3.add("vector", f_X)
                    n_X.bump()
                return x_base

            for sc in range(NSC):
                def f_xnr(e, sc=sc, war=n_X.n):
                    if sc >= 1:
                        e.wait_ge(n_X.h, war)
                    e.dma_start(
                        xnr[:].rearrange("p (j t) -> p j t", j=8),
                        xnt_d[:, :, sc * SC:(sc + 1) * SC].transpose([1, 0, 2]),
                    ).then_inc(n_xnr.h, 16)
                P3.add("sync", f_xnr)
                n_xnr.bump(16)
                xnr_ready = n_xnr.n

                x_base = emit_sx(sc, 0, n_gu.n, xnr_ready)

                if pend_ep[0] is not None:
                    pend_ep[0]()
                    pend_ep[0] = None

                for e_i in range(NE + 1):
                    shared = e_i == NE

                    prev_dn_done = n_dn.n
                    sl_base = n_sl.n
                    h_base0 = n_h.n
                    wdp = n_wdn[ge % 2]

                    def f_wdl(e, e_i=e_i, ge=ge, wdp=wdp, shared=shared,
                              war=(n_dn.n - 8 if ge >= 2 else None)):
                        if war is not None:
                            e.wait_ge(n_dn.h, max(war, 0))
                        src = (wdsh_d if shared else dwn_d[e_i]
                               ).rearrange("p c k -> p (c k)")
                        e.dma_start(wdn_sb[ge % 2][:], src).then_inc(wdp.h, 16)
                    P3.add("gpsimd", f_wdl)
                    wdp.bump(16)
                    wdn_ready = wdp.n


                    for i in range(16):
                        wgp = n_wg[gih % 3]

                        def f_wgl(e, e_i=e_i, i=i, gih=gih, wgp=wgp,
                                  shared=shared,
                                  war=(n_gu.n - 8 if gih >= 3 else None)):
                            if war is not None:
                                e.wait_ge(n_gu.h, max(war, 0))
                            if shared:
                                gsrc = wgt_d[i].rearrange("p c f -> p (c f)")
                                usrc = wut_d[i].rearrange("p c f -> p (c f)")
                            else:
                                gsrc = gup_d[e_i, i].rearrange("p c f -> p (c f)")
                                usrc = gup_d[e_i, 16 + i].rearrange("p c f -> p (c f)")
                            e.dma_start(wgt_t[gih % 3][:], gsrc
                                        ).then_inc(wgp.h, 16)
                            e.dma_start(wut_t[gih % 3][:], usrc
                                        ).then_inc(wgp.h, 16)
                        P3.add("gpsimd", f_wgl)
                        wgp.bump(32)
                        wg_ready = wgp.n

                        for th in range(2):
                            war_sl = sl_base + (2 * i - 2 if th == 0 else 2 * i + 1)
                            war_h = h_base0 + (2 * i - 2 if th == 0 else 2 * i + 1)

                            def f_gu(e, i=i, th=th, gih=gih, wgp=wgp,
                                     wr=wg_ready, xb=x_base,
                                     wsl=max(war_sl, 0), wh=max(war_h, 0)):
                                e.wait_ge(wgp.h, wr)
                                if wsl >= 1:
                                    e.wait_ge(n_sl.h, wsl)
                                if wh >= 1:
                                    e.wait_ge(n_h.h, wh)
                                for ct in range(8):
                                    e.wait_ge(n_X.h, xb + ct + 1)
                                    rhs = xe[:, ct * SC + th * 512:
                                             ct * SC + th * 512 + 512]
                                    ig = e.matmul(ps_g[i % 2][:],
                                                  wgt_t[gih % 3][:, ct * 128:
                                                                 (ct + 1) * 128],
                                                  rhs, start=(ct == 0),
                                                  stop=(ct == 7))
                                ig.then_inc(n_gu.h, 1)
                                for ct in range(8):
                                    rhs = xe[:, ct * SC + th * 512:
                                             ct * SC + th * 512 + 512]
                                    iu = e.matmul(ps_u[i % 2][:],
                                                  wut_t[gih % 3][:, ct * 128:
                                                                 (ct + 1) * 128],
                                                  rhs, start=(ct == 0),
                                                  stop=(ct == 7))
                                iu.then_inc(n_gu.h, 1)
                            P3.add("tensor", f_gu)
                            n_gu.bump(2)
                            gu_ready = n_gu.n

                            def f_sl(e, i=i, gr=(gu_ready - 1), whp=n_h.n):
                                e.wait_ge(n_gu.h, gr)
                                if whp >= 1:
                                    e.wait_ge(n_h.h, whp)
                                e.activation(sgtmp[:], ps_g[i % 2][:], AF.Silu
                                             ).then_inc(n_sl.h, 1)
                                _dr(e)
                            P3.add("scalar", f_sl)
                            n_sl.bump()
                            sl_ready = n_sl.n

                            def f_h(e, i=i, th=th, gr=gu_ready, slr=sl_ready,
                                    pd=prev_dn_done):
                                e.wait_ge(n_gu.h, gr)
                                e.wait_ge(n_sl.h, slr)
                                e.wait_ge(n_dn.h, pd)
                                e.tensor_mul(
                                    h_sb[:, i * SC + th * 512:
                                         i * SC + th * 512 + 512],
                                    sgtmp[:], ps_u[i % 2][:]).then_inc(n_h.h, 1)
                                _dr(e)
                            P3.add("vector", f_h)
                            n_h.bump()
                        gih += 1
                    h_base = n_h.n - 32

                    if e_i < NE:
                        x_base = emit_sx(sc, e_i + 1, n_gu.n, xnr_ready)

                    for tt in range(8):
                        def f_dn(e, tt=tt, ge=ge, gtt=gtt, hb=h_base,
                                 wdp=wdp, wr=wdn_ready, war=n_acc.n):
                            e.wait_ge(wdp.h, wr)
                            if gtt >= 1:
                                e.wait_ge(n_acc.h, war)
                            for ct in range(16):
                                e.wait_ge(n_h.h, hb + 2 * (ct + 1))
                                lhs = h_sb[:, ct * SC + tt * 128:
                                           ct * SC + tt * 128 + 128]
                                e.matmul(ps_d[0][:], lhs,
                                         wdn_sb[ge % 2][:, ct * 1024:
                                                        ct * 1024 + 512],
                                         start=(ct == 0), stop=(ct == 15))
                                inst = e.matmul(ps_d[1][:], lhs,
                                                wdn_sb[ge % 2][:, ct * 1024 + 512:
                                                               (ct + 1) * 1024],
                                                start=(ct == 0), stop=(ct == 15))
                            inst.then_inc(n_dn.h, 1)
                        P3.add("tensor", f_dn)
                        n_dn.bump()
                        dn_ready = n_dn.n
                        gtt += 1

                        if e_i == 0:
                            def f_acc(e, tt=tt, sc=sc, dr=dn_ready,
                                      war=(8 * sc if sc >= 1 else None)):
                                e.wait_ge(n_dn.h, dr)
                                if war is not None:
                                    e.wait_ge(n_ep.h, war)
                                e.tensor_copy(acc[:, tt * 1024:tt * 1024 + 512],
                                              ps_d[0][:])
                                _dr(e)
                                e.tensor_copy(acc[:, tt * 1024 + 512:(tt + 1) * 1024],
                                              ps_d[1][:]).then_inc(n_acc.h, 1)
                                _dr(e)
                        else:
                            def f_acc(e, tt=tt, dr=dn_ready):
                                e.wait_ge(n_dn.h, dr)
                                pd = 0
                                e.tensor_add(acc[:, tt * 1024:tt * 1024 + 512],
                                             acc[:, tt * 1024:tt * 1024 + 512],
                                             ps_d[pd][:])
                                _dr(e)
                                e.tensor_add(acc[:, tt * 1024 + 512:(tt + 1) * 1024],
                                             acc[:, tt * 1024 + 512:(tt + 1) * 1024],
                                             ps_d[pd + 1][:]).then_inc(n_acc.h, 1)
                                _dr(e)
                        P3.add("vector", f_acc)
                        n_acc.bump()
                    ge += 1

                acc_done = n_acc.n

                def emit_epilogue(sc=sc, acc_done=acc_done):
                  for tt in range(8):
                    idx = sc * 8 + tt

                    xap = n_xat[idx % 2]
                    oxp = n_ox[idx % 2]
                    opp = n_out[idx % 2]

                    def f_lx(e, sc=sc, tt=tt, idx=idx, xap=xap, opp=opp,
                             owar=(16 * (idx // 2) if idx >= 2 else None)):
                        if owar is not None:
                            e.wait_ge(opp.h, owar)
                        row = sc * SC + tt * 128
                        e.dma_start(att_t[idx % 2][:], attn_d[row:row + 128, :]
                                    ).then_inc(xap.h, 16)
                    P3.add("sync", f_lx)
                    xap.bump(16)
                    xat_ready = xap.n

                    def f_ep(e, tt=tt, idx=idx, xap=xap,
                             ar=(acc_done - 8 + tt + 1), xr=xat_ready):
                        gt = idx % 2
                        e.wait_ge(n_acc.h, ar)
                        e.wait_ge(xap.h, xr)
                        e.tensor_add(att_t[gt][:],
                                     acc[:, tt * 1024:(tt + 1) * 1024],
                                     att_t[gt][:]).then_inc(n_ep.h, 1)
                        _dr(e)
                    P3.add("vector", f_ep)
                    n_ep.bump()
                    ep_ready = n_ep.n

                    def f_xadd(e, sc=sc, tt=tt, idx=idx, oxp=oxp, er=ep_ready):
                        e.wait_ge(n_ep.h, er)
                        row = sc * SC + tt * 128
                        e.dma_start(att_t[idx % 2][:], x_d[row:row + 128, :],
                                    accum_op=ALU.add).then_inc(oxp.h, 16)
                    P3.add("gpsimd", f_xadd)
                    oxp.bump(16)
                    ox_ready = oxp.n

                    def f_out(e, sc=sc, tt=tt, idx=idx, opp=opp, oxp=oxp,
                              er=ox_ready):
                        e.wait_ge(oxp.h, er)
                        gt = idx % 2
                        row = sc * SC + tt * 128
                        e.dma_start(out_d[row:row + 128, :], att_t[gt][:]
                                    ).then_inc(opp.h, 16)
                    P3.add("sync", f_out)
                    opp.bump(16)

                pend_ep[0] = emit_epilogue

            if pend_ep[0] is not None:
                pend_ep[0]()
                pend_ep[0] = None

            def f_fin(e, w0=n_out[0].n, w1=n_out[1].n):
                e.wait_ge(n_out[0].h, w0)
                e.wait_ge(n_out[1].h, w1)
            P3.add("sync", f_fin)

            blk3 = p3.enter_context(nc.Block("p3"))
            P3.run(blk3)

    return nc


def _host_layout(inputs):
    f32 = np.float32
    g = lambda k: np.asarray(inputs[k])
    conv_w = g("L0_attn_conv_w")
    cw = np.ascontiguousarray(
        conv_w[:, 0, :].T.reshape(16, 128, 4).transpose(1, 0, 2).reshape(128, 64))
    cb = np.ascontiguousarray(g("L0_attn_conv_b").reshape(16, 128).T)
    bdt = np.ascontiguousarray(g("L0_attn_b_dt").reshape(16, 128).T)
    dvec = np.ascontiguousarray(g("L0_attn_D").reshape(16, 128).T)
    alog = np.ascontiguousarray(
        g("L0_attn_A_log").reshape(16, 128, 16).transpose(1, 0, 2).reshape(128, 256))
    onehot = np.zeros((16, 16 * 128), ml_dtypes.bfloat16)
    for n in range(16):
        onehot[n, n * 128:(n + 1) * 128] = 1.0
    def tile_lhs(w, nin, nout):
        return np.ascontiguousarray(
            w.reshape(nin, 128, nout, 128).transpose(2, 1, 0, 3))

    def tile_rhs(w, nin, k):
        return np.ascontiguousarray(
            w.reshape(nin, 128, k).transpose(1, 0, 2))

    gup = g("L0_ffn_gate_up_proj")
    gup_t = np.ascontiguousarray(
        gup.reshape(16, 8, 128, 32, 128).transpose(0, 3, 2, 1, 4))
    dwn = g("L0_ffn_down_proj")
    dwn_t = np.ascontiguousarray(
        dwn.reshape(16, 16, 128, 1024).transpose(0, 2, 1, 3))
    return {
        "win": tile_lhs(g("L0_attn_W_in"), 8, 32),
        "wx": np.ascontiguousarray(g("L0_attn_W_x")),
        "wdt": np.ascontiguousarray(g("L0_attn_W_dt")),
        "convw": cw, "convb": cb, "bdt": bdt, "alog": alog, "dvec": dvec,
        "wout": tile_rhs(g("L0_attn_W_out"), 16, 1024),
        "wrt": np.ascontiguousarray(g("L0_ffn_w_router").T),
        "gup": gup_t,
        "dwn": dwn_t,
        "wgt": tile_lhs(np.ascontiguousarray(g("L0_ffn_w_gate").T), 8, 16),
        "wut": tile_lhs(np.ascontiguousarray(g("L0_ffn_w_up").T), 8, 16),
        "wdsh": tile_rhs(np.ascontiguousarray(g("L0_ffn_w_down").T), 16, 1024),
        "oh": onehot,
        "idb": np.eye(128, dtype=ml_dtypes.bfloat16),
        "idf": np.eye(128, dtype=f32),
        "cvals": np.tile(np.array([[EPS, 1.0, 1.0 / D, -0.5, -1.0, 0.0, 0.0, 0.0]],
                                  dtype=f32), (128, 1)),
    }


def kernel(**inputs):
    common = _host_layout(inputs)
    x = np.asarray(inputs["L0_attn_x"])
    nc = build_program()
    in_maps = []
    for b in range(B):
        m = dict(common)
        m["x"] = np.ascontiguousarray(x[b])
        in_maps.append(m)
    res = run_bass_kernel_spmd(nc, in_maps, core_ids=list(range(B)))
    return np.stack([r["out"] for r in res.results], axis=0)


if __name__ == "__main__":
    import time
    t0 = time.time()
    nc = build_program()
    print("built", nc.n_instructions(), "instrs in", round(time.time() - t0, 1), "s")
